# version 10
# baseline (speedup 1.0000x reference)
import numpy as np

import concourse.bass as bass
import concourse.mybir as mybir
from concourse.bass_utils import run_bass_kernel_spmd

F32 = mybir.dt.float32
F16 = mybir.dt.float16
AF = mybir.ActivationFunctionType
ALU = mybir.AluOpType
AX = mybir.AxisListType

D = 1024
S = 2048
B = 8
H = 16
KVH = 4
HD = 64
LAT = 256
WIN = 256
INNER = 4096
NC = D // 128
TW = 512
MAGIC = 1536.0

Q_CHUNK_HEADS = [(0, 4), (1, 5), (2, 6), (3, 7), (8, 12), (9, 13), (10, 14), (11, 15)]
ENGS = ("sync", "tensor", "vector", "scalar", "gpsimd")

DEBUG_STAGE = None
KEEP_SCOPES = False
SKIP_TABS = False
DEBUG_ROPE = None



def _rope_perm64():
    return np.concatenate([np.arange(0, 64, 2), np.arange(1, 64, 2)])


def _q_col_perm():
    rp = _rope_perm64()
    cols = []
    for (ha, hb) in Q_CHUNK_HEADS:
        cols.append(64 * ha + rp)
        cols.append(64 * hb + rp)
    return np.concatenate(cols)


def _k_col_perm():
    rp = _rope_perm64()
    return np.concatenate([64 * kv + rp for kv in range(KVH)])


def _wo_row_perm():
    rows = []
    for (ha, hb) in Q_CHUNK_HEADS:
        rows.append(64 * ha + np.arange(64))
        rows.append(64 * hb + np.arange(64))
    return np.concatenate(rows)


def _rope_tables(base, s):
    inv = np.power(float(base), -np.arange(0, 32, dtype=np.float64) / 32.0)
    th = np.arange(s, dtype=np.float64)[:, None] * inv[None, :]
    cos = np.cos(th).T
    sin = np.sin(th).T
    C = np.empty((128, s), dtype=np.float32)
    Sg = np.empty((128, s), dtype=np.float32)
    for blk in range(4):
        r = 32 * blk
        C[r:r + 32] = cos.astype(np.float32)
        Sg[r:r + 32] = (sin if blk % 2 else -sin).astype(np.float32)
    return C, Sg


def _swap_mat():
    P = np.zeros((128, 128), dtype=np.float32)
    for p in range(128):
        blk = (p // 64) * 64
        o = p - blk
        P[p, blk + (o + 32) % 64] = 1.0
    return P


def host_tables(s):
    kk = np.arange(128)[:, None]
    qq = np.arange(128)[None, :]
    Cg, Sgt = _rope_tables(1e6, s)
    Cl, Slt = _rope_tables(1e4, s)
    ind2 = np.zeros((128, 2), dtype=np.float32)
    ind2[:64, 0] = 1.0
    ind2[64:, 1] = 1.0
    t = {
        "Cg": Cg, "Sg": Sgt, "Cl": Cl, "Sl": Slt,
        "mask_diag": (kk <= qq).astype(np.float32),
        "mask_su": (kk > qq).astype(np.float32),
        "swap": _swap_mat(),
        "onesq": np.ones((128, 128), dtype=np.float32),
        "ind2": ind2,
        "ind2T": np.ascontiguousarray(ind2.T),
        "ones16": np.ones((128, 1), dtype=np.float16),
    }
    return {k: np.ascontiguousarray(v) for k, v in t.items()}



class KB:
    def __init__(self, nc):
        self.nc = nc
        self.steps = {e: [] for e in ENGS}
        self.counts = {}
        self.sems = {}
        self._ctx = []
        self._scopes = []

    def sem(self, name):
        cm = self.nc.semaphore(name)
        h = cm.__enter__()
        self._ctx.append(cm)
        self.sems[name] = h
        self.counts[name] = 0
        return h

    def sbuf(self, name, shape, dt, scoped=False):
        cm = self.nc.sbuf_tensor(name, shape, dt)
        h = cm.__enter__()
        (self._scopes[-1] if scoped else self._ctx).append(cm)
        return h

    def psum(self, name, shape, dt=F32):
        cm = self.nc.psum_tensor(name, shape, dt)
        h = cm.__enter__()
        self._ctx.append(cm)
        return h

    def push_scope(self):
        self._scopes.append([])

    def pop_scope(self):
        for cm in reversed(self._scopes.pop()):
            cm.__exit__(None, None, None)

    def close(self):
        while self._scopes:
            self.pop_scope()
        while self._ctx:
            self._ctx.pop().__exit__(None, None, None)

    def _push(self, eng, fn):
        self.steps[eng].append(fn)

    def wait(self, eng, sem_name, value):
        h = self.sems[sem_name]
        v = int(value)
        if v > 0:
            self._push(eng, lambda e: e.wait_ge(h, v))

    def guard(self, eng, *sem_names):
        for sn in sem_names:
            self.wait(eng, sn, self.counts[sn])

    def op(self, eng, make, inc=None):
        if inc is None:
            self._push(eng, lambda e: make(e))
            return None
        h = self.sems[inc]
        self.counts[inc] += 1
        self._push(eng, lambda e: make(e).then_inc(h, 1))
        return self.counts[inc]

    def drain(self, eng):
        self._push(eng, lambda e: e.drain())

    def commit(self, eng, sem_name):
        self.drain(eng)
        if eng in ("vector", "gpsimd"):
            return self.op(eng, lambda e: e.engine_nop(), inc=sem_name)
        return self.op(eng, lambda e: e.nop(), inc=sem_name)

    def dma(self, eng, out, in_, sem_name):
        return self.dma_group(eng, [(out, in_)], sem_name)

    def dma_group(self, eng, pairs, sem_name):
        pool = self.dma_pools[eng]
        idx = self.dma_rot[eng] % len(pool)
        self.dma_rot[eng] += 1
        sn = pool[idx]
        self.wait(eng, sn, self.counts[sn])
        h = self.sems[sn]
        for (out, in_) in pairs:
            self.counts[sn] += 16
            self._push(eng, lambda e, o=out, i=in_: e.dma_start(out=o, in_=i)
                       .then_inc(h, 16))
        return (sn, self.counts[sn])

    def wait_tok(self, eng, tok):
        self.wait(eng, tok[0], tok[1])

    def val(self, sem_name):
        return self.counts[sem_name]

    def barrier(self):
        for e in ENGS:
            for sn in list(self.counts):
                self.wait(e, sn, self.counts[sn])


def ap_of(handle, off, dims):
    return bass.AP(handle, off, dims)



def build_program():
    s = S
    nt = s // TW

    nc = bass.Bass("TRN2", target_bir_lowering=False)
    kb = KB(nc)

    xT = nc.dram_tensor("xT", [D, s], F32, kind="ExternalInput").ap()
    w = {}
    for name, shape in [
        ("g_wq", [D, D]), ("g_kv_down", [D, LAT]), ("g_k_up", [LAT, LAT]),
        ("g_v_up", [LAT, LAT]), ("g_wo", [D, D]),
        ("l_wq", [D, D]), ("l_wk", [D, LAT]), ("l_wv", [D, LAT]), ("l_wo", [D, D]),
        ("w_gate", [D, INNER]), ("w_up", [D, INNER]), ("w_down", [INNER, D]),
        ("norm1_w", [128, NC]), ("norm2_w", [128, NC]),
    ]:
        w[name] = nc.dram_tensor(name, shape, F32, kind="ExternalInput").ap()
    tab = {}
    for name, shape, dt in [
        ("Cg", [128, s], F32), ("Sg", [128, s], F32),
        ("Cl", [128, s], F32), ("Sl", [128, s], F32),
        ("mask_diag", [128, 128], F32), ("mask_su", [128, 128], F32),
        ("swap", [128, 128], F32), ("onesq", [128, 128], F32),
        ("ind2", [128, 2], F32), ("ind2T", [2, 128], F32),
        ("ones16", [128, 1], F16),
    ]:
        tab[name] = nc.dram_tensor(name, shape, dt, kind="ExternalInput").ap()
    outT = nc.dram_tensor("outT", [D, s], F32, kind="ExternalOutput").ap()
    dbgT = None
    if DEBUG_STAGE is not None:
        dbgT = nc.dram_tensor("dbgT", [D, s], F32, kind="ExternalOutput").ap()

    xn_scr = nc.dram_tensor("xn_scr", [D, s], F32).ap()
    s_scr = nc.dram_tensor("s_scr", [D, s], F32).ap()
    wgq_scr = nc.dram_tensor("wgq_scr", [D, INNER], F16).ap()
    wuq_scr = nc.dram_tensor("wuq_scr", [D, INNER], F16).ap()
    wdq_scr = nc.dram_tensor("wdq_scr", [INNER, D], F16).ap()

    for sn in ("T", "V", "SC", "G"):
        kb.sem(sn)
    kb.dma_pools = {"sync": [], "scalar": [], "gpsimd": []}
    kb.dma_rot = {"sync": 0, "scalar": 0, "gpsimd": 0}
    for q, pref in (("sync", "LW"), ("scalar", "LS"), ("gpsimd", "LG")):
        for i in range(4):
            sn = f"{pref}{i}"
            kb.sem(sn)
            kb.dma_pools[q].append(sn)

    consts = kb.sbuf("consts", [128, 16], F32)
    rows = kb.sbuf("rows", [128, 3 * TW], F32)
    tmp1 = kb.sbuf("tmp1", [128, TW], F32)
    tmp2 = kb.sbuf("tmp2", [128, TW], F32)
    tmpsq = kb.sbuf("tmpsq", [128, TW], F32)
    Pt = kb.sbuf("Pt", [128, 4 * TW], F32)
    cs = kb.sbuf("cs_sb", [128, 72], F32)
    mw = kb.sbuf("mw_sb", [128, 4], F32)
    tb = {}
    for k in ("mask_diag", "mask_su", "swap", "onesq", "ind2", "ind2T"):
        tb[k] = kb.sbuf("tb_" + k, list(tab[k].tensor.shape), F32)
    tb["ones16"] = kb.sbuf("tb_ones16", [128, 1], F16)

    pbh = [kb.psum(f"pb{i}", [128, 512]) for i in range(8)]
    pb = [t[:, :] for t in pbh]

    st = dict(nc=nc, kb=kb, s=s, nt=nt, w=w, tab=tab, tb=tb,
              consts=consts, rows=rows, tmp1=tmp1, tmp2=tmp2, tmpsq=tmpsq,
              Pt=Pt, cs=cs, mw=mw, pb=pb, pbh=pbh,
              xT=xT, outT=outT, dbgT=dbgT,
              xn_scr=xn_scr, s_scr=s_scr,
              wgq_scr=wgq_scr, wuq_scr=wuq_scr, wdq_scr=wdq_scr)

    emit_all(st)

    blk = nc.Block()
    block = blk.__enter__()
    kb._ctx.append(blk)
    for eng_name in ENGS:
        fns = kb.steps[eng_name]

        def body(e, fns=fns):
            for f in fns:
                f(e)
        getattr(block, eng_name)(body)
    kb.close()
    return nc


def R(st, k, n=TW):
    return st["rows"][0:1, k * TW:k * TW + n]


def R2(st, k, n=TW):
    return st["rows"][0:2, k * TW:k * TW + n]


def R64(st, k, n=TW):
    return st["rows"][64:65, k * TW:k * TW + n]


def R96(st, k, n=TW):
    return st["rows"][96:97, k * TW:k * TW + n]


def emit_all(st):
    kb = st["kb"]

    st["tabs_tok"] = kb.dma_group(
        "sync", [(st["tb"][k][:, :], st["tab"][k][:, :])
                 for k in ("mask_diag", "mask_su", "swap", "onesq",
                           "ind2", "ind2T", "ones16")], None)

    c = st["consts"]
    kb.op("vector", lambda e: e.memset(c[:, 0:1], -8.0))
    kb.op("vector", lambda e: e.memset(c[:, 1:2], 1e-5))
    kb.op("vector", lambda e: e.memset(c[:, 2:3], 1e-6))
    kb.op("vector", lambda e: e.memset(c[:, 3:4], float(np.log(127.0))))
    kb.op("vector", lambda e: e.memset(c[:, 4:5], MAGIC))
    kb.commit("vector", "V")
    st["consts_done"] = kb.val("V")

    kb.push_scope()
    st["xn"] = kb.sbuf("xn", [128, NC * st["s"]], F32, scoped=True)
    st["xst"] = kb.sbuf("xst", [128, NC * TW], F32, scoped=True)
    st["KG"] = kb.sbuf("KG", [128, 2 * st["s"]], F32, scoped=True)
    st["VG"] = kb.sbuf("VG", [128, (st["s"] // 128) * 4 * 65], F32, scoped=True)
    st["QT"] = kb.sbuf("QT", [128, NC * TW], F32, scoped=True)
    st["AO"] = kb.sbuf("AO", [128, NC * TW], F32, scoped=True)
    st["lat"] = kb.sbuf("lat", [128, 2 * st["s"]], F32, scoped=True)
    st["wst"] = kb.sbuf("wst", [128, 2 * 1024], F32, scoped=True)
    st["CSc"] = kb.sbuf("CSc", [128, st["s"]], F32, scoped=True)
    st["CSs"] = kb.sbuf("CSs", [128, st["s"]], F32, scoped=True)

    emit_ln(st)
    if DEBUG_STAGE == "xn_plus":
        rms_scale_bcast(st, 0)
        emit_debug_xn(st, st["xn"])
        emit_tail(st)
        return
    if DEBUG_STAGE == "xn_barrier":
        kb.barrier()
        emit_debug_xn(st, st["xn"])
        emit_tail(st)
        return
    if DEBUG_STAGE == "wq":
        kb.barrier()
        if not KEEP_SCOPES:
            kb.pop_scope()
        kb.push_scope()
        st["wf"] = kb.sbuf("wf", [128, 2 * INNER], F32, scoped=True)
        st["wq16"] = kb.sbuf("wq16", [128, 2 * INNER], F16, scoped=True)
        st["sgn2"] = kb.sbuf("sgn2", [128, 2 * INNER], F16, scoped=True)
        emit_wquant(st)
        kb.barrier()
        kb.guard("gpsimd", "V", "T", "SC")
        kb.dma("gpsimd", st["dbgT"][0:128, 0:72], st["cs"][:, :], None)
        kb.dma("gpsimd", st["outT"][0:128, 0:72], st["cs"][:, :], None)
        emit_tail(st)
        return
    if DEBUG_STAGE == "xn":
        emit_debug_xn(st, st["xn"])
        emit_tail(st)
        return
    emit_attn(st, glob=True)
    if DEBUG_STAGE in ("q0", "sdpa1"):
        kb.barrier()
        kb.guard("gpsimd", "V", "T", "SC")
        if DEBUG_STAGE == "q0":
            pairs = [(st["outT"][ch * 128:(ch + 1) * 128, 0:TW],
                      st["QT"][:, ch * TW:(ch + 1) * TW]) for ch in range(NC)]
        else:
            pairs = [(st["outT"][0:128, 0:TW], st["AO"][:, 0:TW])]
        kb.dma_group("gpsimd", pairs, None)
        emit_tail(st)
        return
    if DEBUG_STAGE in ("norm", "norm_nodma", "rms0", "rmsA", "rmsB"):
        kb.barrier()
        emit_debug_xn(st, st["xn"])
        emit_tail(st)
        return
    if DEBUG_STAGE in ("lat", "k"):
        kb.barrier()
        kb.guard("gpsimd", "V", "T", "SC")
        if DEBUG_STAGE == "lat":
            pairs = [(st["outT"][0:128, :], st["lat"][:, 0:st["s"]]),
                     (st["outT"][128:256, :], st["lat"][:, st["s"]:2 * st["s"]])]
        else:
            pairs = [(st["outT"][0:128, :], st["KG"][:, 0:st["s"]]),
                     (st["outT"][128:256, :], st["KG"][:, st["s"]:2 * st["s"]])]
        kb.dma_group("gpsimd", pairs, None)
        emit_tail(st)
        return
    if DEBUG_STAGE in ("kv", "q0"):
        kb.barrier()
        kb.guard("gpsimd", "V", "T", "SC")
        if DEBUG_STAGE == "kv":
            pairs = [(st["outT"][0:128, :], st["KG"][:, 0:st["s"]]),
                     (st["outT"][128:256, :], st["KG"][:, st["s"]:2 * st["s"]])]
            vgf = (st["s"] // 128) * 260
            off = 0
            row = 256
            while off < vgf:
                wdt = min(st["s"], vgf - off)
                pairs.append((st["outT"][row:row + 128, 0:wdt],
                              st["VG"][:, off:off + wdt]))
                off += wdt
                row += 128
        else:
            pairs = [(st["outT"][ch * 128:(ch + 1) * 128, 0:TW],
                      st["QT"][:, ch * TW:(ch + 1) * TW]) for ch in range(NC)]
        kb.dma_group("gpsimd", pairs, None)
        emit_tail(st)
        return
    if DEBUG_STAGE == "s1":
        emit_debug_xn(st, st["xn"])
        emit_tail(st)
        return
    emit_attn(st, glob=False)
    emit_store_s(st)
    if DEBUG_STAGE == "s":
        emit_debug_xn(st, st["xn"])
        emit_tail(st)
        return
    kb.barrier()
    if not KEEP_SCOPES:
        kb.pop_scope()

    kb.push_scope()
    st["wf"] = kb.sbuf("wf", [128, 2 * INNER], F32, scoped=True)
    st["wq16"] = kb.sbuf("wq16", [128, 2 * INNER], F16, scoped=True)
    st["sgn2"] = kb.sbuf("sgn2", [128, 2 * INNER], F16, scoped=True)
    emit_wquant(st)
    kb.barrier()
    if not KEEP_SCOPES:
        kb.pop_scope()

    kb.push_scope()
    st["arena"] = kb.sbuf("arena", [128, 2 * NC * TW], F32, scoped=True)
    st["xoc"] = kb.sbuf("xoc", [128, 2 * 512], F32, scoped=True)
    st["z1"] = kb.sbuf("z1", [128, NC * TW], F16, scoped=True)
    st["z2"] = kb.sbuf("z2", [128, 32 * TW], F16, scoped=True)
    st["prod"] = kb.sbuf("prod", [128, 32 * TW], F32, scoped=True)
    st["fw"] = kb.sbuf("fw", [128, 3 * NC * TW], F16, scoped=True)
    st["sxb"] = kb.sbuf("sxb", [128, TW], F32, scoped=True)
    st["tgb"] = kb.sbuf("tgb", [128, TW], F32, scoped=True)
    st["tub"] = kb.sbuf("tub", [128, TW], F32, scoped=True)
    st["tdb"] = kb.sbuf("tdb", [128, TW], F32, scoped=True)
    st["tmp4"] = kb.sbuf("tmp4", [128, TW], F32, scoped=True)
    emit_ffn(st)
    kb.pop_scope()

    emit_tail(st)


def emit_tail(st):
    kb = st["kb"]
    for q in ("sync", "scalar", "gpsimd"):
        for sn in kb.dma_pools[q]:
            kb.wait(q, sn, kb.val(sn))


def emit_debug_xn(st, src):
    kb, s = st["kb"], st["s"]
    kb.guard("gpsimd", "V", "T", "SC")
    pairs = []
    for ch in range(NC):
        pairs.append((st["outT"][ch * 128:(ch + 1) * 128, :],
                      src[:, ch * s:(ch + 1) * s]))
        pairs.append((st["dbgT"][ch * 128:(ch + 1) * 128, :],
                      src[:, ch * s:(ch + 1) * s]))
    kb.dma_group("gpsimd", pairs, None)



def emit_ln(st):
    kb, s, nt = st["kb"], st["s"], st["nt"]
    xn, xst, tmpsq, c, pb, tb = (st["xn"], st["xst"], st["tmpsq"], st["consts"],
                                 st["pb"], st["tb"])

    kb.wait_tok("scalar", st["tabs_tok"])
    kb.wait_tok("tensor", st["tabs_tok"])
    kb.wait("vector", "V", st["consts_done"])

    for t in range(nt):
        t0 = t * TW
        kb.guard("sync", "T", "V", "SC")
        xl = kb.dma_group(
            "sync", [(xst[:, ch * TW:(ch + 1) * TW],
                      st["xT"][ch * 128:(ch + 1) * 128, t0:t0 + TW])
                     for ch in range(NC)], None)
        kb.wait_tok("scalar", xl)
        kb.wait_tok("tensor", xl)

        kb.guard("scalar", "T")
        tvals = {}
        for ch in range(NC):
            xc = xst[:, ch * TW:(ch + 1) * TW]
            buf = tmpsq if ch % 2 == 0 else st["tmp2"]
            if ch >= 2:
                kb.wait("scalar", "T", tvals[ch - 2])
            sq = kb.op("scalar", lambda e, xc=xc, buf=buf: e.activation(
                buf[:, :], xc, AF.Square), inc="SC")
            kb.op("tensor", lambda e, xc=xc, f=(ch == 0), g=(ch == NC - 1):
                  e.matmul(pb[0][0:1, :], tb["onesq"][:, 0:1], xc,
                           start=f, stop=g))
            kb.wait("tensor", "SC", sq)
            tvals[ch] = kb.op("tensor", lambda e, buf=buf, f=(ch == 0),
                              g=(ch == NC - 1): e.matmul(
                pb[1][0:1, :], tb["onesq"][:, 0:1], buf[:, :],
                start=f, stop=g), inc="T")
        red = kb.val("T")

        kb.wait("vector", "T", red)
        kb.op("vector", lambda e: e.tensor_scalar(
            R(st, 0), pb[0][0:1, :], 1.0 / D, None, ALU.mult))
        kb.drain("vector")
        kb.op("vector", lambda e: e.tensor_tensor(
            R(st, 1), R(st, 0), R(st, 0), ALU.mult))
        kb.drain("vector")
        vv = kb.commit("vector", "V")
        kb.op("vector", lambda e: e.scalar_tensor_tensor(
            R(st, 1), pb[1][0:1, :], 1.0 / D, R(st, 1), ALU.mult, ALU.subtract))
        vv = kb.commit("vector", "V")
        kb.wait("scalar", "V", vv)
        kb.op("scalar", lambda e: e.activation(
            R(st, 2), R(st, 1), AF.Ln, bias=c[0:1, 1:2]))
        kb.drain("scalar")
        sq2 = kb.op("scalar", lambda e: e.activation(
            R(st, 1), R(st, 2), AF.Exp, bias=0.0, scale=-0.5), inc="SC")
        kb.wait("vector", "SC", sq2)
        kb.op("vector", lambda e: e.tensor_tensor(
            R(st, 2), R(st, 0), R(st, 1), ALU.mult))
        vv = kb.commit("vector", "V")

        kb.wait("tensor", "V", vv)
        kb.op("tensor", lambda e: e.matmul(
            pb[2], tb["onesq"][0:1, :], R(st, 1), start=True, stop=True))
        tbc = kb.op("tensor", lambda e: e.matmul(
            pb[3], tb["onesq"][0:1, :], R(st, 2), start=True, stop=True), inc="T")

        kb.wait("vector", "T", tbc)
        for ch in range(NC):
            xc = xst[:, ch * TW:(ch + 1) * TW]
            xnc = xn[:, ch * s + t0: ch * s + t0 + TW]
            kb.op("vector", lambda e, xc=xc, xnc=xnc: e.tensor_tensor(
                xnc, xc, pb[2], ALU.mult))
        kb.drain("vector")
        for ch in range(NC):
            xnc = xn[:, ch * s + t0: ch * s + t0 + TW]
            kb.op("vector", lambda e, xnc=xnc: e.tensor_tensor(
                xnc, xnc, pb[3], ALU.subtract))
        vv = kb.commit("vector", "V")
        kb.wait("gpsimd", "V", vv)
        kb.dma_group("gpsimd",
                     [(st["xn_scr"][ch * 128:(ch + 1) * 128, t0:t0 + TW],
                       xn[:, ch * s + t0: ch * s + t0 + TW])
                      for ch in range(NC)], None)
        kb.wait("tensor", "V", vv)



def rms_scale_bcast(st, t0):
    kb, s = st["kb"], st["s"]
    pb, tb, tmpsq, c = st["pb"], st["tb"], st["tmpsq"], st["consts"]
    kb.guard("scalar", "T")
    tvals = {}
    for ch in range(NC):
        xc = st["xn"][:, ch * s + t0: ch * s + t0 + TW]
        buf = tmpsq if ch % 2 == 0 else st["tmp2"]
        if ch >= 2:
            kb.wait("scalar", "T", tvals[ch - 2])
        sq = kb.op("scalar", lambda e, xc=xc, buf=buf: e.activation(
            buf[:, :], xc, AF.Square), inc="SC")
        kb.wait("tensor", "SC", sq)
        tvals[ch] = kb.op("tensor", lambda e, buf=buf, f=(ch == 0),
                          g=(ch == NC - 1): e.matmul(
            pb[0][0:1, :], tb["onesq"][:, 0:1], buf[:, :], start=f, stop=g),
            inc="T")
    red = kb.val("T")
    if DEBUG_STAGE == "rmsA":
        return red
    kb.wait("vector", "T", red)
    kb.op("vector", lambda e: e.tensor_scalar(
        R(st, 0), pb[0][0:1, :], 1.0 / D, None, ALU.mult))
    vv = kb.commit("vector", "V")
    kb.wait("scalar", "V", vv)
    kb.op("scalar", lambda e: e.activation(
        R(st, 2), R(st, 0), AF.Ln, bias=c[0:1, 2:3]))
    kb.drain("scalar")
    sq2 = kb.op("scalar", lambda e: e.activation(
        R(st, 0), R(st, 2), AF.Exp, bias=0.0, scale=-0.5), inc="SC")
    if DEBUG_STAGE == "rmsB":
        return kb.val("T")
    kb.wait("tensor", "SC", sq2)
    return kb.op("tensor", lambda e: e.matmul(
        pb[1], tb["onesq"][0:1, :], R(st, 0), start=True, stop=True), inc="T")


def norm_tile(st, t0, dst):
    kb, s = st["kb"], st["s"]
    c = st["consts"]
    for ch in range(NC):
        src_c = st["xn"][:, ch * s + t0: ch * s + t0 + TW]
        dst_c = dst[:, ch * TW:(ch + 1) * TW]
        kb.op("vector", lambda e, a=src_c, b=dst_c, ch=ch: e.scalar_tensor_tensor(
            b, a, c[:, 8 + ch:9 + ch], st["pb"][1], ALU.mult, ALU.mult))
    return kb.commit("vector", "V")


def rope_qknorm(st, src_psum, t0, out_region, ncols=TW):
    kb = st["kb"]
    kb.op("vector", lambda e, o=out_region: e.tensor_copy(o, src_psum))
    vv = kb.commit("vector", "V")
    return rope_qknorm_sb(st, t0, out_region, ncols)


def rope_qknorm_sb(st, t0, chunk, ncols=TW, deps=None):
    kb = st["kb"]
    pb, tb, c = st["pb"], st["tb"], st["consts"]
    tmp1, tmp2, tmpsq = st["tmp1"], st["tmp2"], st["tmpsq"]
    CSc, CSs = st["CSc"], st["CSs"]

    vv = kb.val("V")
    if deps is None:
        kb.guard("scalar", "T", "V")
    else:
        if deps.get("t_ind2") is not None:
            kb.wait("scalar", "T", deps["t_ind2"])
        kb.wait("scalar", "V", deps["chunk_v"])
    sq = kb.op("scalar", lambda e, ch=chunk: e.activation(
        tmpsq[:, 0:ncols], ch, AF.Square), inc="SC")
    if deps is not None and deps.get("v_final") is not None:
        kb.wait("tensor", "V", deps["v_final"])
    kb.wait("tensor", "V", vv if deps is None else deps["chunk_v"])
    kb.op("tensor", lambda e, ch=chunk: e.matmul(
        pb[6][:, 0:ncols], tb["swap"][:, :], ch, start=True, stop=True))
    kb.wait("tensor", "SC", sq)
    tt = kb.op("tensor", lambda e: e.matmul(
        pb[7][0:2, 0:ncols], tb["ind2"][:, :], tmpsq[:, 0:ncols],
        start=True, stop=True), inc="T")
    kb.wait("vector", "T", tt)
    kb.op("vector", lambda e: e.tensor_scalar(
        R2(st, 1, ncols), pb[7][0:2, 0:ncols], 1.0 / HD, None, ALU.mult))
    vv = kb.commit("vector", "V")
    kb.wait("scalar", "V", vv)
    kb.op("scalar", lambda e: e.activation(
        R2(st, 2, ncols), R2(st, 1, ncols), AF.Ln, bias=c[0:2, 2:3]))
    kb.drain("scalar")
    sq2 = kb.op("scalar", lambda e: e.activation(
        R2(st, 1, ncols), R2(st, 2, ncols), AF.Exp, bias=0.0, scale=-0.5),
        inc="SC")
    kb.wait("tensor", "SC", sq2)
    tt = kb.op("tensor", lambda e: e.matmul(
        pb[7][:, 0:ncols], tb["ind2T"][0:2, :], R2(st, 1, ncols),
        start=True, stop=True), inc="T")
    Ct = CSc[:, t0:t0 + ncols]
    Stt = CSs[:, t0:t0 + ncols]
    kb.op("vector", lambda e, ch=chunk, Ct=Ct: e.tensor_tensor(
        tmp1[:, 0:ncols], ch, Ct, ALU.mult))
    kb.wait("vector", "T", tt)
    kb.op("vector", lambda e, Stt=Stt: e.tensor_tensor(
        tmp2[:, 0:ncols], pb[6][:, 0:ncols], Stt, ALU.mult))
    kb.drain("vector")
    kb.op("vector", lambda e: e.tensor_tensor(
        tmp1[:, 0:ncols], tmp1[:, 0:ncols], tmp2[:, 0:ncols], ALU.add))
    kb.drain("vector")
    kb.op("vector", lambda e, ch=chunk: e.tensor_tensor(
        ch, tmp1[:, 0:ncols], pb[7][:, 0:ncols], ALU.mult))
    vfin = kb.commit("vector", "V")
    return {"t_ind2": tt, "v_final": vfin,
            "chunk_v": (deps["chunk_v"] if deps is not None else vfin)}


def band_load(st, dst, src, n=0):
    kb = st["kb"]
    kb.guard("sync", "T")
    return kb.dma("sync", dst, src, None)


def emit_attn(st, glob):
    kb, s, nt = st["kb"], st["s"], st["nt"]
    w, tb, c, pb = st["w"], st["tb"], st["consts"], st["pb"]
    xn, KG, VG, QT, AO, lat, wst, xst = (st["xn"], st["KG"], st["VG"], st["QT"],
                                         st["AO"], st["lat"], st["wst"], st["xst"])

    wq = w["g_wq" if glob else "l_wq"]
    wo = w["g_wo" if glob else "l_wo"]
    nw = w["norm1_w" if glob else "norm2_w"]

    kb.guard("sync", "V")
    tl = kb.dma_group("sync", [
        (st["CSc"][:, :], st["tab"]["Cg" if glob else "Cl"][:, :]),
        (st["CSs"][:, :], st["tab"]["Sg" if glob else "Sl"][:, :]),
        (ap_of(st["consts"], 8, [[16, 128], [1, NC]]), nw[:, :]),
    ], None)
    for e in ("vector", "tensor", "scalar"):
        kb.wait_tok(e, tl)

    if DEBUG_STAGE in ("xn_pre", "xn_pre2", "xn_pre3", "xn_pre4", "xn_pre5"):
        return
    for t in range(nt):
        t0 = t * TW
        bc = rms_scale_bcast(st, t0)
        kb.wait("vector", "T", bc)
        kb.guard("vector", "T")
        if DEBUG_STAGE in ("rms0", "rmsA", "rmsB"):
            break
        nt_done = norm_tile(st, t0, xst)
        if DEBUG_STAGE in ("norm", "norm_nodma"):
            break

        if glob:
            kv_down = w["g_kv_down"]
            kb.wait("tensor", "V", nt_done)
            for kc in range(NC):
                buf = wst[:, (kc % 2) * 1024:(kc % 2) * 1024 + LAT]
                wl = band_load(st, buf, kv_down[kc * 128:(kc + 1) * 128, :], LAT)
                kb.wait_tok("tensor", wl)
                for lc in range(2):
                    kb.op("tensor", lambda e, buf=buf, lc=lc, kc=kc, f=(kc == 0),
                          g=(kc == NC - 1):
                          e.matmul(pb[4 + lc],
                                   buf[:, lc * 128:(lc + 1) * 128],
                                   xst[:, kc * TW:(kc + 1) * TW],
                                   start=f, stop=g),
                          inc=("T" if lc == 1 else None))
            tt = kb.val("T")
            kb.wait("vector", "T", tt)
            kb.guard("vector", "T")
            for lc in range(2):
                kb.op("vector", lambda e, lc=lc, t0=t0: e.tensor_copy(
                    lat[:, lc * s + t0: lc * s + t0 + TW], pb[4 + lc]))
            lat_done = kb.commit("vector", "V")
            if DEBUG_STAGE == "lat":
                continue

            kup = w["g_k_up"]
            kb.guard("sync", "T")
            wl = kb.dma_group("sync", [
                (wst[:, 0:LAT], kup[0:128, :]),
                (wst[:, 1024:1024 + LAT], kup[128:256, :])], None)
            kb.wait_tok("tensor", wl)
            kb.wait("tensor", "V", lat_done)
            for oc in range(2):
                for lc in range(2):
                    kb.op("tensor", lambda e, oc=oc, lc=lc, t0=t0, f=(lc == 0),
                          g=(lc == 1): e.matmul(
                        pb[4], wst[:, lc * 1024 + oc * 128: lc * 1024 + (oc + 1) * 128],
                        lat[:, lc * s + t0: lc * s + t0 + TW], start=f, stop=g),
                        inc=("T" if lc == 1 else None))
                kb.wait("vector", "T", kb.val("T"))
                rope_qknorm(st, pb[4], t0, KG[:, oc * s + t0: oc * s + t0 + TW])
                kb.wait("tensor", "V", kb.val("V"))

            if DEBUG_STAGE == "k":
                continue
            vup = w["g_v_up"]
            kb.guard("sync", "T")
            wl = kb.dma_group("sync", [
                (wst[:, 0:LAT], vup[0:128, :]),
                (wst[:, 1024:1024 + LAT], vup[128:256, :])], None)
            kb.wait_tok("tensor", wl)
            emit_v_proj(st, t0, nk=2,
                        lhs_fn=lambda kc, tc4, t0=t0: lat[:, kc * s + t0 + tc4 * 128:
                                                          kc * s + t0 + (tc4 + 1) * 128],
                        rhs_fn=lambda kc: wst[:, kc * 1024: kc * 1024 + LAT])
        else:
            lwk = w["l_wk"]
            kb.wait("tensor", "V", nt_done)
            for kc in range(NC):
                buf = wst[:, (kc % 2) * 1024:(kc % 2) * 1024 + LAT]
                wl = band_load(st, buf, lwk[kc * 128:(kc + 1) * 128, :], LAT)
                kb.wait_tok("tensor", wl)
                for oc in range(2):
                    kb.op("tensor", lambda e, buf=buf, oc=oc, kc=kc, f=(kc == 0),
                          g=(kc == NC - 1):
                          e.matmul(pb[4 + oc], buf[:, oc * 128:(oc + 1) * 128],
                                   xst[:, kc * TW:(kc + 1) * TW], start=f, stop=g),
                          inc=("T" if oc == 1 else None))
            tt = kb.val("T")
            kb.wait("vector", "T", tt)
            for oc in range(2):
                rope_qknorm(st, pb[4 + oc], t0, KG[:, oc * s + t0: oc * s + t0 + TW])
            kb.wait("tensor", "V", kb.val("V"))

            emit_v_proj_local(st, t0, w["l_wv"])

    if DEBUG_STAGE in ("lat", "k", "kv") and glob:
        return
    for t in range(nt):
        t0 = t * TW
        bc = rms_scale_bcast(st, t0)
        kb.wait("vector", "T", bc)
        kb.guard("vector", "T")
        nt_done = norm_tile(st, t0, xst)
        kb.wait("tensor", "V", nt_done)

        t_after = {}
        toks = {}
        kb.guard("gpsimd", "T")
        toks[0] = kb.dma("gpsimd", wst[:, 0:1024], wq[0:128, :], None)
        for kc in range(NC):
            if kc + 1 < NC:
                if kc >= 1:
                    kb.wait("gpsimd", "T", t_after[kc - 1])
                toks[kc + 1] = kb.dma(
                    "gpsimd", wst[:, ((kc + 1) % 2) * 1024:((kc + 1) % 2) * 1024 + 1024],
                    wq[(kc + 1) * 128:(kc + 2) * 128, :], None)
            kb.wait_tok("tensor", toks[kc])
            buf = wst[:, (kc % 2) * 1024:(kc % 2) * 1024 + 1024]
            for oc in range(NC):
                kb.op("tensor", lambda e, buf=buf, oc=oc, kc=kc, f=(kc == 0),
                      g=(kc == NC - 1):
                      e.matmul(pb[oc], buf[:, oc * 128:(oc + 1) * 128],
                               xst[:, kc * TW:(kc + 1) * TW], start=f, stop=g),
                      inc=("T" if oc == NC - 1 else None))
            t_after[kc] = kb.val("T")
        kb.wait("vector", "T", kb.val("T"))
        for oc in range(NC):
            kb.op("vector", lambda e, oc=oc: e.tensor_copy(
                QT[:, oc * TW:(oc + 1) * TW], pb[oc]))
        qv = kb.commit("vector", "V")
        st["sdpa_sbv"] = {b: qv for b in (0, 1, 6, 7)}
        st["sdpa_pvv"] = {j: qv for j in range(4)}
        deps = {"t_ind2": None, "v_final": None, "chunk_v": qv}
        for oc in range(4):
            deps = rope_qknorm_sb(st, t0, QT[:, oc * TW:(oc + 1) * TW],
                                  deps=deps)
        emit_sdpa_kv(st, glob, t, 0, deep=False, qready=deps["v_final"])
        for oc in range(4, NC):
            deps = rope_qknorm_sb(st, t0, QT[:, oc * TW:(oc + 1) * TW],
                                  deps=deps)
        st["sdpa_sbv"][6] = st["sdpa_sbv"][7] = deps["v_final"]
        emit_sdpa_kv(st, glob, t, 1, qready=deps["v_final"])
        emit_sdpa_kv(st, glob, t, 2)
        emit_sdpa_kv(st, glob, t, 3)

        t_after = {}
        toks = {}
        kb.guard("sync", "T")
        toks[0] = kb.dma("sync", wst[:, 0:1024], wo[0:128, :], None)
        for tok in st.pop("ao_lift", []):
            kb.wait_tok("tensor", tok)
        kb.guard("tensor", "V")
        for kc in range(NC):
            if kc + 1 < NC:
                if kc >= 1:
                    kb.wait("sync", "T", t_after[kc - 1])
                toks[kc + 1] = kb.dma(
                    "sync", wst[:, ((kc + 1) % 2) * 1024:((kc + 1) % 2) * 1024 + 1024],
                    wo[(kc + 1) * 128:(kc + 2) * 128, :], None)
            kb.wait_tok("tensor", toks[kc])
            buf = wst[:, (kc % 2) * 1024:(kc % 2) * 1024 + 1024]
            for oc in range(NC):
                kb.op("tensor", lambda e, buf=buf, oc=oc, kc=kc, f=(kc == 0),
                      g=(kc == NC - 1):
                      e.matmul(pb[oc], buf[:, oc * 128:(oc + 1) * 128],
                               AO[:, kc * TW:(kc + 1) * TW], start=f, stop=g),
                      inc=("T" if oc == NC - 1 else None))
            t_after[kc] = kb.val("T")
        tt = kb.val("T")
        kb.wait("vector", "T", tt)
        for oc in range(NC):
            dstc = xn[:, oc * s + t0: oc * s + t0 + TW]
            kb.op("vector", lambda e, oc=oc, dstc=dstc: e.tensor_tensor(
                dstc, dstc, pb[oc], ALU.add))
        vv = kb.commit("vector", "V")
        kb.wait("tensor", "V", vv)


def emit_sdpa_kv(st, glob, t, kv, deep=True, qready=None):
    kb, s = st["kb"], st["s"]
    pb, pbh, tb, Pt, VG, KG, QT, AO = (st["pb"], st["pbh"], st["tb"], st["Pt"],
                                       st["VG"], st["KG"], st["QT"], st["AO"])
    Pt2 = st["lat"]
    c = st["consts"]
    t0 = t * TW
    nblk = 4 * 65
    vg_free = (s // 128) * nblk
    rb = 64 * (kv % 2)
    kcn = kv // 2
    sbanks = [0, 1, 6, 7] if deep else [0, 1, 0, 1]
    sbv = st["sdpa_sbv"]
    pvv = st["sdpa_pvv"]
    lastpv = st.setdefault("sdpa_lastpv", {})

    if qready is not None:
        kb.wait("tensor", "V", qready)

    pos = []
    for qc, (ha, hb) in enumerate(Q_CHUNK_HEADS):
        if ha // 4 == kv:
            pos.append((qc, 0))
        if hb // 4 == kv:
            pos.append((qc, 1))
    assert len(pos) == 4

    chunks = []
    if glob:
        for j in range(0, t * 4 + 4):
            k0 = j * 128
            qoff = max(0, k0 - t0)
            subs = [(qoff, "d")] if k0 >= t0 else []
            chunks.append((k0, qoff, TW - qoff, subs))
    else:
        for j in range(-2, 4):
            k0 = t0 + j * 128
            if k0 < 0:
                continue
            lo = max(0, j) * 128
            hi = min(4, j + 3) * 128
            subs = []
            if j >= 0:
                subs.append((j * 128, "d"))
            if j + 2 <= 3:
                subs.append(((j + 2) * 128, "u"))
            chunks.append((k0, lo, hi - lo, subs))
    L = len(chunks)

    def pt_ap(j, ci, a, n):
        buf = Pt if ci % 2 == 0 else Pt2
        return buf[:, j * TW + a: j * TW + a + n]

    exp_sc = {}
    pv_t = {}
    for ci, (k0, qoff, N, subs) in enumerate(chunks):
        last = (ci == L - 1)
        lhsK = KG[rb:rb + 64, kcn * s + k0: kcn * s + k0 + 128]
        lhsV = ap_of(VG, (k0 // 128) * nblk + 65 * kv, [[vg_free, 128], [1, 65]])
        for j, (qc, half) in enumerate(pos):
            qb = 64 * half
            scb = pbh[sbanks[j]]
            rhsQ = QT[qb:qb + 64, qc * TW + qoff: qc * TW + qoff + N]
            if ci == 0:
                v = sbv.get(sbanks[j])
                if v is not None and (deep or j < 2):
                    kb.wait("tensor", "V", v)
            if deep:
                prev = (ci - 1, j)
            else:
                prev = (ci, j - 2) if j >= 2 else (ci - 1, j + 2)
            if prev in exp_sc:
                kb.wait("tensor", "SC", exp_sc[prev])
            kb.op("tensor", lambda e, lhsK=lhsK, rhsQ=rhsQ, qoff=qoff, N=N,
                  b=scb: e.matmul(b[0:128, qoff:qoff + N], lhsK, rhsQ,
                                  start=True, stop=True), inc="T")
            tt = kb.val("T")
            kb.wait("scalar", "T", tt)
            if (ci - 2, j) in pv_t:
                kb.wait("scalar", "T", pv_t[(ci - 2, j)])
            elif ci < 2 and (j, ci % 2) in lastpv:
                kb.wait("scalar", "T", lastpv[(j, ci % 2)])
            sc = kb.op("scalar", lambda e, qoff=qoff, N=N, b=scb, j=j, ci=ci:
                       e.activation(pt_ap(j, ci, qoff, N),
                                    b[0:128, qoff:qoff + N], AF.Exp,
                                    bias=c[:, 0:1], scale=0.125), inc="SC")
            exp_sc[(ci, j)] = sc
        for j, (qc, half) in enumerate(pos):
            pvb = pbh[2 + j]
            sc = exp_sc[(ci, j)]
            if subs:
                kb.wait("vector", "SC", sc)
                for (soff, kind) in subs:
                    mk = tb["mask_diag"] if kind == "d" else tb["mask_su"]
                    kb.op("vector", lambda e, soff=soff, mk=mk, j=j, ci=ci:
                          e.tensor_tensor(
                              pt_ap(j, ci, soff, 128),
                              pt_ap(j, ci, soff, 128),
                              mk[:, :], ALU.mult))
                mdone = kb.commit("vector", "V")
                kb.wait("tensor", "V", mdone)
            else:
                kb.wait("tensor", "SC", sc)
            if ci == 0:
                v = pvv.get(j)
                if v is not None:
                    kb.wait("tensor", "V", v)
            if glob:
                kb.op("tensor", lambda e, lhsV=lhsV, qoff=qoff, N=N, b=pvb,
                      j=j, ci=ci, f=(ci == 0), g=last: e.matmul(
                          b[0:65, qoff:qoff + N], lhsV,
                          pt_ap(j, ci, qoff, N),
                          start=f, stop=g), inc="T")
            else:
                nsub = N // 128
                for sbt in range(nsub):
                    so = qoff + sbt * 128
                    kb.op("tensor", lambda e, lhsV=lhsV, so=so, b=pvb, j=j,
                          ci=ci,
                          f=(ci == 0 and sbt == 0),
                          g=(last and sbt == nsub - 1): e.matmul(
                              b[0:65, so:so + 128], lhsV,
                              pt_ap(j, ci, so, 128),
                              start=f, stop=g),
                          inc=("T" if sbt == nsub - 1 else None))
            pv_t[(ci, j)] = kb.val("T")

    for j in range(4):
        lastpv[(j, (L - 1) % 2)] = pv_t[(L - 1, j)]
        if L >= 2:
            lastpv[(j, (L - 2) % 2)] = pv_t[(L - 2, j)]

    xstb = st["xst"]
    Cslots = [R64(st, 0), R64(st, 1), R64(st, 2), xstb[64:65, 2 * TW:3 * TW]]
    rws = st["rows"]
    Lslots = [rws[32:33, 0:TW], rws[32:33, TW:2 * TW], rws[32:33, 2 * TW:3 * TW],
              rws[96:97, 0:TW]]
    kb.wait("vector", "T", pv_t[(L - 1, 3)])
    for j in range(4):
        kb.op("vector", lambda e, j=j: e.tensor_copy(
            Cslots[j], pbh[2 + j][64:65, 0:TW]))
    cv = kb.commit("vector", "V")
    kb.wait("scalar", "V", cv)
    for j in range(4):
        kb.op("scalar", lambda e, j=j: e.activation(Lslots[j], Cslots[j], AF.Ln))
    kb.drain("scalar")
    sqv = {}
    for j in range(4):
        sqv[j] = kb.op("scalar", lambda e, j=j: e.activation(
            Cslots[j], Lslots[j], AF.Exp, bias=0.0, scale=-1.0), inc="SC")
    stgs = [st["xst"][0:64, 0:TW], st["xst"][0:64, TW:2 * TW]]
    ttv = None
    for pair in ((0, 1), (2, 3)):
        for j in pair:
            kb.wait("tensor", "SC", sqv[j])
            if ttv is not None and not deep and j == pair[0]:
                kb.wait("tensor", "V", ttv)
            lx = (L - 1, 2 + (j % 2)) if not deep else (L - 1, j)
            kb.wait("tensor", "SC", exp_sc[lx])
            kb.op("tensor", lambda e, j=j, b=sbanks[j]: e.matmul(
                pbh[b][0:64, 0:TW], tb["onesq"][64:65, 0:64],
                Cslots[j], start=True, stop=True), inc="T")
        tt = kb.val("T")
        kb.wait("vector", "T", tt)
        bufs = {pair[0]: xstb[0:64, 2 * TW:3 * TW],
                pair[1]: xstb[0:64, 3 * TW:4 * TW]}
        for j in pair:
            kb.op("vector", lambda e, j=j, b=bufs[j]: e.tensor_copy(
                b, pbh[sbanks[j]][0:64, 0:TW]))
        kb.drain("vector")
        for j in pair:
            qc, half = pos[j]
            if half == 0:
                dst = AO[0:64, qc * TW:(qc + 1) * TW]
                kb.op("vector", lambda e, j=j, b=bufs[j], dst=dst: e.tensor_tensor(
                    dst, pbh[2 + j][0:64, 0:TW], b, ALU.mult))
            else:
                stg = stgs[j % 2]
                ltok = st.setdefault("lift_tok_slot", {}).get(j % 2)
                if ltok is not None:
                    kb.wait_tok("vector", ltok)
                kb.op("vector", lambda e, j=j, b=bufs[j], stg=stg: e.tensor_tensor(
                    stg, pbh[2 + j][0:64, 0:TW], b, ALU.mult))
                vvl = kb.commit("vector", "V")
                kb.wait("gpsimd", "V", vvl)
                tok = kb.dma("gpsimd", AO[64:128, qc * TW:(qc + 1) * TW], stg, None)
                st.setdefault("ao_lift", []).append(tok)
                st["lift_tok_slot"][j % 2] = tok
        ttv = kb.commit("vector", "V")
        for j in pair:
            sbv[sbanks[j]] = ttv
            pvv[j] = ttv


def emit_v_proj(st, t0, nk, lhs_fn, rhs_fn):
    kb = st["kb"]
    pb, VG = st["pb"], st["VG"]
    nblk = 4 * 65
    vg_free = (st["s"] // 128) * nblk
    for tc4 in range(TW // 128):
        tca = (t0 // 128) + tc4
        kb.guard("tensor", "V")
        for kc in range(nk):
            kb.op("tensor", lambda e, kc=kc, tc4=tc4, f=(kc == 0),
                  g=(kc == nk - 1): e.matmul(
                st["pbh"][4][0:128, 0:256], lhs_fn(kc, tc4), rhs_fn(kc),
                start=f, stop=g),
                inc=("T" if kc == nk - 1 else None))
        tt = kb.val("T")
        kb.wait("vector", "T", tt)
        out_ap = ap_of(VG, tca * nblk, [[vg_free, 128], [65, 4], [1, 64]])
        in_ap = ap_of(st["pbh"][4], 0, [[512, 128], [64, 4], [1, 64]])
        kb.op("vector", lambda e, o=out_ap, i=in_ap: e.tensor_copy(o, i))
        kb.commit("vector", "V")
    ones_ap = ap_of(VG, (t0 // 128) * nblk + 64,
                    [[vg_free, 128], [nblk, TW // 128], [65, 4], [1, 1]])
    kb.op("vector", lambda e, o=ones_ap: e.memset(o, 1.0))
    kb.commit("vector", "V")


def emit_v_proj_local(st, t0, lwv):
    kb = st["kb"]
    wst, xst = st["wst"], st["xst"]
    nblk = 4 * 65
    vg_free = (st["s"] // 128) * nblk

    def acc_ap(tc4, dims):
        return ap_of(st["pbh"][4 + (tc4 // 2)], (tc4 % 2) * 256, dims)

    kb.guard("tensor", "V")
    for kc in range(NC):
        buf = wst[:, (kc % 2) * 1024:(kc % 2) * 1024 + LAT]
        wl = band_load(st, buf, lwv[kc * 128:(kc + 1) * 128, :], LAT)
        kb.wait_tok("tensor", wl)
        for tc4 in range(TW // 128):
            lhs = xst[:, kc * TW + tc4 * 128: kc * TW + (tc4 + 1) * 128]
            out = acc_ap(tc4, [[512, 128], [1, 256]])
            kb.op("tensor", lambda e, lhs=lhs, out=out, buf=buf, kc=kc,
                  f=(kc == 0 and tc4 % 2 == 0),
                  g=(kc == NC - 1 and tc4 % 2 == 1):
                  e.matmul(out, lhs, buf, start=f, stop=g),
                  inc=("T" if tc4 == TW // 128 - 1 else None))
    tt = kb.val("T")
    kb.wait("vector", "T", tt)
    for tc4 in range(TW // 128):
        tca = (t0 // 128) + tc4
        out_ap = ap_of(st["VG"], tca * nblk, [[vg_free, 128], [65, 4], [1, 64]])
        in_ap = acc_ap(tc4, [[512, 128], [64, 4], [1, 64]])
        kb.op("vector", lambda e, o=out_ap, i=in_ap: e.tensor_copy(o, i))
        kb.drain("vector")
    kb.commit("vector", "V")
    ones_ap = ap_of(st["VG"], (t0 // 128) * nblk + 64,
                    [[vg_free, 128], [nblk, TW // 128], [65, 4], [1, 1]])
    kb.op("vector", lambda e, o=ones_ap: e.memset(o, 1.0))
    kb.commit("vector", "V")


def emit_store_s(st):
    kb, s = st["kb"], st["s"]
    kb.guard("gpsimd", "V")
    kb.dma_group("gpsimd",
                 [(st["s_scr"][ch * 128:(ch + 1) * 128, :],
                   st["xn"][:, ch * s:(ch + 1) * s]) for ch in range(NC)], None)



def emit_wquant(st):
    kb = st["kb"]
    c, pb, pbh, tb, cs, mw = (st["consts"], st["pb"], st["pbh"], st["tb"],
                              st["cs"], st["mw"])
    wf, wq16, sgn2 = st["wf"], st["wq16"], st["sgn2"]
    tmp1, tmp2 = st["tmp1"], st["tmp2"]

    specs = [
        ("w_gate", st["wgq_scr"], INNER, 0, 0),
        ("w_up", st["wuq_scr"], INNER, 32, 1),
        ("w_down", st["wdq_scr"], D, 64, 2),
    ]
    for (wname, scr, nout, cs_off, mw_i) in specs:
        src = st["w"][wname]
        nin_ch = src.tensor.shape[0] // 128

        def wfs(ic, nout=nout):
            sl = ic % 2
            return wf[:, sl * INNER: sl * INNER + nout]

        toks = {}
        rv = {}
        kb.guard("vector", "SC")
        kb.guard("sync", "V", "SC")
        toks[0] = kb.dma("sync", wfs(0), src[0:128, :], None)
        for ic in range(nin_ch):
            if ic + 1 < nin_ch:
                if ic >= 1:
                    kb.wait("sync", "V", rv[ic - 1])
                toks[ic + 1] = kb.dma("sync", wfs(ic + 1),
                                      src[(ic + 1) * 128:(ic + 2) * 128, :], None)
            kb.wait_tok("vector", toks[ic])
            rv[ic] = kb.op("vector", lambda e, ic=ic, wfs=wfs: e.reduce_sum(
                tmp1[:, ic:ic + 1], wfs(ic), AX.X, apply_absolute_value=True),
                inc="V")
        kb.drain("vector")
        kb.op("vector", lambda e, n=nin_ch: e.reduce_sum(
            tmp2[:, 0:1], tmp1[:, 0:n], AX.X))
        va = kb.commit("vector", "V")
        kb.wait("tensor", "V", va)
        tt = kb.op("tensor", lambda e: e.matmul(
            pbh[5][:, 0:1], tb["onesq"][:, :], tmp2[:, 0:1], start=True, stop=True),
            inc="T")
        kb.wait("vector", "T", tt)
        kb.op("vector", lambda e, mw_i=mw_i: e.tensor_scalar(
            mw[:, mw_i:mw_i + 1], pbh[5][:, 0:1], 1.0 / (1 << 22), 1e-5,
            ALU.mult, ALU.max))
        kb.drain("vector")
        kb.op("vector", lambda e, mw_i=mw_i: e.tensor_scalar(
            tmp1[:, 0:1], mw[:, mw_i:mw_i + 1], 0.5, None, ALU.mult))
        kb.drain("vector")
        kb.op("vector", lambda e, mw_i=mw_i: e.tensor_scalar(
            tmp1[:, 1:2], mw[:, mw_i:mw_i + 1], -0.5, None, ALU.mult))
        vb = kb.commit("vector", "V")
        kb.wait("scalar", "V", vb)

        def wqs(ic, nout=nout):
            sl = ic % 2
            return wq16[:, sl * INNER: sl * INNER + nout]

        def sgs(ic, nout=nout):
            sl = ic % 2
            return sgn2[:, sl * INNER: sl * INNER + nout]

        slot_store = st.setdefault("wq_store_toks", {})
        colsum_t = {}
        toksB = {}
        sc_after = {}
        kb.guard("sync", "SC", "V", "T")
        toksB[0] = kb.dma("sync", wfs(0), src[0:128, :], None)
        for ic in range(nin_ch):
            sl = ic % 2
            if ic + 1 < nin_ch:
                if ic >= 1:
                    kb.wait("sync", "SC", sc_after[ic - 1])
                toksB[ic + 1] = kb.dma("sync", wfs(ic + 1),
                                       src[(ic + 1) * 128:(ic + 2) * 128, :], None)
            kb.wait_tok("scalar", toksB[ic])
            if sl in colsum_t:
                kb.wait("scalar", "T", colsum_t[sl])
            if slot_store.get(sl) is not None:
                kb.wait_tok("scalar", slot_store[sl])
            kb.op("scalar", lambda e, ic=ic, wqs=wqs, wfs=wfs: e.activation(
                wqs(ic), wfs(ic), AF.Sign, bias=tmp1[:, 1:2]))
            s2 = kb.op("scalar", lambda e, ic=ic, sgs=sgs, wfs=wfs: e.activation(
                sgs(ic), wfs(ic), AF.Sign, bias=tmp1[:, 0:1]), inc="SC")
            sc_after[ic] = s2
            kb.wait("vector", "SC", s2)
            kb.op("vector", lambda e, ic=ic, wqs=wqs, sgs=sgs: e.tensor_tensor(
                wqs(ic), wqs(ic), sgs(ic), ALU.add))
            vq = kb.commit("vector", "V")
            kb.wait("gpsimd", "V", vq)
            slot_store[sl] = kb.dma("gpsimd", scr[ic * 128:(ic + 1) * 128, :],
                                    wqs(ic), None)
            kb.wait("tensor", "V", vq)
            for os_ in range(nout // 128):
                o0 = sl * INNER + os_ * 128
                kb.op("tensor", lambda e, os_=os_, ic=ic, cs_off=cs_off, o0=o0,
                      f=(ic == 0 and os_ == 0),
                      g=(ic == nin_ch - 1 and os_ == nout // 128 - 1): e.matmul(
                          pbh[6][:, cs_off + os_: cs_off + os_ + 1],
                          wq16[:, o0:o0 + 128],
                          tb["ones16"][:, 0:1],
                          start=f, stop=g),
                      inc=("T" if os_ == nout // 128 - 1 else None))
            colsum_t[sl] = kb.val("T")
        tt = kb.val("T")
        kb.wait("vector", "T", tt)
        kb.op("vector", lambda e, cs_off=cs_off, n=nout // 128: e.tensor_scalar(
            cs[:, cs_off:cs_off + n], pbh[6][:, cs_off:cs_off + n], MAGIC, None,
            ALU.mult))
        kb.commit("vector", "V")



def bcast_row(st, row_ap, dst_sb):
    kb, tb = st["kb"], st["tb"]
    kb.guard("tensor", "V")
    tt = kb.op("tensor", lambda e: e.matmul(
        st["pbh"][7][:, :], tb["onesq"][0:1, :], row_ap, start=True, stop=True),
        inc="T")
    kb.wait("vector", "T", tt)
    kb.op("vector", lambda e: e.tensor_copy(dst_sb[:, :], st["pbh"][7][:, :]))
    return kb.commit("vector", "V")


def emit_amax(st, chunk_fn, nch, dst_row):
    kb = st["kb"]
    accs = [st["tmp1"], st["tmp2"], st["tmpsq"], st["tmp4"]]
    na = min(len(accs), max(1, nch // 2))
    if st.get("tmps_sc") is not None:
        kb.wait("vector", "SC", st["tmps_sc"])
    for i in range(na):
        kb.op("vector", lambda e, i=i: e.tensor_tensor(
            accs[i][:, :], chunk_fn(i), chunk_fn(na + i), ALU.max))
    kb.drain("vector")
    nxt = 2 * na
    while nxt < nch:
        for i in range(na):
            if nxt + i < nch:
                kb.op("vector", lambda e, i=i, j=nxt + i: e.tensor_tensor(
                    accs[i][:, :], accs[i][:, :], chunk_fn(j), ALU.max))
        kb.drain("vector")
        nxt += na
    done = 0
    while done < nch:
        for i in range(na):
            if done + i < nch:
                kb.op("vector", lambda e, i=i, j=done + i: e.scalar_tensor_tensor(
                    accs[i][:, :], chunk_fn(j), -1.0, accs[i][:, :],
                    ALU.mult, ALU.max))
        kb.drain("vector")
        done += na
    step = na // 2
    while step >= 1:
        for i in range(step):
            kb.op("vector", lambda e, i=i, s2=step: e.tensor_tensor(
                accs[i][:, :], accs[i][:, :], accs[i + s2][:, :], ALU.max))
        kb.drain("vector")
        step //= 2
    vv = kb.commit("vector", "V")
    kb.wait("gpsimd", "V", vv)
    kb.op("gpsimd", lambda e: e.reduce_max(R(st, 0), accs[0][:, :], AX.C))
    gg = kb.commit("gpsimd", "G")
    kb.wait("vector", "G", gg)
    kb.op("vector", lambda e, d=dst_row: e.tensor_scalar(
        d, R(st, 0), 1e-5, None, ALU.max))
    return kb.commit("vector", "V")


def emit_ffn(st):
    kb, s, nt = st["kb"], st["s"], st["nt"]
    pb, pbh, tb, c, cs, mw = (st["pb"], st["pbh"], st["tb"], st["consts"],
                              st["cs"], st["mw"])
    arena, z1, z2, prod, fw = st["arena"], st["z1"], st["z2"], st["prod"], st["fw"]
    Pt = st["Pt"]
    sxb, tgb, tub, tdb = st["sxb"], st["tgb"], st["tub"], st["tdb"]
    tmp1, tmp2, tmpsq = st["tmp1"], st["tmp2"], st["tmpsq"]
    tmps4 = [tmp1, tmp2, tmpsq, st["tmp4"]]
    xoc = st["xoc"]
    XNT = 0
    S0 = NC * TW
    bankv = st.setdefault("ffn_bankv", {})

    def bank_wait(banks):
        for b in banks:
            v = bankv.get(b)
            if v is not None:
                kb.wait("tensor", "V", v)

    def A(base, ch):
        return arena[:, base + ch * TW: base + (ch + 1) * TW]

    STT_ = NC * TW

    def emit_q1g(t):
        t0 = t * TW
        kb.guard("gpsimd", "V")
        pairs = []
        for ch in range(NC):
            pairs.append((A(XNT, ch),
                          st["xn_scr"][ch * 128:(ch + 1) * 128, t0:t0 + TW]))
        tl = kb.dma_group("gpsimd", pairs, None)
        kb.wait_tok("vector", tl)

        av = emit_amax(st, lambda ch: A(XNT, ch), NC, R(st, 2))
        kb.op("vector", lambda e: e.tensor_scalar(
            R(st, 0), R(st, 2), mw[0:1, 0:1], 1.0 / 254.0, ALU.mult, ALU.mult))
        kb.op("vector", lambda e: e.tensor_scalar(
            R(st, 1), R(st, 2), mw[0:1, 1:2], 1.0 / 254.0, ALU.mult, ALU.mult))
        vv2 = kb.commit("vector", "V")
        kb.wait("scalar", "V", vv2)
        kb.op("scalar", lambda e: e.activation(
            st["rows"][32:33, TW:2 * TW], R(st, 2), AF.Ln))
        kb.drain("scalar")
        sxv = kb.op("scalar", lambda e: e.activation(
            R(st, 2), st["rows"][32:33, TW:2 * TW], AF.Exp,
            bias=c[0:1, 3:4], scale=-1.0), inc="SC")
        kb.guard("tensor", "V")
        kb.wait("tensor", "V", vv2)
        t1 = kb.op("tensor", lambda e: e.matmul(
            pbh[7][:, :], tb["onesq"][0:1, :], R(st, 0), start=True, stop=True))
        t2 = kb.op("tensor", lambda e: e.matmul(
            pbh[6][:, :], tb["onesq"][0:1, :], R(st, 1), start=True, stop=True),
            inc="T")
        kb.wait("vector", "T", kb.val("T"))
        kb.op("vector", lambda e: e.tensor_copy(tgb[:, :], pbh[7][:, :]))
        kb.op("vector", lambda e: e.tensor_copy(tub[:, :], pbh[6][:, :]))
        cbv = kb.commit("vector", "V")
        bankv[6] = cbv
        kb.wait("tensor", "SC", sxv)
        kb.wait("tensor", "V", cbv)
        kb.op("tensor", lambda e: e.matmul(
            pbh[7][:, :], tb["onesq"][0:1, :], R(st, 2), start=True, stop=True),
            inc="T")
        kb.wait("vector", "T", kb.val("T"))
        kb.op("vector", lambda e: e.tensor_copy(sxb[:, :], pbh[7][:, :]))
        bankv[7] = kb.commit("vector", "V")

        prevsc = st.get("tmps_sc")
        for b0 in (0, 4):
            if prevsc is not None:
                kb.wait("vector", "SC", prevsc)
            for i in range(4):
                ch = b0 + i
                kb.op("vector", lambda e, ch=ch, i=i: e.tensor_tensor(
                    tmps4[i][:, :], A(XNT, ch), sxb[:, :], ALU.mult))
            zv = kb.commit("vector", "V")
            kb.wait("scalar", "V", zv)
            if b0 == 0 and st.get("ffn_gu_last_t") is not None:
                kb.wait("scalar", "T", st["ffn_gu_last_t"])
            for i in range(4):
                ch = b0 + i
                kb.op("scalar", lambda e, ch=ch, i=i: e.activation(
                    z1[:, ch * TW:(ch + 1) * TW], tmps4[i][:, :], AF.Relu,
                    bias=c[:, 4:5], scale=1.0), inc=("SC" if i == 3 else None))
            prevsc = kb.val("SC")
            st["tmps_sc"] = prevsc
        st["ffn_z1v"] = prevsc

    def gu_load(og, slot):
        return kb.dma_group("sync" if og % 2 == 0 else "gpsimd", [
            (ap_of(fw, slot * S0, [[3 * NC * TW, 128], [256, NC], [1, 256]]),
             ap_of(st["wgq_scr"].tensor, og * 256,
                   [[INNER, 128], [128 * INNER, NC], [1, 256]])),
            (ap_of(fw, slot * S0 + NC * 256,
                   [[3 * NC * TW, 128], [256, NC], [1, 256]]),
             ap_of(st["wuq_scr"].tensor, og * 256,
                   [[INNER, 128], [128 * INNER, NC], [1, 256]])),
        ], None)

    def emit_gu(t):
        kb.wait("tensor", "SC", st["ffn_z1v"])
        gu_t_after = {}
        gu_toks = {}
        kb.guard("sync", "T")
        kb.guard("gpsimd", "T")
        gu_toks[0] = gu_load(0, 0)
        for og in range(16):
            slot = og % 2
            bs = [0, 1, 2, 3] if og % 2 == 0 else [4, 5, 6, 7]
            if og + 1 < 16:
                if og >= 1:
                    kb.wait("sync" if (og + 1) % 2 == 0 else "gpsimd",
                            "T", gu_t_after[og - 1])
                gu_toks[og + 1] = gu_load(og + 1, (og + 1) % 2)
            kb.wait_tok("tensor", gu_toks[og])
            bank_wait(bs)
            for ic in range(NC):
                for oo in range(2):
                    off = slot * S0 + ic * 256 + oo * 128
                    kb.op("tensor", lambda e, off=off, oo=oo, ic=ic, bs=bs,
                          f=(ic == 0), g=(ic == NC - 1): e.matmul(
                        pb[bs[oo]], fw[:, off:off + 128],
                        z1[:, ic * TW:(ic + 1) * TW], start=f, stop=g))
                for oo in range(2):
                    off = slot * S0 + NC * 256 + ic * 256 + oo * 128
                    kb.op("tensor", lambda e, off=off, oo=oo, ic=ic, bs=bs,
                          f=(ic == 0), g=(ic == NC - 1): e.matmul(
                        pb[bs[2 + oo]], fw[:, off:off + 128],
                        z1[:, ic * TW:(ic + 1) * TW], start=f, stop=g),
                        inc=("T" if (ic == NC - 1 and oo == 1) else None))
            gu_t_after[og] = kb.val("T")
            tt = kb.val("T")
            kb.wait("vector", "T", tt)
            if st.get("tmps_sc") is not None:
                kb.wait("vector", "SC", st["tmps_sc"])
            for oo in range(2):
                oc = og * 2 + oo
                kb.op("vector", lambda e, oo=oo, oc=oc, bs=bs:
                      e.scalar_tensor_tensor(
                          tmps4[oo][:, :], pb[bs[oo]], cs[:, oc:oc + 1],
                          tgb[:, :], ALU.subtract, ALU.mult))
            cv = kb.commit("vector", "V")
            kb.wait("scalar", "V", cv)
            for oo in range(2):
                kb.op("scalar", lambda e, oo=oo: e.activation(
                    Pt[:, oo * TW:(oo + 1) * TW], tmps4[oo][:, :], AF.Silu),
                    inc=("SC" if oo == 1 else None))
            sg = kb.val("SC")
            st["tmps_sc"] = sg
            kb.wait("vector", "SC", sg)
            for oo in range(2):
                oc = og * 2 + oo
                kb.op("vector", lambda e, oo=oo, oc=oc, bs=bs:
                      e.scalar_tensor_tensor(
                          tmps4[2 + oo][:, :], pb[bs[2 + oo]],
                          cs[:, 32 + oc:33 + oc], tub[:, :],
                          ALU.subtract, ALU.mult))
            kb.drain("vector")
            for oo in range(2):
                oc = og * 2 + oo
                kb.op("vector", lambda e, oo=oo, oc=oc: e.tensor_tensor(
                    prod[:, oc * TW:(oc + 1) * TW], Pt[:, oo * TW:(oo + 1) * TW],
                    tmps4[2 + oo][:, :], ALU.mult))
            ev = kb.commit("vector", "V")
            for b in bs:
                bankv[b] = ev
        st["ffn_gu_last_t"] = kb.val("T")

    def emit_q2(t):
        av = emit_amax(st, lambda chh: prod[:, chh * TW:(chh + 1) * TW], 32,
                       R(st, 2))
        kb.op("vector", lambda e: e.tensor_scalar(
            R(st, 1), R(st, 2), mw[0:1, 2:3], 1.0 / 254.0, ALU.mult, ALU.mult))
        vv2 = kb.commit("vector", "V")
        kb.wait("scalar", "V", vv2)
        kb.op("scalar", lambda e: e.activation(
            st["rows"][32:33, TW:2 * TW], R(st, 2), AF.Ln))
        kb.drain("scalar")
        sxv = kb.op("scalar", lambda e: e.activation(
            R(st, 2), st["rows"][32:33, TW:2 * TW], AF.Exp,
            bias=c[0:1, 3:4], scale=-1.0), inc="SC")
        kb.guard("tensor", "V")
        kb.wait("tensor", "V", vv2)
        kb.op("tensor", lambda e: e.matmul(
            pbh[6][:, :], tb["onesq"][0:1, :], R(st, 1), start=True, stop=True))
        kb.wait("tensor", "SC", sxv)
        kb.op("tensor", lambda e: e.matmul(
            pbh[7][:, :], tb["onesq"][0:1, :], R(st, 2), start=True, stop=True),
            inc="T")
        kb.wait("vector", "T", kb.val("T"))
        kb.op("vector", lambda e: e.tensor_copy(tdb[:, :], pbh[6][:, :]))
        kb.op("vector", lambda e: e.tensor_copy(sxb[:, :], pbh[7][:, :]))
        cbv = kb.commit("vector", "V")
        bankv[6] = cbv
        bankv[7] = cbv
        prevsc = st.get("tmps_sc")
        for b0 in range(0, 32, 4):
            if prevsc is not None:
                kb.wait("vector", "SC", prevsc)
            for i in range(4):
                chh = b0 + i
                kb.op("vector", lambda e, chh=chh, i=i: e.tensor_tensor(
                    tmps4[i][:, :], prod[:, chh * TW:(chh + 1) * TW], sxb[:, :],
                    ALU.mult))
            zv = kb.commit("vector", "V")
            kb.wait("scalar", "V", zv)
            if b0 == 0 and st.get("ffn_dn_last_t") is not None:
                kb.wait("scalar", "T", st["ffn_dn_last_t"])
            for i in range(4):
                chh = b0 + i
                kb.op("scalar", lambda e, chh=chh, i=i: e.activation(
                    z2[:, chh * TW:(chh + 1) * TW], tmps4[i][:, :], AF.Relu,
                    bias=c[:, 4:5], scale=1.0), inc=("SC" if i == 3 else None))
            prevsc = kb.val("SC")
            st["tmps_sc"] = prevsc
        st["ffn_z2v"] = prevsc

    def dn_load(oc):
        return kb.dma("sync", ap_of(fw, 2 * S0,
                                    [[3 * NC * TW, 128], [128, 32], [1, 128]]),
                      ap_of(st["wdq_scr"].tensor, oc * 128,
                            [[D, 128], [128 * D, 32], [1, 128]]), None)

    def emit_dn(t):
        t0 = t * TW
        kb.guard("gpsimd", "V")
        stl = kb.dma_group("gpsimd", [
            (A(STT_, ch), st["s_scr"][ch * 128:(ch + 1) * 128, t0:t0 + TW])
            for ch in range(NC)], None)
        kb.wait("tensor", "SC", st["ffn_z2v"])
        dn_t_after = {}
        dn_toks = {}
        xtoks = {}
        xv_after = {}
        dn_ev = {}
        kb.guard("sync", "T")
        dn_toks[0] = dn_load(0)
        kb.guard("gpsimd", "V")
        xtoks[0] = kb.dma("gpsimd", xoc[:, 0:512],
                          st["xT"][0:128, t0:t0 + TW], None)
        for oc in range(NC):
            bank = 4 + oc % 2
            if oc + 1 < NC:
                if oc >= 1:
                    kb.wait("gpsimd", "V", xv_after[oc - 1])
                xtoks[oc + 1] = kb.dma(
                    "gpsimd",
                    xoc[:, ((oc + 1) % 2) * 512:((oc + 1) % 2) * 512 + 512],
                    st["xT"][(oc + 1) * 128:(oc + 2) * 128, t0:t0 + TW], None)
            kb.wait_tok("tensor", dn_toks[oc])
            bank_wait([bank])
            for ic in range(32):
                off = 2 * S0 + ic * 128
                kb.op("tensor", lambda e, ic=ic, off=off, bank=bank,
                      f=(ic == 0), g=(ic == 31): e.matmul(
                    pb[bank], fw[:, off:off + 128],
                    z2[:, ic * TW:(ic + 1) * TW], start=f, stop=g),
                    inc=("T" if ic == 31 else None))
            dn_t_after[oc] = kb.val("T")
            if oc + 1 < NC:
                kb.wait("sync", "T", dn_t_after[oc])
                dn_toks[oc + 1] = dn_load(oc + 1)
            tt = kb.val("T")
            kb.wait("vector", "T", tt)
            if st.get("tmps_sc") is not None:
                kb.wait("vector", "SC", st["tmps_sc"])
            kb.op("vector", lambda e, bank=bank, oc=oc: e.scalar_tensor_tensor(
                tmp1[:, :], pb[bank], cs[:, 64 + oc:65 + oc], tdb[:, :],
                ALU.subtract, ALU.mult))
            vv = kb.commit("vector", "V")
            kb.wait("scalar", "V", vv)
            sg = kb.op("scalar", lambda e: e.activation(
                tmpsq[:, :], tmp1[:, :], AF.Silu), inc="SC")
            st["tmps_sc"] = sg
            kb.wait("vector", "SC", sg)
            if oc == 0:
                kb.wait_tok("vector", stl)
            kb.op("vector", lambda e, oc=oc: e.tensor_tensor(
                tmp2[:, :], tmpsq[:, :], A(STT_, oc), ALU.mult))
            kb.drain("vector")
            kb.wait_tok("vector", xtoks[oc])
            kb.op("vector", lambda e, oc=oc: e.tensor_tensor(
                tmp2[:, :], tmp2[:, :],
                xoc[:, (oc % 2) * 512:(oc % 2) * 512 + 512], ALU.add))
            vv = kb.commit("vector", "V")
            xv_after[oc] = vv
            bankv[bank] = vv
            kb.wait("gpsimd", "V", vv)
            otok = kb.dma("gpsimd",
                          st["outT"][oc * 128:(oc + 1) * 128, t0:t0 + TW],
                          tmp2[:, :], None)
            kb.wait_tok("vector", otok)
        st["ffn_dn_last_t"] = kb.val("T")

    emit_q1g(0)
    emit_gu(0)
    for t in range(1, nt):
        emit_q1g(t)
        emit_q2(t - 1)
        emit_gu(t)
        emit_dn(t - 1)
    emit_q2(nt - 1)
    emit_dn(nt - 1)



def kernel(**inputs):
    ins = {k: np.asarray(v) for k, v in inputs.items()}
    x = ins["L0_attn_hidden"]
    xTs = [np.ascontiguousarray(x[b].T) for b in range(x.shape[0])]

    qp = _q_col_perm()
    kp = _k_col_perm()
    wop = _wo_row_perm()
    wmap = {
        "g_wq": ins["L0_attn_g_wq"][:, qp],
        "g_kv_down": ins["L0_attn_g_kv_down"],
        "g_k_up": ins["L0_attn_g_k_up"][:, kp],
        "g_v_up": ins["L0_attn_g_v_up"],
        "g_wo": ins["L0_attn_g_wo"][wop, :],
        "l_wq": ins["L0_attn_l_wq"][:, qp],
        "l_wk": ins["L0_attn_l_wk"][:, kp],
        "l_wv": ins["L0_attn_l_wv"],
        "l_wo": ins["L0_attn_l_wo"][wop, :],
        "w_gate": ins["L0_ffn_w_gate"].T,
        "w_up": ins["L0_ffn_w_up"].T,
        "w_down": ins["L0_ffn_w_down"].T,
        "norm1_w": np.ascontiguousarray(ins["L0_attn_norm1_w"].reshape(NC, 128).T),
        "norm2_w": np.ascontiguousarray(ins["L0_attn_norm2_w"].reshape(NC, 128).T),
    }
    wmap = {k: np.ascontiguousarray(np.asarray(v, dtype=np.float32))
            for k, v in wmap.items()}
    tmap = host_tables(S)

    prog = build_program()
    in_maps = []
    for b in range(x.shape[0]):
        m = {"xT": xTs[b]}
        m.update(wmap)
        m.update(tmap)
        in_maps.append(m)
    res = run_bass_kernel_spmd(prog, in_maps, core_ids=list(range(len(in_maps))))
    out = np.stack([np.ascontiguousarray(r["outT"].T) for r in res.results])
    return out


# version 11
# speedup vs baseline: 1.0029x; 1.0029x over previous
import numpy as np

import concourse.bass as bass
import concourse.mybir as mybir
from concourse.bass_utils import run_bass_kernel_spmd

F32 = mybir.dt.float32
F16 = mybir.dt.float16
AF = mybir.ActivationFunctionType
ALU = mybir.AluOpType
AX = mybir.AxisListType

D = 1024
S = 2048
B = 8
H = 16
KVH = 4
HD = 64
LAT = 256
WIN = 256
INNER = 4096
NC = D // 128
TW = 512
MAGIC = 1536.0

Q_CHUNK_HEADS = [(0, 4), (1, 5), (2, 6), (3, 7), (8, 12), (9, 13), (10, 14), (11, 15)]
ENGS = ("sync", "tensor", "vector", "scalar", "gpsimd")

DEBUG_STAGE = None
KEEP_SCOPES = False
SKIP_TABS = False
DEBUG_ROPE = None



def _rope_perm64():
    return np.concatenate([np.arange(0, 64, 2), np.arange(1, 64, 2)])


def _q_col_perm():
    rp = _rope_perm64()
    cols = []
    for (ha, hb) in Q_CHUNK_HEADS:
        cols.append(64 * ha + rp)
        cols.append(64 * hb + rp)
    return np.concatenate(cols)


def _k_col_perm():
    rp = _rope_perm64()
    return np.concatenate([64 * kv + rp for kv in range(KVH)])


def _wo_row_perm():
    rows = []
    for (ha, hb) in Q_CHUNK_HEADS:
        rows.append(64 * ha + np.arange(64))
        rows.append(64 * hb + np.arange(64))
    return np.concatenate(rows)


def _rope_tables(base, s):
    inv = np.power(float(base), -np.arange(0, 32, dtype=np.float64) / 32.0)
    th = np.arange(s, dtype=np.float64)[:, None] * inv[None, :]
    cos = np.cos(th).T
    sin = np.sin(th).T
    C = np.empty((128, s), dtype=np.float32)
    Sg = np.empty((128, s), dtype=np.float32)
    for blk in range(4):
        r = 32 * blk
        C[r:r + 32] = cos.astype(np.float32)
        Sg[r:r + 32] = (sin if blk % 2 else -sin).astype(np.float32)
    return C, Sg


def _swap_mat():
    P = np.zeros((128, 128), dtype=np.float32)
    for p in range(128):
        blk = (p // 64) * 64
        o = p - blk
        P[p, blk + (o + 32) % 64] = 1.0
    return P


def host_tables(s):
    kk = np.arange(128)[:, None]
    qq = np.arange(128)[None, :]
    Cg, Sgt = _rope_tables(1e6, s)
    Cl, Slt = _rope_tables(1e4, s)
    ind2 = np.zeros((128, 2), dtype=np.float32)
    ind2[:64, 0] = 1.0
    ind2[64:, 1] = 1.0
    t = {
        "Cg": Cg, "Sg": Sgt, "Cl": Cl, "Sl": Slt,
        "mask_diag": (kk <= qq).astype(np.float32),
        "mask_su": (kk > qq).astype(np.float32),
        "swap": _swap_mat(),
        "onesq": np.ones((128, 128), dtype=np.float32),
        "ind2": ind2,
        "ind2T": np.ascontiguousarray(ind2.T),
        "ones16": np.ones((128, 1), dtype=np.float16),
    }
    return {k: np.ascontiguousarray(v) for k, v in t.items()}



class KB:
    def __init__(self, nc):
        self.nc = nc
        self.steps = {e: [] for e in ENGS}
        self.counts = {}
        self.sems = {}
        self._ctx = []
        self._scopes = []

    def sem(self, name):
        cm = self.nc.semaphore(name)
        h = cm.__enter__()
        self._ctx.append(cm)
        self.sems[name] = h
        self.counts[name] = 0
        return h

    def sbuf(self, name, shape, dt, scoped=False):
        cm = self.nc.sbuf_tensor(name, shape, dt)
        h = cm.__enter__()
        (self._scopes[-1] if scoped else self._ctx).append(cm)
        return h

    def psum(self, name, shape, dt=F32):
        cm = self.nc.psum_tensor(name, shape, dt)
        h = cm.__enter__()
        self._ctx.append(cm)
        return h

    def push_scope(self):
        self._scopes.append([])

    def pop_scope(self):
        for cm in reversed(self._scopes.pop()):
            cm.__exit__(None, None, None)

    def close(self):
        while self._scopes:
            self.pop_scope()
        while self._ctx:
            self._ctx.pop().__exit__(None, None, None)

    def _push(self, eng, fn):
        self.steps[eng].append(fn)

    def wait(self, eng, sem_name, value):
        h = self.sems[sem_name]
        v = int(value)
        if v > 0:
            self._push(eng, lambda e: e.wait_ge(h, v))

    def guard(self, eng, *sem_names):
        for sn in sem_names:
            self.wait(eng, sn, self.counts[sn])

    def op(self, eng, make, inc=None):
        if inc is None:
            self._push(eng, lambda e: make(e))
            return None
        h = self.sems[inc]
        self.counts[inc] += 1
        self._push(eng, lambda e: make(e).then_inc(h, 1))
        return self.counts[inc]

    def drain(self, eng):
        self._push(eng, lambda e: e.drain())

    def commit(self, eng, sem_name):
        self.drain(eng)
        if eng in ("vector", "gpsimd"):
            return self.op(eng, lambda e: e.engine_nop(), inc=sem_name)
        return self.op(eng, lambda e: e.nop(), inc=sem_name)

    def dma(self, eng, out, in_, sem_name):
        return self.dma_group(eng, [(out, in_)], sem_name)

    def dma_group(self, eng, pairs, sem_name):
        pool = self.dma_pools[eng]
        idx = self.dma_rot[eng] % len(pool)
        self.dma_rot[eng] += 1
        sn = pool[idx]
        self.wait(eng, sn, self.counts[sn])
        h = self.sems[sn]
        for (out, in_) in pairs:
            self.counts[sn] += 16
            self._push(eng, lambda e, o=out, i=in_: e.dma_start(out=o, in_=i)
                       .then_inc(h, 16))
        return (sn, self.counts[sn])

    def wait_tok(self, eng, tok):
        self.wait(eng, tok[0], tok[1])

    def val(self, sem_name):
        return self.counts[sem_name]

    def barrier(self):
        for e in ENGS:
            for sn in list(self.counts):
                self.wait(e, sn, self.counts[sn])


def ap_of(handle, off, dims):
    return bass.AP(handle, off, dims)



def build_program():
    s = S
    nt = s // TW

    nc = bass.Bass("TRN2", target_bir_lowering=False)
    kb = KB(nc)

    xT = nc.dram_tensor("xT", [D, s], F32, kind="ExternalInput").ap()
    w = {}
    for name, shape in [
        ("g_wq", [D, D]), ("g_kv_down", [D, LAT]), ("g_k_up", [LAT, LAT]),
        ("g_v_up", [LAT, LAT]), ("g_wo", [D, D]),
        ("l_wq", [D, D]), ("l_wk", [D, LAT]), ("l_wv", [D, LAT]), ("l_wo", [D, D]),
        ("w_gate", [D, INNER]), ("w_up", [D, INNER]), ("w_down", [INNER, D]),
        ("norm1_w", [128, NC]), ("norm2_w", [128, NC]),
    ]:
        w[name] = nc.dram_tensor(name, shape, F32, kind="ExternalInput").ap()
    tab = {}
    for name, shape, dt in [
        ("Cg", [128, s], F32), ("Sg", [128, s], F32),
        ("Cl", [128, s], F32), ("Sl", [128, s], F32),
        ("mask_diag", [128, 128], F32), ("mask_su", [128, 128], F32),
        ("swap", [128, 128], F32), ("onesq", [128, 128], F32),
        ("ind2", [128, 2], F32), ("ind2T", [2, 128], F32),
        ("ones16", [128, 1], F16),
    ]:
        tab[name] = nc.dram_tensor(name, shape, dt, kind="ExternalInput").ap()
    outT = nc.dram_tensor("outT", [D, s], F32, kind="ExternalOutput").ap()
    dbgT = None
    if DEBUG_STAGE is not None:
        dbgT = nc.dram_tensor("dbgT", [D, s], F32, kind="ExternalOutput").ap()

    xn_scr = nc.dram_tensor("xn_scr", [D, s], F32).ap()
    s_scr = nc.dram_tensor("s_scr", [D, s], F32).ap()
    wgq_scr = nc.dram_tensor("wgq_scr", [D, INNER], F16).ap()
    wuq_scr = nc.dram_tensor("wuq_scr", [D, INNER], F16).ap()
    wdq_scr = nc.dram_tensor("wdq_scr", [INNER, D], F16).ap()

    for sn in ("T", "V", "SC", "G"):
        kb.sem(sn)
    kb.dma_pools = {"sync": [], "scalar": [], "gpsimd": []}
    kb.dma_rot = {"sync": 0, "scalar": 0, "gpsimd": 0}
    for q, pref in (("sync", "LW"), ("scalar", "LS"), ("gpsimd", "LG")):
        for i in range(4):
            sn = f"{pref}{i}"
            kb.sem(sn)
            kb.dma_pools[q].append(sn)

    consts = kb.sbuf("consts", [128, 16], F32)
    rows = kb.sbuf("rows", [128, 3 * TW], F32)
    tmp1 = kb.sbuf("tmp1", [128, TW], F32)
    tmp2 = kb.sbuf("tmp2", [128, TW], F32)
    tmpsq = kb.sbuf("tmpsq", [128, TW], F32)
    Pt = kb.sbuf("Pt", [128, 4 * TW], F32)
    cs = kb.sbuf("cs_sb", [128, 72], F32)
    mw = kb.sbuf("mw_sb", [128, 4], F32)
    tb = {}
    for k in ("mask_diag", "mask_su", "swap", "onesq", "ind2", "ind2T"):
        tb[k] = kb.sbuf("tb_" + k, list(tab[k].tensor.shape), F32)
    tb["ones16"] = kb.sbuf("tb_ones16", [128, 1], F16)

    pbh = [kb.psum(f"pb{i}", [128, 512]) for i in range(8)]
    pb = [t[:, :] for t in pbh]

    st = dict(nc=nc, kb=kb, s=s, nt=nt, w=w, tab=tab, tb=tb,
              consts=consts, rows=rows, tmp1=tmp1, tmp2=tmp2, tmpsq=tmpsq,
              Pt=Pt, cs=cs, mw=mw, pb=pb, pbh=pbh,
              xT=xT, outT=outT, dbgT=dbgT,
              xn_scr=xn_scr, s_scr=s_scr,
              wgq_scr=wgq_scr, wuq_scr=wuq_scr, wdq_scr=wdq_scr)

    emit_all(st)

    blk = nc.Block()
    block = blk.__enter__()
    kb._ctx.append(blk)
    for eng_name in ENGS:
        fns = kb.steps[eng_name]

        def body(e, fns=fns):
            for f in fns:
                f(e)
        getattr(block, eng_name)(body)
    kb.close()
    return nc


def R(st, k, n=TW):
    return st["rows"][0:1, k * TW:k * TW + n]


def R2(st, k, n=TW):
    return st["rows"][0:2, k * TW:k * TW + n]


def R64(st, k, n=TW):
    return st["rows"][64:65, k * TW:k * TW + n]


def R96(st, k, n=TW):
    return st["rows"][96:97, k * TW:k * TW + n]


def emit_all(st):
    kb = st["kb"]

    st["tabs_tok"] = kb.dma_group(
        "sync", [(st["tb"][k][:, :], st["tab"][k][:, :])
                 for k in ("mask_diag", "mask_su", "swap", "onesq",
                           "ind2", "ind2T", "ones16")], None)

    c = st["consts"]
    kb.op("vector", lambda e: e.memset(c[:, 0:1], -8.0))
    kb.op("vector", lambda e: e.memset(c[:, 1:2], 1e-5))
    kb.op("vector", lambda e: e.memset(c[:, 2:3], 1e-6))
    kb.op("vector", lambda e: e.memset(c[:, 3:4], float(np.log(127.0))))
    kb.op("vector", lambda e: e.memset(c[:, 4:5], MAGIC))
    kb.commit("vector", "V")
    st["consts_done"] = kb.val("V")

    kb.push_scope()
    st["xn"] = kb.sbuf("xn", [128, NC * st["s"]], F32, scoped=True)
    st["xst"] = kb.sbuf("xst", [128, NC * TW], F32, scoped=True)
    st["KG"] = kb.sbuf("KG", [128, 2 * st["s"]], F32, scoped=True)
    st["VG"] = kb.sbuf("VG", [128, (st["s"] // 128) * 4 * 65], F32, scoped=True)
    st["QT"] = kb.sbuf("QT", [128, NC * TW], F32, scoped=True)
    st["AO"] = kb.sbuf("AO", [128, NC * TW], F32, scoped=True)
    st["lat"] = kb.sbuf("lat", [128, 2 * st["s"]], F32, scoped=True)
    st["wst"] = kb.sbuf("wst", [128, 2 * 1024], F32, scoped=True)
    st["CSc"] = kb.sbuf("CSc", [128, st["s"]], F32, scoped=True)
    st["CSs"] = kb.sbuf("CSs", [128, st["s"]], F32, scoped=True)

    emit_ln(st)
    if DEBUG_STAGE == "xn_plus":
        rms_scale_bcast(st, 0)
        emit_debug_xn(st, st["xn"])
        emit_tail(st)
        return
    if DEBUG_STAGE == "xn_barrier":
        kb.barrier()
        emit_debug_xn(st, st["xn"])
        emit_tail(st)
        return
    if DEBUG_STAGE == "wq":
        kb.barrier()
        if not KEEP_SCOPES:
            kb.pop_scope()
        kb.push_scope()
        st["wf"] = kb.sbuf("wf", [128, 2 * INNER], F32, scoped=True)
        st["wq16"] = kb.sbuf("wq16", [128, 2 * INNER], F16, scoped=True)
        st["sgn2"] = kb.sbuf("sgn2", [128, 2 * INNER], F16, scoped=True)
        emit_wquant(st)
        kb.barrier()
        kb.guard("gpsimd", "V", "T", "SC")
        kb.dma("gpsimd", st["dbgT"][0:128, 0:72], st["cs"][:, :], None)
        kb.dma("gpsimd", st["outT"][0:128, 0:72], st["cs"][:, :], None)
        emit_tail(st)
        return
    if DEBUG_STAGE == "xn":
        emit_debug_xn(st, st["xn"])
        emit_tail(st)
        return
    emit_attn(st, glob=True)
    if DEBUG_STAGE in ("q0", "sdpa1"):
        kb.barrier()
        kb.guard("gpsimd", "V", "T", "SC")
        if DEBUG_STAGE == "q0":
            pairs = [(st["outT"][ch * 128:(ch + 1) * 128, 0:TW],
                      st["QT"][:, ch * TW:(ch + 1) * TW]) for ch in range(NC)]
        else:
            pairs = [(st["outT"][0:128, 0:TW], st["AO"][:, 0:TW])]
        kb.dma_group("gpsimd", pairs, None)
        emit_tail(st)
        return
    if DEBUG_STAGE in ("norm", "norm_nodma", "rms0", "rmsA", "rmsB"):
        kb.barrier()
        emit_debug_xn(st, st["xn"])
        emit_tail(st)
        return
    if DEBUG_STAGE in ("lat", "k"):
        kb.barrier()
        kb.guard("gpsimd", "V", "T", "SC")
        if DEBUG_STAGE == "lat":
            pairs = [(st["outT"][0:128, :], st["lat"][:, 0:st["s"]]),
                     (st["outT"][128:256, :], st["lat"][:, st["s"]:2 * st["s"]])]
        else:
            pairs = [(st["outT"][0:128, :], st["KG"][:, 0:st["s"]]),
                     (st["outT"][128:256, :], st["KG"][:, st["s"]:2 * st["s"]])]
        kb.dma_group("gpsimd", pairs, None)
        emit_tail(st)
        return
    if DEBUG_STAGE in ("kv", "q0"):
        kb.barrier()
        kb.guard("gpsimd", "V", "T", "SC")
        if DEBUG_STAGE == "kv":
            pairs = [(st["outT"][0:128, :], st["KG"][:, 0:st["s"]]),
                     (st["outT"][128:256, :], st["KG"][:, st["s"]:2 * st["s"]])]
            vgf = (st["s"] // 128) * 260
            off = 0
            row = 256
            while off < vgf:
                wdt = min(st["s"], vgf - off)
                pairs.append((st["outT"][row:row + 128, 0:wdt],
                              st["VG"][:, off:off + wdt]))
                off += wdt
                row += 128
        else:
            pairs = [(st["outT"][ch * 128:(ch + 1) * 128, 0:TW],
                      st["QT"][:, ch * TW:(ch + 1) * TW]) for ch in range(NC)]
        kb.dma_group("gpsimd", pairs, None)
        emit_tail(st)
        return
    if DEBUG_STAGE == "s1":
        emit_debug_xn(st, st["xn"])
        emit_tail(st)
        return
    emit_attn(st, glob=False)
    emit_store_s(st)
    if DEBUG_STAGE == "s":
        emit_debug_xn(st, st["xn"])
        emit_tail(st)
        return
    kb.barrier()
    if not KEEP_SCOPES:
        kb.pop_scope()

    kb.push_scope()
    st["wf"] = kb.sbuf("wf", [128, 2 * INNER], F32, scoped=True)
    st["wq16"] = kb.sbuf("wq16", [128, 2 * INNER], F16, scoped=True)
    st["sgn2"] = kb.sbuf("sgn2", [128, 2 * INNER], F16, scoped=True)
    emit_wquant(st)
    kb.barrier()
    if not KEEP_SCOPES:
        kb.pop_scope()

    kb.push_scope()
    st["arena"] = kb.sbuf("arena", [128, 2 * NC * TW], F32, scoped=True)
    st["xoc"] = kb.sbuf("xoc", [128, 2 * 512], F32, scoped=True)
    st["z1"] = kb.sbuf("z1", [128, NC * TW], F16, scoped=True)
    st["z2"] = kb.sbuf("z2", [128, 32 * TW], F16, scoped=True)
    st["prod"] = kb.sbuf("prod", [128, 32 * TW], F32, scoped=True)
    st["fw"] = kb.sbuf("fw", [128, 3 * NC * TW], F16, scoped=True)
    st["sxb"] = kb.sbuf("sxb", [128, TW], F32, scoped=True)
    st["tgb"] = kb.sbuf("tgb", [128, TW], F32, scoped=True)
    st["tub"] = kb.sbuf("tub", [128, TW], F32, scoped=True)
    st["tdb"] = kb.sbuf("tdb", [128, TW], F32, scoped=True)
    st["tmp4"] = kb.sbuf("tmp4", [128, TW], F32, scoped=True)
    emit_ffn(st)
    kb.pop_scope()

    emit_tail(st)


def emit_tail(st):
    kb = st["kb"]
    for q in ("sync", "scalar", "gpsimd"):
        for sn in kb.dma_pools[q]:
            kb.wait(q, sn, kb.val(sn))


def emit_debug_xn(st, src):
    kb, s = st["kb"], st["s"]
    kb.guard("gpsimd", "V", "T", "SC")
    pairs = []
    for ch in range(NC):
        pairs.append((st["outT"][ch * 128:(ch + 1) * 128, :],
                      src[:, ch * s:(ch + 1) * s]))
        pairs.append((st["dbgT"][ch * 128:(ch + 1) * 128, :],
                      src[:, ch * s:(ch + 1) * s]))
    kb.dma_group("gpsimd", pairs, None)



def emit_ln(st):
    kb, s, nt = st["kb"], st["s"], st["nt"]
    xn, xst, tmpsq, c, pb, tb = (st["xn"], st["xst"], st["tmpsq"], st["consts"],
                                 st["pb"], st["tb"])

    kb.wait_tok("scalar", st["tabs_tok"])
    kb.wait_tok("tensor", st["tabs_tok"])
    kb.wait("vector", "V", st["consts_done"])

    for t in range(nt):
        t0 = t * TW
        kb.guard("sync", "T", "V", "SC")
        xl = kb.dma_group(
            "sync", [(xst[:, ch * TW:(ch + 1) * TW],
                      st["xT"][ch * 128:(ch + 1) * 128, t0:t0 + TW])
                     for ch in range(NC)], None)
        kb.wait_tok("scalar", xl)
        kb.wait_tok("tensor", xl)

        kb.guard("scalar", "T")
        tvals = {}
        for ch in range(NC):
            xc = xst[:, ch * TW:(ch + 1) * TW]
            buf = tmpsq if ch % 2 == 0 else st["tmp2"]
            if ch >= 2:
                kb.wait("scalar", "T", tvals[ch - 2])
            sq = kb.op("scalar", lambda e, xc=xc, buf=buf: e.activation(
                buf[:, :], xc, AF.Square), inc="SC")
            kb.op("tensor", lambda e, xc=xc, f=(ch == 0), g=(ch == NC - 1):
                  e.matmul(pb[0][0:1, :], tb["onesq"][:, 0:1], xc,
                           start=f, stop=g))
            kb.wait("tensor", "SC", sq)
            tvals[ch] = kb.op("tensor", lambda e, buf=buf, f=(ch == 0),
                              g=(ch == NC - 1): e.matmul(
                pb[1][0:1, :], tb["onesq"][:, 0:1], buf[:, :],
                start=f, stop=g), inc="T")
        red = kb.val("T")

        kb.wait("vector", "T", red)
        kb.op("vector", lambda e: e.tensor_scalar(
            R(st, 0), pb[0][0:1, :], 1.0 / D, None, ALU.mult))
        kb.drain("vector")
        kb.op("vector", lambda e: e.tensor_tensor(
            R(st, 1), R(st, 0), R(st, 0), ALU.mult))
        kb.drain("vector")
        vv = kb.commit("vector", "V")
        kb.op("vector", lambda e: e.scalar_tensor_tensor(
            R(st, 1), pb[1][0:1, :], 1.0 / D, R(st, 1), ALU.mult, ALU.subtract))
        vv = kb.commit("vector", "V")
        kb.wait("scalar", "V", vv)
        kb.op("scalar", lambda e: e.activation(
            R(st, 2), R(st, 1), AF.Ln, bias=c[0:1, 1:2]))
        kb.drain("scalar")
        sq2 = kb.op("scalar", lambda e: e.activation(
            R(st, 1), R(st, 2), AF.Exp, bias=0.0, scale=-0.5), inc="SC")
        kb.wait("vector", "SC", sq2)
        kb.op("vector", lambda e: e.tensor_tensor(
            R(st, 2), R(st, 0), R(st, 1), ALU.mult))
        vv = kb.commit("vector", "V")

        kb.wait("tensor", "V", vv)
        kb.op("tensor", lambda e: e.matmul(
            pb[2], tb["onesq"][0:1, :], R(st, 1), start=True, stop=True))
        tbc = kb.op("tensor", lambda e: e.matmul(
            pb[3], tb["onesq"][0:1, :], R(st, 2), start=True, stop=True), inc="T")

        kb.wait("vector", "T", tbc)
        for ch in range(NC):
            xc = xst[:, ch * TW:(ch + 1) * TW]
            xnc = xn[:, ch * s + t0: ch * s + t0 + TW]
            kb.op("vector", lambda e, xc=xc, xnc=xnc: e.tensor_tensor(
                xnc, xc, pb[2], ALU.mult))
        kb.drain("vector")
        for ch in range(NC):
            xnc = xn[:, ch * s + t0: ch * s + t0 + TW]
            kb.op("vector", lambda e, xnc=xnc: e.tensor_tensor(
                xnc, xnc, pb[3], ALU.subtract))
        vv = kb.commit("vector", "V")
        kb.wait("gpsimd", "V", vv)
        kb.dma_group("gpsimd",
                     [(st["xn_scr"][ch * 128:(ch + 1) * 128, t0:t0 + TW],
                       xn[:, ch * s + t0: ch * s + t0 + TW])
                      for ch in range(NC)], None)
        kb.wait("tensor", "V", vv)



def rms_scale_bcast(st, t0):
    kb, s = st["kb"], st["s"]
    pb, tb, tmpsq, c = st["pb"], st["tb"], st["tmpsq"], st["consts"]
    kb.guard("scalar", "T")
    tvals = {}
    for ch in range(NC):
        xc = st["xn"][:, ch * s + t0: ch * s + t0 + TW]
        buf = tmpsq if ch % 2 == 0 else st["tmp2"]
        if ch >= 2:
            kb.wait("scalar", "T", tvals[ch - 2])
        sq = kb.op("scalar", lambda e, xc=xc, buf=buf: e.activation(
            buf[:, :], xc, AF.Square), inc="SC")
        kb.wait("tensor", "SC", sq)
        tvals[ch] = kb.op("tensor", lambda e, buf=buf, f=(ch == 0),
                          g=(ch == NC - 1): e.matmul(
            pb[0][0:1, :], tb["onesq"][:, 0:1], buf[:, :], start=f, stop=g),
            inc="T")
    red = kb.val("T")
    if DEBUG_STAGE == "rmsA":
        return red
    kb.wait("vector", "T", red)
    kb.op("vector", lambda e: e.tensor_scalar(
        R(st, 0), pb[0][0:1, :], 1.0 / D, None, ALU.mult))
    vv = kb.commit("vector", "V")
    kb.wait("scalar", "V", vv)
    kb.op("scalar", lambda e: e.activation(
        R(st, 2), R(st, 0), AF.Ln, bias=c[0:1, 2:3]))
    kb.drain("scalar")
    sq2 = kb.op("scalar", lambda e: e.activation(
        R(st, 0), R(st, 2), AF.Exp, bias=0.0, scale=-0.5), inc="SC")
    if DEBUG_STAGE == "rmsB":
        return kb.val("T")
    kb.wait("tensor", "SC", sq2)
    return kb.op("tensor", lambda e: e.matmul(
        pb[1], tb["onesq"][0:1, :], R(st, 0), start=True, stop=True), inc="T")


def norm_tile(st, t0, dst):
    kb, s = st["kb"], st["s"]
    c = st["consts"]
    for ch in range(NC):
        src_c = st["xn"][:, ch * s + t0: ch * s + t0 + TW]
        dst_c = dst[:, ch * TW:(ch + 1) * TW]
        kb.op("vector", lambda e, a=src_c, b=dst_c, ch=ch: e.scalar_tensor_tensor(
            b, a, c[:, 8 + ch:9 + ch], st["pb"][1], ALU.mult, ALU.mult))
    return kb.commit("vector", "V")


def rope_qknorm(st, src_psum, t0, out_region, ncols=TW):
    kb = st["kb"]
    kb.op("vector", lambda e, o=out_region: e.tensor_copy(o, src_psum))
    vv = kb.commit("vector", "V")
    return rope_qknorm_sb(st, t0, out_region, ncols)


def rope_qknorm_sb(st, t0, chunk, ncols=TW, deps=None):
    kb = st["kb"]
    pb, tb, c = st["pb"], st["tb"], st["consts"]
    tmp1, tmp2, tmpsq = st["tmp1"], st["tmp2"], st["tmpsq"]
    CSc, CSs = st["CSc"], st["CSs"]

    vv = kb.val("V")
    if deps is None:
        kb.guard("scalar", "T", "V")
    else:
        if deps.get("t_ind2") is not None:
            kb.wait("scalar", "T", deps["t_ind2"])
        kb.wait("scalar", "V", deps["chunk_v"])
    sq = kb.op("scalar", lambda e, ch=chunk: e.activation(
        tmpsq[:, 0:ncols], ch, AF.Square), inc="SC")
    if deps is not None and deps.get("v_final") is not None:
        kb.wait("tensor", "V", deps["v_final"])
    kb.wait("tensor", "V", vv if deps is None else deps["chunk_v"])
    kb.op("tensor", lambda e, ch=chunk: e.matmul(
        pb[6][:, 0:ncols], tb["swap"][:, :], ch, start=True, stop=True))
    kb.wait("tensor", "SC", sq)
    tt = kb.op("tensor", lambda e: e.matmul(
        pb[7][0:2, 0:ncols], tb["ind2"][:, :], tmpsq[:, 0:ncols],
        start=True, stop=True), inc="T")
    kb.wait("vector", "T", tt)
    kb.op("vector", lambda e: e.tensor_scalar(
        R2(st, 1, ncols), pb[7][0:2, 0:ncols], 1.0 / HD, None, ALU.mult))
    vv = kb.commit("vector", "V")
    kb.wait("scalar", "V", vv)
    kb.op("scalar", lambda e: e.activation(
        R2(st, 2, ncols), R2(st, 1, ncols), AF.Ln, bias=c[0:2, 2:3]))
    kb.drain("scalar")
    sq2 = kb.op("scalar", lambda e: e.activation(
        R2(st, 1, ncols), R2(st, 2, ncols), AF.Exp, bias=0.0, scale=-0.5),
        inc="SC")
    kb.wait("tensor", "SC", sq2)
    tt = kb.op("tensor", lambda e: e.matmul(
        pb[7][:, 0:ncols], tb["ind2T"][0:2, :], R2(st, 1, ncols),
        start=True, stop=True), inc="T")
    Ct = CSc[:, t0:t0 + ncols]
    Stt = CSs[:, t0:t0 + ncols]
    kb.op("vector", lambda e, ch=chunk, Ct=Ct: e.tensor_tensor(
        tmp1[:, 0:ncols], ch, Ct, ALU.mult))
    kb.wait("vector", "T", tt)
    kb.op("vector", lambda e, Stt=Stt: e.tensor_tensor(
        tmp2[:, 0:ncols], pb[6][:, 0:ncols], Stt, ALU.mult))
    kb.drain("vector")
    kb.op("vector", lambda e: e.tensor_tensor(
        tmp1[:, 0:ncols], tmp1[:, 0:ncols], tmp2[:, 0:ncols], ALU.add))
    kb.drain("vector")
    kb.op("vector", lambda e, ch=chunk: e.tensor_tensor(
        ch, tmp1[:, 0:ncols], pb[7][:, 0:ncols], ALU.mult))
    vfin = kb.commit("vector", "V")
    return {"t_ind2": tt, "v_final": vfin,
            "chunk_v": (deps["chunk_v"] if deps is not None else vfin)}


def band_load(st, dst, src, n=0):
    kb = st["kb"]
    kb.guard("sync", "T")
    return kb.dma("sync", dst, src, None)


def emit_attn(st, glob):
    kb, s, nt = st["kb"], st["s"], st["nt"]
    w, tb, c, pb = st["w"], st["tb"], st["consts"], st["pb"]
    xn, KG, VG, QT, AO, lat, wst, xst = (st["xn"], st["KG"], st["VG"], st["QT"],
                                         st["AO"], st["lat"], st["wst"], st["xst"])

    wq = w["g_wq" if glob else "l_wq"]
    wo = w["g_wo" if glob else "l_wo"]
    nw = w["norm1_w" if glob else "norm2_w"]

    kb.guard("sync", "V")
    tl = kb.dma_group("sync", [
        (st["CSc"][:, :], st["tab"]["Cg" if glob else "Cl"][:, :]),
        (st["CSs"][:, :], st["tab"]["Sg" if glob else "Sl"][:, :]),
        (ap_of(st["consts"], 8, [[16, 128], [1, NC]]), nw[:, :]),
    ], None)
    for e in ("vector", "tensor", "scalar"):
        kb.wait_tok(e, tl)

    if DEBUG_STAGE in ("xn_pre", "xn_pre2", "xn_pre3", "xn_pre4", "xn_pre5"):
        return
    for t in range(nt):
        t0 = t * TW
        bc = rms_scale_bcast(st, t0)
        kb.wait("vector", "T", bc)
        kb.guard("vector", "T")
        if DEBUG_STAGE in ("rms0", "rmsA", "rmsB"):
            break
        nt_done = norm_tile(st, t0, xst)
        if DEBUG_STAGE in ("norm", "norm_nodma"):
            break

        if glob:
            kv_down = w["g_kv_down"]
            kb.wait("tensor", "V", nt_done)
            for kc in range(NC):
                buf = wst[:, (kc % 2) * 1024:(kc % 2) * 1024 + LAT]
                wl = band_load(st, buf, kv_down[kc * 128:(kc + 1) * 128, :], LAT)
                kb.wait_tok("tensor", wl)
                for lc in range(2):
                    kb.op("tensor", lambda e, buf=buf, lc=lc, kc=kc, f=(kc == 0),
                          g=(kc == NC - 1):
                          e.matmul(pb[4 + lc],
                                   buf[:, lc * 128:(lc + 1) * 128],
                                   xst[:, kc * TW:(kc + 1) * TW],
                                   start=f, stop=g),
                          inc=("T" if lc == 1 else None))
            tt = kb.val("T")
            kb.wait("vector", "T", tt)
            kb.guard("vector", "T")
            for lc in range(2):
                kb.op("vector", lambda e, lc=lc, t0=t0: e.tensor_copy(
                    lat[:, lc * s + t0: lc * s + t0 + TW], pb[4 + lc]))
            lat_done = kb.commit("vector", "V")
            if DEBUG_STAGE == "lat":
                continue

            kup = w["g_k_up"]
            kb.guard("sync", "T")
            wl = kb.dma_group("sync", [
                (wst[:, 0:LAT], kup[0:128, :]),
                (wst[:, 1024:1024 + LAT], kup[128:256, :])], None)
            kb.wait_tok("tensor", wl)
            kb.wait("tensor", "V", lat_done)
            for oc in range(2):
                for lc in range(2):
                    kb.op("tensor", lambda e, oc=oc, lc=lc, t0=t0, f=(lc == 0),
                          g=(lc == 1): e.matmul(
                        pb[4], wst[:, lc * 1024 + oc * 128: lc * 1024 + (oc + 1) * 128],
                        lat[:, lc * s + t0: lc * s + t0 + TW], start=f, stop=g),
                        inc=("T" if lc == 1 else None))
                kb.wait("vector", "T", kb.val("T"))
                rope_qknorm(st, pb[4], t0, KG[:, oc * s + t0: oc * s + t0 + TW])
                kb.wait("tensor", "V", kb.val("V"))

            if DEBUG_STAGE == "k":
                continue
            vup = w["g_v_up"]
            kb.guard("sync", "T")
            wl = kb.dma_group("sync", [
                (wst[:, 0:LAT], vup[0:128, :]),
                (wst[:, 1024:1024 + LAT], vup[128:256, :])], None)
            kb.wait_tok("tensor", wl)
            emit_v_proj(st, t0, nk=2,
                        lhs_fn=lambda kc, tc4, t0=t0: lat[:, kc * s + t0 + tc4 * 128:
                                                          kc * s + t0 + (tc4 + 1) * 128],
                        rhs_fn=lambda kc: wst[:, kc * 1024: kc * 1024 + LAT])
        else:
            lwk = w["l_wk"]
            kb.wait("tensor", "V", nt_done)
            for kc in range(NC):
                buf = wst[:, (kc % 2) * 1024:(kc % 2) * 1024 + LAT]
                wl = band_load(st, buf, lwk[kc * 128:(kc + 1) * 128, :], LAT)
                kb.wait_tok("tensor", wl)
                for oc in range(2):
                    kb.op("tensor", lambda e, buf=buf, oc=oc, kc=kc, f=(kc == 0),
                          g=(kc == NC - 1):
                          e.matmul(pb[4 + oc], buf[:, oc * 128:(oc + 1) * 128],
                                   xst[:, kc * TW:(kc + 1) * TW], start=f, stop=g),
                          inc=("T" if oc == 1 else None))
            tt = kb.val("T")
            kb.wait("vector", "T", tt)
            for oc in range(2):
                rope_qknorm(st, pb[4 + oc], t0, KG[:, oc * s + t0: oc * s + t0 + TW])
            kb.wait("tensor", "V", kb.val("V"))

            emit_v_proj_local(st, t0, w["l_wv"])

    if DEBUG_STAGE in ("lat", "k", "kv") and glob:
        return
    for t in range(nt):
        t0 = t * TW
        bc = rms_scale_bcast(st, t0)
        kb.wait("vector", "T", bc)
        kb.guard("vector", "T")
        nt_done = norm_tile(st, t0, xst)
        kb.wait("tensor", "V", nt_done)

        t_after = {}
        toks = {}
        kb.guard("gpsimd", "T")
        toks[0] = kb.dma("gpsimd", wst[:, 0:1024], wq[0:128, :], None)
        for kc in range(NC):
            if kc + 1 < NC:
                if kc >= 1:
                    kb.wait("gpsimd", "T", t_after[kc - 1])
                toks[kc + 1] = kb.dma(
                    "gpsimd", wst[:, ((kc + 1) % 2) * 1024:((kc + 1) % 2) * 1024 + 1024],
                    wq[(kc + 1) * 128:(kc + 2) * 128, :], None)
            kb.wait_tok("tensor", toks[kc])
            buf = wst[:, (kc % 2) * 1024:(kc % 2) * 1024 + 1024]
            for oc in range(NC):
                kb.op("tensor", lambda e, buf=buf, oc=oc, kc=kc, f=(kc == 0),
                      g=(kc == NC - 1):
                      e.matmul(pb[oc], buf[:, oc * 128:(oc + 1) * 128],
                               xst[:, kc * TW:(kc + 1) * TW], start=f, stop=g),
                      inc=("T" if oc == NC - 1 else None))
            t_after[kc] = kb.val("T")
        kb.wait("vector", "T", kb.val("T"))
        for oc in range(NC):
            kb.op("vector", lambda e, oc=oc: e.tensor_copy(
                QT[:, oc * TW:(oc + 1) * TW], pb[oc]))
        qv = kb.commit("vector", "V")
        kb.guard("sync", "T")
        wo_toks = {0: kb.dma("sync", wst[:, 0:1024], wo[0:128, :], None),
                   1: kb.dma("sync", wst[:, 1024:2048], wo[128:256, :], None)}
        st["sdpa_sbv"] = {b: qv for b in (0, 1, 6, 7)}
        st["sdpa_pvv"] = {j: qv for j in range(4)}
        deps = {"t_ind2": None, "v_final": None, "chunk_v": qv}
        for oc in range(4):
            deps = rope_qknorm_sb(st, t0, QT[:, oc * TW:(oc + 1) * TW],
                                  deps=deps)
        emit_sdpa_kv(st, glob, t, 0, deep=False, qready=deps["v_final"])
        for oc in range(4, NC):
            deps = rope_qknorm_sb(st, t0, QT[:, oc * TW:(oc + 1) * TW],
                                  deps=deps)
        st["sdpa_sbv"][6] = st["sdpa_sbv"][7] = deps["v_final"]
        emit_sdpa_kv(st, glob, t, 1, qready=deps["v_final"])
        emit_sdpa_kv(st, glob, t, 2)
        emit_sdpa_kv(st, glob, t, 3)

        t_after = {}
        toks = wo_toks
        for tok in st.pop("ao_lift", []):
            kb.wait_tok("tensor", tok)
        kb.guard("tensor", "V")
        for kc in range(NC):
            kb.wait_tok("tensor", toks[kc])
            buf = wst[:, (kc % 2) * 1024:(kc % 2) * 1024 + 1024]
            for oc in range(NC):
                kb.op("tensor", lambda e, buf=buf, oc=oc, kc=kc, f=(kc == 0),
                      g=(kc == NC - 1):
                      e.matmul(pb[oc], buf[:, oc * 128:(oc + 1) * 128],
                               AO[:, kc * TW:(kc + 1) * TW], start=f, stop=g),
                      inc=("T" if oc == NC - 1 else None))
            t_after[kc] = kb.val("T")
            if kc + 2 < NC:
                kb.wait("sync", "T", t_after[kc])
                toks[kc + 2] = kb.dma(
                    "sync", wst[:, (kc % 2) * 1024:(kc % 2) * 1024 + 1024],
                    wo[(kc + 2) * 128:(kc + 3) * 128, :], None)
        tt = kb.val("T")
        kb.wait("vector", "T", tt)
        for oc in range(NC):
            dstc = xn[:, oc * s + t0: oc * s + t0 + TW]
            kb.op("vector", lambda e, oc=oc, dstc=dstc: e.tensor_tensor(
                dstc, dstc, pb[oc], ALU.add))
        vv = kb.commit("vector", "V")
        kb.wait("tensor", "V", vv)


def emit_sdpa_kv(st, glob, t, kv, deep=True, qready=None):
    kb, s = st["kb"], st["s"]
    pb, pbh, tb, Pt, VG, KG, QT, AO = (st["pb"], st["pbh"], st["tb"], st["Pt"],
                                       st["VG"], st["KG"], st["QT"], st["AO"])
    Pt2 = st["lat"]
    c = st["consts"]
    t0 = t * TW
    nblk = 4 * 65
    vg_free = (s // 128) * nblk
    rb = 64 * (kv % 2)
    kcn = kv // 2
    sbanks = [0, 1, 6, 7] if deep else [0, 1, 0, 1]
    sbv = st["sdpa_sbv"]
    pvv = st["sdpa_pvv"]
    lastpv = st.setdefault("sdpa_lastpv", {})

    if qready is not None:
        kb.wait("tensor", "V", qready)

    pos = []
    for qc, (ha, hb) in enumerate(Q_CHUNK_HEADS):
        if ha // 4 == kv:
            pos.append((qc, 0))
        if hb // 4 == kv:
            pos.append((qc, 1))
    assert len(pos) == 4

    chunks = []
    if glob:
        for j in range(0, t * 4 + 4):
            k0 = j * 128
            qoff = max(0, k0 - t0)
            subs = [(qoff, "d")] if k0 >= t0 else []
            chunks.append((k0, qoff, TW - qoff, subs))
    else:
        for j in range(-2, 4):
            k0 = t0 + j * 128
            if k0 < 0:
                continue
            lo = max(0, j) * 128
            hi = min(4, j + 3) * 128
            subs = []
            if j >= 0:
                subs.append((j * 128, "d"))
            if j + 2 <= 3:
                subs.append(((j + 2) * 128, "u"))
            chunks.append((k0, lo, hi - lo, subs))
    L = len(chunks)

    def pt_ap(j, ci, a, n):
        buf = Pt if ci % 2 == 0 else Pt2
        return buf[:, j * TW + a: j * TW + a + n]

    exp_sc = {}
    pv_t = {}
    for ci, (k0, qoff, N, subs) in enumerate(chunks):
        last = (ci == L - 1)
        lhsK = KG[rb:rb + 64, kcn * s + k0: kcn * s + k0 + 128]
        lhsV = ap_of(VG, (k0 // 128) * nblk + 65 * kv, [[vg_free, 128], [1, 65]])
        for j, (qc, half) in enumerate(pos):
            qb = 64 * half
            scb = pbh[sbanks[j]]
            rhsQ = QT[qb:qb + 64, qc * TW + qoff: qc * TW + qoff + N]
            if ci == 0:
                v = sbv.get(sbanks[j])
                if v is not None and (deep or j < 2):
                    kb.wait("tensor", "V", v)
            if deep:
                prev = (ci - 1, j)
            else:
                prev = (ci, j - 2) if j >= 2 else (ci - 1, j + 2)
            if prev in exp_sc:
                kb.wait("tensor", "SC", exp_sc[prev])
            kb.op("tensor", lambda e, lhsK=lhsK, rhsQ=rhsQ, qoff=qoff, N=N,
                  b=scb: e.matmul(b[0:128, qoff:qoff + N], lhsK, rhsQ,
                                  start=True, stop=True), inc="T")
            tt = kb.val("T")
            kb.wait("scalar", "T", tt)
            if (ci - 2, j) in pv_t:
                kb.wait("scalar", "T", pv_t[(ci - 2, j)])
            elif ci < 2 and (j, ci % 2) in lastpv:
                kb.wait("scalar", "T", lastpv[(j, ci % 2)])
            sc = kb.op("scalar", lambda e, qoff=qoff, N=N, b=scb, j=j, ci=ci:
                       e.activation(pt_ap(j, ci, qoff, N),
                                    b[0:128, qoff:qoff + N], AF.Exp,
                                    bias=c[:, 0:1], scale=0.125), inc="SC")
            exp_sc[(ci, j)] = sc
        for j, (qc, half) in enumerate(pos):
            pvb = pbh[2 + j]
            sc = exp_sc[(ci, j)]
            if subs:
                kb.wait("vector", "SC", sc)
                for (soff, kind) in subs:
                    mk = tb["mask_diag"] if kind == "d" else tb["mask_su"]
                    kb.op("vector", lambda e, soff=soff, mk=mk, j=j, ci=ci:
                          e.tensor_tensor(
                              pt_ap(j, ci, soff, 128),
                              pt_ap(j, ci, soff, 128),
                              mk[:, :], ALU.mult))
                mdone = kb.commit("vector", "V")
                kb.wait("tensor", "V", mdone)
            else:
                kb.wait("tensor", "SC", sc)
            if ci == 0:
                v = pvv.get(j)
                if v is not None:
                    kb.wait("tensor", "V", v)
            if glob:
                kb.op("tensor", lambda e, lhsV=lhsV, qoff=qoff, N=N, b=pvb,
                      j=j, ci=ci, f=(ci == 0), g=last: e.matmul(
                          b[0:65, qoff:qoff + N], lhsV,
                          pt_ap(j, ci, qoff, N),
                          start=f, stop=g), inc="T")
            else:
                nsub = N // 128
                for sbt in range(nsub):
                    so = qoff + sbt * 128
                    kb.op("tensor", lambda e, lhsV=lhsV, so=so, b=pvb, j=j,
                          ci=ci,
                          f=(ci == 0 and sbt == 0),
                          g=(last and sbt == nsub - 1): e.matmul(
                              b[0:65, so:so + 128], lhsV,
                              pt_ap(j, ci, so, 128),
                              start=f, stop=g),
                          inc=("T" if sbt == nsub - 1 else None))
            pv_t[(ci, j)] = kb.val("T")

    for j in range(4):
        lastpv[(j, (L - 1) % 2)] = pv_t[(L - 1, j)]
        if L >= 2:
            lastpv[(j, (L - 2) % 2)] = pv_t[(L - 2, j)]

    xstb = st["xst"]
    Cslots = [R64(st, 0), R64(st, 1), R64(st, 2), xstb[64:65, 2 * TW:3 * TW]]
    rws = st["rows"]
    Lslots = [rws[32:33, 0:TW], rws[32:33, TW:2 * TW], rws[32:33, 2 * TW:3 * TW],
              rws[96:97, 0:TW]]
    kb.wait("vector", "T", pv_t[(L - 1, 3)])
    for j in range(4):
        kb.op("vector", lambda e, j=j: e.tensor_copy(
            Cslots[j], pbh[2 + j][64:65, 0:TW]))
    cv = kb.commit("vector", "V")
    kb.wait("scalar", "V", cv)
    for j in range(4):
        kb.op("scalar", lambda e, j=j: e.activation(Lslots[j], Cslots[j], AF.Ln))
    kb.drain("scalar")
    sqv = {}
    for j in range(4):
        sqv[j] = kb.op("scalar", lambda e, j=j: e.activation(
            Cslots[j], Lslots[j], AF.Exp, bias=0.0, scale=-1.0), inc="SC")
    stgs = [st["xst"][0:64, 0:TW], st["xst"][0:64, TW:2 * TW]]
    ttv = None
    for pair in ((0, 1), (2, 3)):
        for j in pair:
            kb.wait("tensor", "SC", sqv[j])
            if ttv is not None and not deep and j == pair[0]:
                kb.wait("tensor", "V", ttv)
            lx = (L - 1, 2 + (j % 2)) if not deep else (L - 1, j)
            kb.wait("tensor", "SC", exp_sc[lx])
            kb.op("tensor", lambda e, j=j, b=sbanks[j]: e.matmul(
                pbh[b][0:64, 0:TW], tb["onesq"][64:65, 0:64],
                Cslots[j], start=True, stop=True), inc="T")
        tt = kb.val("T")
        kb.wait("vector", "T", tt)
        bufs = {pair[0]: xstb[0:64, 2 * TW:3 * TW],
                pair[1]: xstb[0:64, 3 * TW:4 * TW]}
        for j in pair:
            kb.op("vector", lambda e, j=j, b=bufs[j]: e.tensor_copy(
                b, pbh[sbanks[j]][0:64, 0:TW]))
        kb.drain("vector")
        for j in pair:
            qc, half = pos[j]
            if half == 0:
                dst = AO[0:64, qc * TW:(qc + 1) * TW]
                kb.op("vector", lambda e, j=j, b=bufs[j], dst=dst: e.tensor_tensor(
                    dst, pbh[2 + j][0:64, 0:TW], b, ALU.mult))
            else:
                stg = stgs[j % 2]
                ltok = st.setdefault("lift_tok_slot", {}).get(j % 2)
                if ltok is not None:
                    kb.wait_tok("vector", ltok)
                kb.op("vector", lambda e, j=j, b=bufs[j], stg=stg: e.tensor_tensor(
                    stg, pbh[2 + j][0:64, 0:TW], b, ALU.mult))
                vvl = kb.commit("vector", "V")
                kb.wait("gpsimd", "V", vvl)
                tok = kb.dma("gpsimd", AO[64:128, qc * TW:(qc + 1) * TW], stg, None)
                st.setdefault("ao_lift", []).append(tok)
                st["lift_tok_slot"][j % 2] = tok
        ttv = kb.commit("vector", "V")
        for j in pair:
            sbv[sbanks[j]] = ttv
            pvv[j] = ttv


def emit_v_proj(st, t0, nk, lhs_fn, rhs_fn):
    kb = st["kb"]
    pb, VG = st["pb"], st["VG"]
    nblk = 4 * 65
    vg_free = (st["s"] // 128) * nblk
    for tc4 in range(TW // 128):
        tca = (t0 // 128) + tc4
        kb.guard("tensor", "V")
        for kc in range(nk):
            kb.op("tensor", lambda e, kc=kc, tc4=tc4, f=(kc == 0),
                  g=(kc == nk - 1): e.matmul(
                st["pbh"][4][0:128, 0:256], lhs_fn(kc, tc4), rhs_fn(kc),
                start=f, stop=g),
                inc=("T" if kc == nk - 1 else None))
        tt = kb.val("T")
        kb.wait("vector", "T", tt)
        out_ap = ap_of(VG, tca * nblk, [[vg_free, 128], [65, 4], [1, 64]])
        in_ap = ap_of(st["pbh"][4], 0, [[512, 128], [64, 4], [1, 64]])
        kb.op("vector", lambda e, o=out_ap, i=in_ap: e.tensor_copy(o, i))
        kb.commit("vector", "V")
    ones_ap = ap_of(VG, (t0 // 128) * nblk + 64,
                    [[vg_free, 128], [nblk, TW // 128], [65, 4], [1, 1]])
    kb.op("vector", lambda e, o=ones_ap: e.memset(o, 1.0))
    kb.commit("vector", "V")


def emit_v_proj_local(st, t0, lwv):
    kb = st["kb"]
    wst, xst = st["wst"], st["xst"]
    nblk = 4 * 65
    vg_free = (st["s"] // 128) * nblk

    def acc_ap(tc4, dims):
        return ap_of(st["pbh"][4 + (tc4 // 2)], (tc4 % 2) * 256, dims)

    kb.guard("tensor", "V")
    for kc in range(NC):
        buf = wst[:, (kc % 2) * 1024:(kc % 2) * 1024 + LAT]
        wl = band_load(st, buf, lwv[kc * 128:(kc + 1) * 128, :], LAT)
        kb.wait_tok("tensor", wl)
        for tc4 in range(TW // 128):
            lhs = xst[:, kc * TW + tc4 * 128: kc * TW + (tc4 + 1) * 128]
            out = acc_ap(tc4, [[512, 128], [1, 256]])
            kb.op("tensor", lambda e, lhs=lhs, out=out, buf=buf, kc=kc,
                  f=(kc == 0 and tc4 % 2 == 0),
                  g=(kc == NC - 1 and tc4 % 2 == 1):
                  e.matmul(out, lhs, buf, start=f, stop=g),
                  inc=("T" if tc4 == TW // 128 - 1 else None))
    tt = kb.val("T")
    kb.wait("vector", "T", tt)
    for tc4 in range(TW // 128):
        tca = (t0 // 128) + tc4
        out_ap = ap_of(st["VG"], tca * nblk, [[vg_free, 128], [65, 4], [1, 64]])
        in_ap = acc_ap(tc4, [[512, 128], [64, 4], [1, 64]])
        kb.op("vector", lambda e, o=out_ap, i=in_ap: e.tensor_copy(o, i))
        kb.drain("vector")
    kb.commit("vector", "V")
    ones_ap = ap_of(st["VG"], (t0 // 128) * nblk + 64,
                    [[vg_free, 128], [nblk, TW // 128], [65, 4], [1, 1]])
    kb.op("vector", lambda e, o=ones_ap: e.memset(o, 1.0))
    kb.commit("vector", "V")


def emit_store_s(st):
    kb, s = st["kb"], st["s"]
    kb.guard("gpsimd", "V")
    kb.dma_group("gpsimd",
                 [(st["s_scr"][ch * 128:(ch + 1) * 128, :],
                   st["xn"][:, ch * s:(ch + 1) * s]) for ch in range(NC)], None)



def emit_wquant(st):
    kb = st["kb"]
    c, pb, pbh, tb, cs, mw = (st["consts"], st["pb"], st["pbh"], st["tb"],
                              st["cs"], st["mw"])
    wf, wq16, sgn2 = st["wf"], st["wq16"], st["sgn2"]
    tmp1, tmp2 = st["tmp1"], st["tmp2"]

    specs = [
        ("w_gate", st["wgq_scr"], INNER, 0, 0),
        ("w_up", st["wuq_scr"], INNER, 32, 1),
        ("w_down", st["wdq_scr"], D, 64, 2),
    ]
    for (wname, scr, nout, cs_off, mw_i) in specs:
        src = st["w"][wname]
        nin_ch = src.tensor.shape[0] // 128

        def wfs(ic, nout=nout):
            sl = ic % 2
            return wf[:, sl * INNER: sl * INNER + nout]

        toks = {}
        rv = {}
        kb.guard("vector", "SC")
        kb.guard("sync", "V", "SC")
        toks[0] = kb.dma("sync", wfs(0), src[0:128, :], None)
        for ic in range(nin_ch):
            if ic + 1 < nin_ch:
                if ic >= 1:
                    kb.wait("sync", "V", rv[ic - 1])
                toks[ic + 1] = kb.dma("sync", wfs(ic + 1),
                                      src[(ic + 1) * 128:(ic + 2) * 128, :], None)
            kb.wait_tok("vector", toks[ic])
            rv[ic] = kb.op("vector", lambda e, ic=ic, wfs=wfs: e.reduce_sum(
                tmp1[:, ic:ic + 1], wfs(ic), AX.X, apply_absolute_value=True),
                inc="V")
        kb.drain("vector")
        kb.op("vector", lambda e, n=nin_ch: e.reduce_sum(
            tmp2[:, 0:1], tmp1[:, 0:n], AX.X))
        va = kb.commit("vector", "V")
        kb.wait("tensor", "V", va)
        tt = kb.op("tensor", lambda e: e.matmul(
            pbh[5][:, 0:1], tb["onesq"][:, :], tmp2[:, 0:1], start=True, stop=True),
            inc="T")
        kb.wait("vector", "T", tt)
        kb.op("vector", lambda e, mw_i=mw_i: e.tensor_scalar(
            mw[:, mw_i:mw_i + 1], pbh[5][:, 0:1], 1.0 / (1 << 22), 1e-5,
            ALU.mult, ALU.max))
        kb.drain("vector")
        kb.op("vector", lambda e, mw_i=mw_i: e.tensor_scalar(
            tmp1[:, 0:1], mw[:, mw_i:mw_i + 1], 0.5, None, ALU.mult))
        kb.drain("vector")
        kb.op("vector", lambda e, mw_i=mw_i: e.tensor_scalar(
            tmp1[:, 1:2], mw[:, mw_i:mw_i + 1], -0.5, None, ALU.mult))
        vb = kb.commit("vector", "V")
        kb.wait("scalar", "V", vb)

        def wqs(ic, nout=nout):
            sl = ic % 2
            return wq16[:, sl * INNER: sl * INNER + nout]

        def sgs(ic, nout=nout):
            sl = ic % 2
            return sgn2[:, sl * INNER: sl * INNER + nout]

        slot_store = st.setdefault("wq_store_toks", {})
        colsum_t = {}
        toksB = {}
        sc_after = {}
        kb.guard("sync", "SC", "V", "T")
        toksB[0] = kb.dma("sync", wfs(0), src[0:128, :], None)
        for ic in range(nin_ch):
            sl = ic % 2
            if ic + 1 < nin_ch:
                if ic >= 1:
                    kb.wait("sync", "SC", sc_after[ic - 1])
                toksB[ic + 1] = kb.dma("sync", wfs(ic + 1),
                                       src[(ic + 1) * 128:(ic + 2) * 128, :], None)
            kb.wait_tok("scalar", toksB[ic])
            if sl in colsum_t:
                kb.wait("scalar", "T", colsum_t[sl])
            if slot_store.get(sl) is not None:
                kb.wait_tok("scalar", slot_store[sl])
            kb.op("scalar", lambda e, ic=ic, wqs=wqs, wfs=wfs: e.activation(
                wqs(ic), wfs(ic), AF.Sign, bias=tmp1[:, 1:2]))
            s2 = kb.op("scalar", lambda e, ic=ic, sgs=sgs, wfs=wfs: e.activation(
                sgs(ic), wfs(ic), AF.Sign, bias=tmp1[:, 0:1]), inc="SC")
            sc_after[ic] = s2
            kb.wait("vector", "SC", s2)
            kb.op("vector", lambda e, ic=ic, wqs=wqs, sgs=sgs: e.tensor_tensor(
                wqs(ic), wqs(ic), sgs(ic), ALU.add))
            vq = kb.commit("vector", "V")
            kb.wait("gpsimd", "V", vq)
            slot_store[sl] = kb.dma("gpsimd", scr[ic * 128:(ic + 1) * 128, :],
                                    wqs(ic), None)
            kb.wait("tensor", "V", vq)
            for os_ in range(nout // 128):
                o0 = sl * INNER + os_ * 128
                kb.op("tensor", lambda e, os_=os_, ic=ic, cs_off=cs_off, o0=o0,
                      f=(ic == 0 and os_ == 0),
                      g=(ic == nin_ch - 1 and os_ == nout // 128 - 1): e.matmul(
                          pbh[6][:, cs_off + os_: cs_off + os_ + 1],
                          wq16[:, o0:o0 + 128],
                          tb["ones16"][:, 0:1],
                          start=f, stop=g),
                      inc=("T" if os_ == nout // 128 - 1 else None))
            colsum_t[sl] = kb.val("T")
        tt = kb.val("T")
        kb.wait("vector", "T", tt)
        kb.op("vector", lambda e, cs_off=cs_off, n=nout // 128: e.tensor_scalar(
            cs[:, cs_off:cs_off + n], pbh[6][:, cs_off:cs_off + n], MAGIC, None,
            ALU.mult))
        kb.commit("vector", "V")



def bcast_row(st, row_ap, dst_sb):
    kb, tb = st["kb"], st["tb"]
    kb.guard("tensor", "V")
    tt = kb.op("tensor", lambda e: e.matmul(
        st["pbh"][7][:, :], tb["onesq"][0:1, :], row_ap, start=True, stop=True),
        inc="T")
    kb.wait("vector", "T", tt)
    kb.op("vector", lambda e: e.tensor_copy(dst_sb[:, :], st["pbh"][7][:, :]))
    return kb.commit("vector", "V")


def emit_amax(st, chunk_fn, nch, dst_row):
    kb = st["kb"]
    accs = [st["tmp1"], st["tmp2"], st["tmpsq"], st["tmp4"]]
    na = min(len(accs), max(1, nch // 2))
    if st.get("tmps_sc") is not None:
        kb.wait("vector", "SC", st["tmps_sc"])
    for i in range(na):
        kb.op("vector", lambda e, i=i: e.tensor_tensor(
            accs[i][:, :], chunk_fn(i), chunk_fn(na + i), ALU.max))
    kb.drain("vector")
    nxt = 2 * na
    while nxt < nch:
        for i in range(na):
            if nxt + i < nch:
                kb.op("vector", lambda e, i=i, j=nxt + i: e.tensor_tensor(
                    accs[i][:, :], accs[i][:, :], chunk_fn(j), ALU.max))
        kb.drain("vector")
        nxt += na
    done = 0
    while done < nch:
        for i in range(na):
            if done + i < nch:
                kb.op("vector", lambda e, i=i, j=done + i: e.scalar_tensor_tensor(
                    accs[i][:, :], chunk_fn(j), -1.0, accs[i][:, :],
                    ALU.mult, ALU.max))
        kb.drain("vector")
        done += na
    step = na // 2
    while step >= 1:
        for i in range(step):
            kb.op("vector", lambda e, i=i, s2=step: e.tensor_tensor(
                accs[i][:, :], accs[i][:, :], accs[i + s2][:, :], ALU.max))
        kb.drain("vector")
        step //= 2
    vv = kb.commit("vector", "V")
    kb.wait("gpsimd", "V", vv)
    kb.op("gpsimd", lambda e: e.reduce_max(R(st, 0), accs[0][:, :], AX.C))
    gg = kb.commit("gpsimd", "G")
    kb.wait("vector", "G", gg)
    kb.op("vector", lambda e, d=dst_row: e.tensor_scalar(
        d, R(st, 0), 1e-5, None, ALU.max))
    return kb.commit("vector", "V")


def emit_ffn(st):
    kb, s, nt = st["kb"], st["s"], st["nt"]
    pb, pbh, tb, c, cs, mw = (st["pb"], st["pbh"], st["tb"], st["consts"],
                              st["cs"], st["mw"])
    arena, z1, z2, prod, fw = st["arena"], st["z1"], st["z2"], st["prod"], st["fw"]
    Pt = st["Pt"]
    sxb, tgb, tub, tdb = st["sxb"], st["tgb"], st["tub"], st["tdb"]
    tmp1, tmp2, tmpsq = st["tmp1"], st["tmp2"], st["tmpsq"]
    tmps4 = [tmp1, tmp2, tmpsq, st["tmp4"]]
    xoc = st["xoc"]
    XNT = 0
    S0 = NC * TW
    bankv = st.setdefault("ffn_bankv", {})

    def bank_wait(banks):
        for b in banks:
            v = bankv.get(b)
            if v is not None:
                kb.wait("tensor", "V", v)

    def A(base, ch):
        return arena[:, base + ch * TW: base + (ch + 1) * TW]

    STT_ = NC * TW

    def emit_q1g(t):
        t0 = t * TW
        kb.guard("gpsimd", "V")
        pairs = []
        for ch in range(NC):
            pairs.append((A(XNT, ch),
                          st["xn_scr"][ch * 128:(ch + 1) * 128, t0:t0 + TW]))
        tl = kb.dma_group("gpsimd", pairs, None)
        kb.wait_tok("vector", tl)

        av = emit_amax(st, lambda ch: A(XNT, ch), NC, R(st, 2))
        kb.op("vector", lambda e: e.tensor_scalar(
            R(st, 0), R(st, 2), mw[0:1, 0:1], 1.0 / 254.0, ALU.mult, ALU.mult))
        kb.op("vector", lambda e: e.tensor_scalar(
            R(st, 1), R(st, 2), mw[0:1, 1:2], 1.0 / 254.0, ALU.mult, ALU.mult))
        vv2 = kb.commit("vector", "V")
        kb.wait("scalar", "V", vv2)
        kb.op("scalar", lambda e: e.activation(
            st["rows"][32:33, TW:2 * TW], R(st, 2), AF.Ln))
        kb.drain("scalar")
        sxv = kb.op("scalar", lambda e: e.activation(
            R(st, 2), st["rows"][32:33, TW:2 * TW], AF.Exp,
            bias=c[0:1, 3:4], scale=-1.0), inc="SC")
        kb.guard("tensor", "V")
        kb.wait("tensor", "V", vv2)
        t1 = kb.op("tensor", lambda e: e.matmul(
            pbh[7][:, :], tb["onesq"][0:1, :], R(st, 0), start=True, stop=True))
        t2 = kb.op("tensor", lambda e: e.matmul(
            pbh[6][:, :], tb["onesq"][0:1, :], R(st, 1), start=True, stop=True),
            inc="T")
        kb.wait("vector", "T", kb.val("T"))
        kb.op("vector", lambda e: e.tensor_copy(tgb[:, :], pbh[7][:, :]))
        kb.op("vector", lambda e: e.tensor_copy(tub[:, :], pbh[6][:, :]))
        cbv = kb.commit("vector", "V")
        bankv[6] = cbv
        kb.wait("tensor", "SC", sxv)
        kb.wait("tensor", "V", cbv)
        kb.op("tensor", lambda e: e.matmul(
            pbh[7][:, :], tb["onesq"][0:1, :], R(st, 2), start=True, stop=True),
            inc="T")
        kb.wait("vector", "T", kb.val("T"))
        kb.op("vector", lambda e: e.tensor_copy(sxb[:, :], pbh[7][:, :]))
        bankv[7] = kb.commit("vector", "V")

        prevsc = st.get("tmps_sc")
        for b0 in (0, 4):
            if prevsc is not None:
                kb.wait("vector", "SC", prevsc)
            for i in range(4):
                ch = b0 + i
                kb.op("vector", lambda e, ch=ch, i=i: e.tensor_tensor(
                    tmps4[i][:, :], A(XNT, ch), sxb[:, :], ALU.mult))
            zv = kb.commit("vector", "V")
            kb.wait("scalar", "V", zv)
            if b0 == 0 and st.get("ffn_gu_last_t") is not None:
                kb.wait("scalar", "T", st["ffn_gu_last_t"])
            for i in range(4):
                ch = b0 + i
                kb.op("scalar", lambda e, ch=ch, i=i: e.activation(
                    z1[:, ch * TW:(ch + 1) * TW], tmps4[i][:, :], AF.Relu,
                    bias=c[:, 4:5], scale=1.0), inc=("SC" if i == 3 else None))
            prevsc = kb.val("SC")
            st["tmps_sc"] = prevsc
        st["ffn_z1v"] = prevsc

    def gu_load(og, slot):
        return kb.dma_group("sync" if og % 2 == 0 else "gpsimd", [
            (ap_of(fw, slot * S0, [[3 * NC * TW, 128], [256, NC], [1, 256]]),
             ap_of(st["wgq_scr"].tensor, og * 256,
                   [[INNER, 128], [128 * INNER, NC], [1, 256]])),
            (ap_of(fw, slot * S0 + NC * 256,
                   [[3 * NC * TW, 128], [256, NC], [1, 256]]),
             ap_of(st["wuq_scr"].tensor, og * 256,
                   [[INNER, 128], [128 * INNER, NC], [1, 256]])),
        ], None)

    def emit_gu(t):
        kb.wait("tensor", "SC", st["ffn_z1v"])
        gu_t_after = {}
        gu_toks = {}
        kb.guard("sync", "T")
        kb.guard("gpsimd", "T")
        gu_toks[0] = gu_load(0, 0)
        for og in range(16):
            slot = og % 2
            bs = [0, 1, 2, 3] if og % 2 == 0 else [4, 5, 6, 7]
            if og + 1 < 16:
                if og >= 1:
                    kb.wait("sync" if (og + 1) % 2 == 0 else "gpsimd",
                            "T", gu_t_after[og - 1])
                gu_toks[og + 1] = gu_load(og + 1, (og + 1) % 2)
            kb.wait_tok("tensor", gu_toks[og])
            bank_wait(bs)
            for ic in range(NC):
                for oo in range(2):
                    off = slot * S0 + ic * 256 + oo * 128
                    kb.op("tensor", lambda e, off=off, oo=oo, ic=ic, bs=bs,
                          f=(ic == 0), g=(ic == NC - 1): e.matmul(
                        pb[bs[oo]], fw[:, off:off + 128],
                        z1[:, ic * TW:(ic + 1) * TW], start=f, stop=g))
                for oo in range(2):
                    off = slot * S0 + NC * 256 + ic * 256 + oo * 128
                    kb.op("tensor", lambda e, off=off, oo=oo, ic=ic, bs=bs,
                          f=(ic == 0), g=(ic == NC - 1): e.matmul(
                        pb[bs[2 + oo]], fw[:, off:off + 128],
                        z1[:, ic * TW:(ic + 1) * TW], start=f, stop=g),
                        inc=("T" if (ic == NC - 1 and oo == 1) else None))
            gu_t_after[og] = kb.val("T")
            tt = kb.val("T")
            kb.wait("vector", "T", tt)
            if st.get("tmps_sc") is not None:
                kb.wait("vector", "SC", st["tmps_sc"])
            for oo in range(2):
                oc = og * 2 + oo
                kb.op("vector", lambda e, oo=oo, oc=oc, bs=bs:
                      e.scalar_tensor_tensor(
                          tmps4[oo][:, :], pb[bs[oo]], cs[:, oc:oc + 1],
                          tgb[:, :], ALU.subtract, ALU.mult))
            cv = kb.commit("vector", "V")
            kb.wait("scalar", "V", cv)
            for oo in range(2):
                kb.op("scalar", lambda e, oo=oo: e.activation(
                    Pt[:, oo * TW:(oo + 1) * TW], tmps4[oo][:, :], AF.Silu),
                    inc=("SC" if oo == 1 else None))
            sg = kb.val("SC")
            st["tmps_sc"] = sg
            kb.wait("vector", "SC", sg)
            for oo in range(2):
                oc = og * 2 + oo
                kb.op("vector", lambda e, oo=oo, oc=oc, bs=bs:
                      e.scalar_tensor_tensor(
                          tmps4[2 + oo][:, :], pb[bs[2 + oo]],
                          cs[:, 32 + oc:33 + oc], tub[:, :],
                          ALU.subtract, ALU.mult))
            kb.drain("vector")
            for oo in range(2):
                oc = og * 2 + oo
                kb.op("vector", lambda e, oo=oo, oc=oc: e.tensor_tensor(
                    prod[:, oc * TW:(oc + 1) * TW], Pt[:, oo * TW:(oo + 1) * TW],
                    tmps4[2 + oo][:, :], ALU.mult))
            ev = kb.commit("vector", "V")
            for b in bs:
                bankv[b] = ev
        st["ffn_gu_last_t"] = kb.val("T")

    def emit_q2(t):
        av = emit_amax(st, lambda chh: prod[:, chh * TW:(chh + 1) * TW], 32,
                       R(st, 2))
        kb.op("vector", lambda e: e.tensor_scalar(
            R(st, 1), R(st, 2), mw[0:1, 2:3], 1.0 / 254.0, ALU.mult, ALU.mult))
        vv2 = kb.commit("vector", "V")
        kb.wait("scalar", "V", vv2)
        kb.op("scalar", lambda e: e.activation(
            st["rows"][32:33, TW:2 * TW], R(st, 2), AF.Ln))
        kb.drain("scalar")
        sxv = kb.op("scalar", lambda e: e.activation(
            R(st, 2), st["rows"][32:33, TW:2 * TW], AF.Exp,
            bias=c[0:1, 3:4], scale=-1.0), inc="SC")
        kb.guard("tensor", "V")
        kb.wait("tensor", "V", vv2)
        kb.op("tensor", lambda e: e.matmul(
            pbh[6][:, :], tb["onesq"][0:1, :], R(st, 1), start=True, stop=True))
        kb.wait("tensor", "SC", sxv)
        kb.op("tensor", lambda e: e.matmul(
            pbh[7][:, :], tb["onesq"][0:1, :], R(st, 2), start=True, stop=True),
            inc="T")
        kb.wait("vector", "T", kb.val("T"))
        kb.op("vector", lambda e: e.tensor_copy(tdb[:, :], pbh[6][:, :]))
        kb.op("vector", lambda e: e.tensor_copy(sxb[:, :], pbh[7][:, :]))
        cbv = kb.commit("vector", "V")
        bankv[6] = cbv
        bankv[7] = cbv
        prevsc = st.get("tmps_sc")
        for b0 in range(0, 32, 4):
            if prevsc is not None:
                kb.wait("vector", "SC", prevsc)
            for i in range(4):
                chh = b0 + i
                kb.op("vector", lambda e, chh=chh, i=i: e.tensor_tensor(
                    tmps4[i][:, :], prod[:, chh * TW:(chh + 1) * TW], sxb[:, :],
                    ALU.mult))
            zv = kb.commit("vector", "V")
            kb.wait("scalar", "V", zv)
            if b0 == 0 and st.get("ffn_dn_last_t") is not None:
                kb.wait("scalar", "T", st["ffn_dn_last_t"])
            for i in range(4):
                chh = b0 + i
                kb.op("scalar", lambda e, chh=chh, i=i: e.activation(
                    z2[:, chh * TW:(chh + 1) * TW], tmps4[i][:, :], AF.Relu,
                    bias=c[:, 4:5], scale=1.0), inc=("SC" if i == 3 else None))
            prevsc = kb.val("SC")
            st["tmps_sc"] = prevsc
        st["ffn_z2v"] = prevsc

    def dn_load(oc):
        return kb.dma("sync", ap_of(fw, 2 * S0,
                                    [[3 * NC * TW, 128], [128, 32], [1, 128]]),
                      ap_of(st["wdq_scr"].tensor, oc * 128,
                            [[D, 128], [128 * D, 32], [1, 128]]), None)

    def emit_dn(t):
        t0 = t * TW
        kb.guard("gpsimd", "V")
        stl = kb.dma_group("gpsimd", [
            (A(STT_, ch), st["s_scr"][ch * 128:(ch + 1) * 128, t0:t0 + TW])
            for ch in range(NC)], None)
        kb.wait("tensor", "SC", st["ffn_z2v"])
        dn_t_after = {}
        dn_toks = {}
        xtoks = {}
        xv_after = {}
        dn_ev = {}
        kb.guard("sync", "T")
        dn_toks[0] = dn_load(0)
        kb.guard("gpsimd", "V")
        xtoks[0] = kb.dma("gpsimd", xoc[:, 0:512],
                          st["xT"][0:128, t0:t0 + TW], None)
        for oc in range(NC):
            bank = 4 + oc % 2
            if oc + 1 < NC:
                if oc >= 1:
                    kb.wait("gpsimd", "V", xv_after[oc - 1])
                xtoks[oc + 1] = kb.dma(
                    "gpsimd",
                    xoc[:, ((oc + 1) % 2) * 512:((oc + 1) % 2) * 512 + 512],
                    st["xT"][(oc + 1) * 128:(oc + 2) * 128, t0:t0 + TW], None)
            kb.wait_tok("tensor", dn_toks[oc])
            bank_wait([bank])
            for ic in range(32):
                off = 2 * S0 + ic * 128
                kb.op("tensor", lambda e, ic=ic, off=off, bank=bank,
                      f=(ic == 0), g=(ic == 31): e.matmul(
                    pb[bank], fw[:, off:off + 128],
                    z2[:, ic * TW:(ic + 1) * TW], start=f, stop=g),
                    inc=("T" if ic == 31 else None))
            dn_t_after[oc] = kb.val("T")
            if oc + 1 < NC:
                kb.wait("sync", "T", dn_t_after[oc])
                dn_toks[oc + 1] = dn_load(oc + 1)
            tt = kb.val("T")
            kb.wait("vector", "T", tt)
            if st.get("tmps_sc") is not None:
                kb.wait("vector", "SC", st["tmps_sc"])
            kb.op("vector", lambda e, bank=bank, oc=oc: e.scalar_tensor_tensor(
                tmp1[:, :], pb[bank], cs[:, 64 + oc:65 + oc], tdb[:, :],
                ALU.subtract, ALU.mult))
            vv = kb.commit("vector", "V")
            kb.wait("scalar", "V", vv)
            sg = kb.op("scalar", lambda e: e.activation(
                tmpsq[:, :], tmp1[:, :], AF.Silu), inc="SC")
            st["tmps_sc"] = sg
            kb.wait("vector", "SC", sg)
            if oc == 0:
                kb.wait_tok("vector", stl)
            kb.op("vector", lambda e, oc=oc: e.tensor_tensor(
                tmp2[:, :], tmpsq[:, :], A(STT_, oc), ALU.mult))
            kb.drain("vector")
            kb.wait_tok("vector", xtoks[oc])
            kb.op("vector", lambda e, oc=oc: e.tensor_tensor(
                tmp2[:, :], tmp2[:, :],
                xoc[:, (oc % 2) * 512:(oc % 2) * 512 + 512], ALU.add))
            vv = kb.commit("vector", "V")
            xv_after[oc] = vv
            bankv[bank] = vv
            kb.wait("gpsimd", "V", vv)
            otok = kb.dma("gpsimd",
                          st["outT"][oc * 128:(oc + 1) * 128, t0:t0 + TW],
                          tmp2[:, :], None)
            kb.wait_tok("vector", otok)
        st["ffn_dn_last_t"] = kb.val("T")

    emit_q1g(0)
    emit_gu(0)
    for t in range(1, nt):
        emit_q1g(t)
        emit_q2(t - 1)
        emit_gu(t)
        emit_dn(t - 1)
    emit_q2(nt - 1)
    emit_dn(nt - 1)



def kernel(**inputs):
    ins = {k: np.asarray(v) for k, v in inputs.items()}
    x = ins["L0_attn_hidden"]
    xTs = [np.ascontiguousarray(x[b].T) for b in range(x.shape[0])]

    qp = _q_col_perm()
    kp = _k_col_perm()
    wop = _wo_row_perm()
    wmap = {
        "g_wq": ins["L0_attn_g_wq"][:, qp],
        "g_kv_down": ins["L0_attn_g_kv_down"],
        "g_k_up": ins["L0_attn_g_k_up"][:, kp],
        "g_v_up": ins["L0_attn_g_v_up"],
        "g_wo": ins["L0_attn_g_wo"][wop, :],
        "l_wq": ins["L0_attn_l_wq"][:, qp],
        "l_wk": ins["L0_attn_l_wk"][:, kp],
        "l_wv": ins["L0_attn_l_wv"],
        "l_wo": ins["L0_attn_l_wo"][wop, :],
        "w_gate": ins["L0_ffn_w_gate"].T,
        "w_up": ins["L0_ffn_w_up"].T,
        "w_down": ins["L0_ffn_w_down"].T,
        "norm1_w": np.ascontiguousarray(ins["L0_attn_norm1_w"].reshape(NC, 128).T),
        "norm2_w": np.ascontiguousarray(ins["L0_attn_norm2_w"].reshape(NC, 128).T),
    }
    wmap = {k: np.ascontiguousarray(np.asarray(v, dtype=np.float32))
            for k, v in wmap.items()}
    tmap = host_tables(S)

    prog = build_program()
    in_maps = []
    for b in range(x.shape[0]):
        m = {"xT": xTs[b]}
        m.update(wmap)
        m.update(tmap)
        in_maps.append(m)
    res = run_bass_kernel_spmd(prog, in_maps, core_ids=list(range(len(in_maps))))
    out = np.stack([np.ascontiguousarray(r["outT"].T) for r in res.results])
    return out


# version 12
# speedup vs baseline: 1.0281x; 1.0251x over previous
import numpy as np

import concourse.bass as bass
import concourse.mybir as mybir
from concourse.bass_utils import run_bass_kernel_spmd

F32 = mybir.dt.float32
F16 = mybir.dt.float16
AF = mybir.ActivationFunctionType
ALU = mybir.AluOpType
AX = mybir.AxisListType

D = 1024
S = 2048
B = 8
H = 16
KVH = 4
HD = 64
LAT = 256
WIN = 256
INNER = 4096
NC = D // 128
TW = 512
MAGIC = 1536.0

Q_CHUNK_HEADS = [(0, 4), (1, 5), (2, 6), (3, 7), (8, 12), (9, 13), (10, 14), (11, 15)]
ENGS = ("sync", "tensor", "vector", "scalar", "gpsimd")

DEBUG_STAGE = None
KEEP_SCOPES = False
SKIP_TABS = False
DEBUG_ROPE = None



def _rope_perm64():
    return np.concatenate([np.arange(0, 64, 2), np.arange(1, 64, 2)])


def _q_col_perm():
    rp = _rope_perm64()
    cols = []
    for (ha, hb) in Q_CHUNK_HEADS:
        cols.append(64 * ha + rp)
        cols.append(64 * hb + rp)
    return np.concatenate(cols)


def _k_col_perm():
    rp = _rope_perm64()
    return np.concatenate([64 * kv + rp for kv in range(KVH)])


def _wo_row_perm():
    rows = []
    for (ha, hb) in Q_CHUNK_HEADS:
        rows.append(64 * ha + np.arange(64))
        rows.append(64 * hb + np.arange(64))
    return np.concatenate(rows)


def _rope_tables(base, s):
    inv = np.power(float(base), -np.arange(0, 32, dtype=np.float64) / 32.0)
    th = np.arange(s, dtype=np.float64)[:, None] * inv[None, :]
    cos = np.cos(th).T
    sin = np.sin(th).T
    C = np.empty((128, s), dtype=np.float32)
    Sg = np.empty((128, s), dtype=np.float32)
    for blk in range(4):
        r = 32 * blk
        C[r:r + 32] = cos.astype(np.float32)
        Sg[r:r + 32] = (sin if blk % 2 else -sin).astype(np.float32)
    return C, Sg


def _swap_mat():
    P = np.zeros((128, 128), dtype=np.float32)
    for p in range(128):
        blk = (p // 64) * 64
        o = p - blk
        P[p, blk + (o + 32) % 64] = 1.0
    return P


def host_tables(s):
    kk = np.arange(128)[:, None]
    qq = np.arange(128)[None, :]
    Cg, Sgt = _rope_tables(1e6, s)
    Cl, Slt = _rope_tables(1e4, s)
    ind2 = np.zeros((128, 2), dtype=np.float32)
    ind2[:64, 0] = 1.0
    ind2[64:, 1] = 1.0
    t = {
        "Cg": Cg, "Sg": Sgt, "Cl": Cl, "Sl": Slt,
        "mask_diag": (kk <= qq).astype(np.float32),
        "mask_su": (kk > qq).astype(np.float32),
        "swap": _swap_mat(),
        "onesq": np.ones((128, 128), dtype=np.float32),
        "ind2": ind2,
        "ind2T": np.ascontiguousarray(ind2.T),
        "ones16": np.ones((128, 1), dtype=np.float16),
    }
    return {k: np.ascontiguousarray(v) for k, v in t.items()}



class KB:
    def __init__(self, nc):
        self.nc = nc
        self.steps = {e: [] for e in ENGS}
        self.counts = {}
        self.sems = {}
        self._ctx = []
        self._scopes = []

    def sem(self, name):
        cm = self.nc.semaphore(name)
        h = cm.__enter__()
        self._ctx.append(cm)
        self.sems[name] = h
        self.counts[name] = 0
        return h

    def sbuf(self, name, shape, dt, scoped=False):
        cm = self.nc.sbuf_tensor(name, shape, dt)
        h = cm.__enter__()
        (self._scopes[-1] if scoped else self._ctx).append(cm)
        return h

    def psum(self, name, shape, dt=F32):
        cm = self.nc.psum_tensor(name, shape, dt)
        h = cm.__enter__()
        self._ctx.append(cm)
        return h

    def push_scope(self):
        self._scopes.append([])

    def pop_scope(self):
        for cm in reversed(self._scopes.pop()):
            cm.__exit__(None, None, None)

    def close(self):
        while self._scopes:
            self.pop_scope()
        while self._ctx:
            self._ctx.pop().__exit__(None, None, None)

    def _push(self, eng, fn):
        self.steps[eng].append(fn)

    def wait(self, eng, sem_name, value):
        h = self.sems[sem_name]
        v = int(value)
        if v > 0:
            self._push(eng, lambda e: e.wait_ge(h, v))

    def guard(self, eng, *sem_names):
        for sn in sem_names:
            self.wait(eng, sn, self.counts[sn])

    def op(self, eng, make, inc=None):
        if inc is None:
            self._push(eng, lambda e: make(e))
            return None
        h = self.sems[inc]
        self.counts[inc] += 1
        self._push(eng, lambda e: make(e).then_inc(h, 1))
        return self.counts[inc]

    def drain(self, eng):
        self._push(eng, lambda e: e.drain())

    def commit(self, eng, sem_name):
        self.drain(eng)
        if eng in ("vector", "gpsimd"):
            return self.op(eng, lambda e: e.engine_nop(), inc=sem_name)
        return self.op(eng, lambda e: e.nop(), inc=sem_name)

    def dma(self, eng, out, in_, sem_name):
        return self.dma_group(eng, [(out, in_)], sem_name)

    def dma_group(self, eng, pairs, sem_name):
        pool = self.dma_pools[eng]
        idx = self.dma_rot[eng] % len(pool)
        self.dma_rot[eng] += 1
        sn = pool[idx]
        self.wait(eng, sn, self.counts[sn])
        h = self.sems[sn]
        for (out, in_) in pairs:
            self.counts[sn] += 16
            self._push(eng, lambda e, o=out, i=in_: e.dma_start(out=o, in_=i)
                       .then_inc(h, 16))
        return (sn, self.counts[sn])

    def wait_tok(self, eng, tok):
        self.wait(eng, tok[0], tok[1])

    def val(self, sem_name):
        return self.counts[sem_name]

    def barrier(self):
        for e in ENGS:
            for sn in list(self.counts):
                self.wait(e, sn, self.counts[sn])


def ap_of(handle, off, dims):
    return bass.AP(handle, off, dims)



def build_program():
    s = S
    nt = s // TW

    nc = bass.Bass("TRN2", target_bir_lowering=False)
    kb = KB(nc)

    xT = nc.dram_tensor("xT", [D, s], F32, kind="ExternalInput").ap()
    w = {}
    for name, shape in [
        ("g_wq", [D, D]), ("g_kv_down", [D, LAT]), ("g_k_up", [LAT, LAT]),
        ("g_v_up", [LAT, LAT]), ("g_wo", [D, D]),
        ("l_wq", [D, D]), ("l_wk", [D, LAT]), ("l_wv", [D, LAT]), ("l_wo", [D, D]),
        ("w_gate", [D, INNER]), ("w_up", [D, INNER]), ("w_down", [INNER, D]),
        ("norm1_w", [128, NC]), ("norm2_w", [128, NC]),
    ]:
        w[name] = nc.dram_tensor(name, shape, F32, kind="ExternalInput").ap()
    tab = {}
    for name, shape, dt in [
        ("Cg", [128, s], F32), ("Sg", [128, s], F32),
        ("Cl", [128, s], F32), ("Sl", [128, s], F32),
        ("mask_diag", [128, 128], F32), ("mask_su", [128, 128], F32),
        ("swap", [128, 128], F32), ("onesq", [128, 128], F32),
        ("ind2", [128, 2], F32), ("ind2T", [2, 128], F32),
        ("ones16", [128, 1], F16),
    ]:
        tab[name] = nc.dram_tensor(name, shape, dt, kind="ExternalInput").ap()
    outT = nc.dram_tensor("outT", [D, s], F32, kind="ExternalOutput").ap()
    dbgT = None
    if DEBUG_STAGE is not None:
        dbgT = nc.dram_tensor("dbgT", [D, s], F32, kind="ExternalOutput").ap()

    xn_scr = nc.dram_tensor("xn_scr", [D, s], F32).ap()
    s_scr = nc.dram_tensor("s_scr", [D, s], F32).ap()
    wgq_scr = nc.dram_tensor("wgq_scr", [D, INNER], F16).ap()
    wuq_scr = nc.dram_tensor("wuq_scr", [D, INNER], F16).ap()
    wdq_scr = nc.dram_tensor("wdq_scr", [INNER, D], F16).ap()

    for sn in ("T", "V", "SC", "G"):
        kb.sem(sn)
    kb.dma_pools = {"sync": [], "scalar": [], "gpsimd": []}
    kb.dma_rot = {"sync": 0, "scalar": 0, "gpsimd": 0}
    for q, pref in (("sync", "LW"), ("scalar", "LS"), ("gpsimd", "LG")):
        for i in range(4):
            sn = f"{pref}{i}"
            kb.sem(sn)
            kb.dma_pools[q].append(sn)

    consts = kb.sbuf("consts", [128, 16], F32)
    rows = kb.sbuf("rows", [128, 3 * TW], F32)
    tmp1 = kb.sbuf("tmp1", [128, TW], F32)
    tmp2 = kb.sbuf("tmp2", [128, TW], F32)
    tmpsq = kb.sbuf("tmpsq", [128, TW], F32)
    Pt = kb.sbuf("Pt", [128, 4 * TW], F32)
    cs = kb.sbuf("cs_sb", [128, 72], F32)
    mw = kb.sbuf("mw_sb", [128, 4], F32)
    tb = {}
    for k in ("mask_diag", "mask_su", "swap", "onesq", "ind2", "ind2T"):
        tb[k] = kb.sbuf("tb_" + k, list(tab[k].tensor.shape), F32)
    tb["ones16"] = kb.sbuf("tb_ones16", [128, 1], F16)

    pbh = [kb.psum(f"pb{i}", [128, 512]) for i in range(8)]
    pb = [t[:, :] for t in pbh]

    st = dict(nc=nc, kb=kb, s=s, nt=nt, w=w, tab=tab, tb=tb,
              consts=consts, rows=rows, tmp1=tmp1, tmp2=tmp2, tmpsq=tmpsq,
              Pt=Pt, cs=cs, mw=mw, pb=pb, pbh=pbh,
              xT=xT, outT=outT, dbgT=dbgT,
              xn_scr=xn_scr, s_scr=s_scr,
              wgq_scr=wgq_scr, wuq_scr=wuq_scr, wdq_scr=wdq_scr)

    emit_all(st)

    blk = nc.Block()
    block = blk.__enter__()
    kb._ctx.append(blk)
    for eng_name in ENGS:
        fns = kb.steps[eng_name]

        def body(e, fns=fns):
            for f in fns:
                f(e)
        getattr(block, eng_name)(body)
    kb.close()
    return nc


def R(st, k, n=TW):
    return st["rows"][0:1, k * TW:k * TW + n]


def R2(st, k, n=TW):
    return st["rows"][0:2, k * TW:k * TW + n]


def R64(st, k, n=TW):
    return st["rows"][64:65, k * TW:k * TW + n]


def R96(st, k, n=TW):
    return st["rows"][96:97, k * TW:k * TW + n]


def emit_all(st):
    kb = st["kb"]

    st["tabs_tok"] = kb.dma_group(
        "sync", [(st["tb"][k][:, :], st["tab"][k][:, :])
                 for k in ("mask_diag", "mask_su", "swap", "onesq",
                           "ind2", "ind2T", "ones16")], None)

    c = st["consts"]
    kb.op("vector", lambda e: e.memset(c[:, 0:1], -8.0))
    kb.op("vector", lambda e: e.memset(c[:, 1:2], 1e-5))
    kb.op("vector", lambda e: e.memset(c[:, 2:3], 1e-6))
    kb.op("vector", lambda e: e.memset(c[:, 3:4], float(np.log(127.0))))
    kb.op("vector", lambda e: e.memset(c[:, 4:5], MAGIC))
    kb.commit("vector", "V")
    st["consts_done"] = kb.val("V")

    kb.push_scope()
    st["xn"] = kb.sbuf("xn", [128, NC * st["s"]], F32, scoped=True)
    st["xst"] = kb.sbuf("xst", [128, NC * TW], F32, scoped=True)
    st["KG"] = kb.sbuf("KG", [128, 2 * st["s"]], F32, scoped=True)
    st["VG"] = kb.sbuf("VG", [128, (st["s"] // 128) * 4 * 65], F32, scoped=True)
    st["QT"] = kb.sbuf("QT", [128, NC * TW], F32, scoped=True)
    st["AO"] = kb.sbuf("AO", [128, NC * TW], F32, scoped=True)
    st["lat"] = kb.sbuf("lat", [128, 2 * st["s"]], F32, scoped=True)
    st["wst"] = kb.sbuf("wst", [128, 2 * 1024], F32, scoped=True)
    st["CSc"] = kb.sbuf("CSc", [128, st["s"]], F32, scoped=True)
    st["CSs"] = kb.sbuf("CSs", [128, st["s"]], F32, scoped=True)

    emit_ln(st)
    if DEBUG_STAGE == "xn_plus":
        rms_scale_bcast(st, 0)
        emit_debug_xn(st, st["xn"])
        emit_tail(st)
        return
    if DEBUG_STAGE == "xn_barrier":
        kb.barrier()
        emit_debug_xn(st, st["xn"])
        emit_tail(st)
        return
    if DEBUG_STAGE == "wq":
        kb.barrier()
        if not KEEP_SCOPES:
            kb.pop_scope()
        kb.push_scope()
        st["wf"] = kb.sbuf("wf", [128, 2 * INNER], F32, scoped=True)
        st["wq16"] = kb.sbuf("wq16", [128, 2 * INNER], F16, scoped=True)
        st["sgn2"] = kb.sbuf("sgn2", [128, 2 * INNER], F16, scoped=True)
        emit_wquant(st)
        kb.barrier()
        kb.guard("gpsimd", "V", "T", "SC")
        kb.dma("gpsimd", st["dbgT"][0:128, 0:72], st["cs"][:, :], None)
        kb.dma("gpsimd", st["outT"][0:128, 0:72], st["cs"][:, :], None)
        emit_tail(st)
        return
    if DEBUG_STAGE == "xn":
        emit_debug_xn(st, st["xn"])
        emit_tail(st)
        return
    emit_attn(st, glob=True)
    if DEBUG_STAGE in ("q0", "sdpa1"):
        kb.barrier()
        kb.guard("gpsimd", "V", "T", "SC")
        if DEBUG_STAGE == "q0":
            pairs = [(st["outT"][ch * 128:(ch + 1) * 128, 0:TW],
                      st["QT"][:, ch * TW:(ch + 1) * TW]) for ch in range(NC)]
        else:
            pairs = [(st["outT"][0:128, 0:TW], st["AO"][:, 0:TW])]
        kb.dma_group("gpsimd", pairs, None)
        emit_tail(st)
        return
    if DEBUG_STAGE in ("norm", "norm_nodma", "rms0", "rmsA", "rmsB"):
        kb.barrier()
        emit_debug_xn(st, st["xn"])
        emit_tail(st)
        return
    if DEBUG_STAGE in ("lat", "k"):
        kb.barrier()
        kb.guard("gpsimd", "V", "T", "SC")
        if DEBUG_STAGE == "lat":
            pairs = [(st["outT"][0:128, :], st["lat"][:, 0:st["s"]]),
                     (st["outT"][128:256, :], st["lat"][:, st["s"]:2 * st["s"]])]
        else:
            pairs = [(st["outT"][0:128, :], st["KG"][:, 0:st["s"]]),
                     (st["outT"][128:256, :], st["KG"][:, st["s"]:2 * st["s"]])]
        kb.dma_group("gpsimd", pairs, None)
        emit_tail(st)
        return
    if DEBUG_STAGE in ("kv", "q0"):
        kb.barrier()
        kb.guard("gpsimd", "V", "T", "SC")
        if DEBUG_STAGE == "kv":
            pairs = [(st["outT"][0:128, :], st["KG"][:, 0:st["s"]]),
                     (st["outT"][128:256, :], st["KG"][:, st["s"]:2 * st["s"]])]
            vgf = (st["s"] // 128) * 260
            off = 0
            row = 256
            while off < vgf:
                wdt = min(st["s"], vgf - off)
                pairs.append((st["outT"][row:row + 128, 0:wdt],
                              st["VG"][:, off:off + wdt]))
                off += wdt
                row += 128
        else:
            pairs = [(st["outT"][ch * 128:(ch + 1) * 128, 0:TW],
                      st["QT"][:, ch * TW:(ch + 1) * TW]) for ch in range(NC)]
        kb.dma_group("gpsimd", pairs, None)
        emit_tail(st)
        return
    if DEBUG_STAGE == "s1":
        emit_debug_xn(st, st["xn"])
        emit_tail(st)
        return
    emit_attn(st, glob=False)
    emit_store_s(st)
    if DEBUG_STAGE == "s":
        emit_debug_xn(st, st["xn"])
        emit_tail(st)
        return
    kb.barrier()
    if not KEEP_SCOPES:
        kb.pop_scope()

    kb.push_scope()
    st["wf"] = kb.sbuf("wf", [128, 2 * INNER], F32, scoped=True)
    st["wq16"] = kb.sbuf("wq16", [128, 2 * INNER], F16, scoped=True)
    st["sgn2"] = kb.sbuf("sgn2", [128, 2 * INNER], F16, scoped=True)
    emit_wquant(st)
    kb.barrier()
    if not KEEP_SCOPES:
        kb.pop_scope()

    kb.push_scope()
    st["arena"] = kb.sbuf("arena", [128, 2 * NC * TW], F32, scoped=True)
    st["xoc"] = kb.sbuf("xoc", [128, 2 * 512], F32, scoped=True)
    st["z1"] = kb.sbuf("z1", [128, NC * TW], F16, scoped=True)
    st["z2"] = kb.sbuf("z2", [128, 32 * TW], F16, scoped=True)
    st["prod"] = kb.sbuf("prod", [128, 32 * TW], F32, scoped=True)
    st["fw"] = kb.sbuf("fw", [128, 3 * NC * TW], F16, scoped=True)
    st["sxb"] = kb.sbuf("sxb", [128, TW], F32, scoped=True)
    st["tgb"] = kb.sbuf("tgb", [128, TW], F32, scoped=True)
    st["tub"] = kb.sbuf("tub", [128, TW], F32, scoped=True)
    st["tdb"] = kb.sbuf("tdb", [128, TW], F32, scoped=True)
    st["tmp4"] = kb.sbuf("tmp4", [128, TW], F32, scoped=True)
    emit_ffn(st)
    kb.pop_scope()

    emit_tail(st)


def emit_tail(st):
    kb = st["kb"]
    for q in ("sync", "scalar", "gpsimd"):
        for sn in kb.dma_pools[q]:
            kb.wait(q, sn, kb.val(sn))


def emit_debug_xn(st, src):
    kb, s = st["kb"], st["s"]
    kb.guard("gpsimd", "V", "T", "SC")
    pairs = []
    for ch in range(NC):
        pairs.append((st["outT"][ch * 128:(ch + 1) * 128, :],
                      src[:, ch * s:(ch + 1) * s]))
        pairs.append((st["dbgT"][ch * 128:(ch + 1) * 128, :],
                      src[:, ch * s:(ch + 1) * s]))
    kb.dma_group("gpsimd", pairs, None)



def emit_ln(st):
    kb, s, nt = st["kb"], st["s"], st["nt"]
    xn, xst, tmpsq, c, pb, tb = (st["xn"], st["xst"], st["tmpsq"], st["consts"],
                                 st["pb"], st["tb"])

    kb.wait_tok("scalar", st["tabs_tok"])
    kb.wait_tok("tensor", st["tabs_tok"])
    kb.wait("vector", "V", st["consts_done"])

    for t in range(nt):
        t0 = t * TW
        kb.guard("sync", "T", "V", "SC")
        xl = kb.dma_group(
            "sync", [(xst[:, ch * TW:(ch + 1) * TW],
                      st["xT"][ch * 128:(ch + 1) * 128, t0:t0 + TW])
                     for ch in range(NC)], None)
        kb.wait_tok("scalar", xl)
        kb.wait_tok("tensor", xl)

        kb.guard("scalar", "T")
        tvals = {}
        for ch in range(NC):
            xc = xst[:, ch * TW:(ch + 1) * TW]
            buf = tmpsq if ch % 2 == 0 else st["tmp2"]
            if ch >= 2:
                kb.wait("scalar", "T", tvals[ch - 2])
            sq = kb.op("scalar", lambda e, xc=xc, buf=buf: e.activation(
                buf[:, :], xc, AF.Square), inc="SC")
            kb.op("tensor", lambda e, xc=xc, f=(ch == 0), g=(ch == NC - 1):
                  e.matmul(pb[0][0:1, :], tb["onesq"][:, 0:1], xc,
                           start=f, stop=g))
            kb.wait("tensor", "SC", sq)
            tvals[ch] = kb.op("tensor", lambda e, buf=buf, f=(ch == 0),
                              g=(ch == NC - 1): e.matmul(
                pb[1][0:1, :], tb["onesq"][:, 0:1], buf[:, :],
                start=f, stop=g), inc="T")
        red = kb.val("T")

        kb.wait("vector", "T", red)
        kb.op("vector", lambda e: e.tensor_scalar(
            R(st, 0), pb[0][0:1, :], 1.0 / D, None, ALU.mult))
        kb.drain("vector")
        kb.op("vector", lambda e: e.tensor_tensor(
            R(st, 1), R(st, 0), R(st, 0), ALU.mult))
        kb.drain("vector")
        vv = kb.commit("vector", "V")
        kb.op("vector", lambda e: e.scalar_tensor_tensor(
            R(st, 1), pb[1][0:1, :], 1.0 / D, R(st, 1), ALU.mult, ALU.subtract))
        vv = kb.commit("vector", "V")
        kb.wait("scalar", "V", vv)
        kb.op("scalar", lambda e: e.activation(
            R(st, 2), R(st, 1), AF.Ln, bias=c[0:1, 1:2]))
        kb.drain("scalar")
        sq2 = kb.op("scalar", lambda e: e.activation(
            R(st, 1), R(st, 2), AF.Exp, bias=0.0, scale=-0.5), inc="SC")
        kb.wait("vector", "SC", sq2)
        kb.op("vector", lambda e: e.tensor_tensor(
            R(st, 2), R(st, 0), R(st, 1), ALU.mult))
        vv = kb.commit("vector", "V")

        kb.wait("tensor", "V", vv)
        kb.op("tensor", lambda e: e.matmul(
            pb[2], tb["onesq"][0:1, :], R(st, 1), start=True, stop=True))
        tbc = kb.op("tensor", lambda e: e.matmul(
            pb[3], tb["onesq"][0:1, :], R(st, 2), start=True, stop=True), inc="T")

        kb.wait("vector", "T", tbc)
        for ch in range(NC):
            xc = xst[:, ch * TW:(ch + 1) * TW]
            xnc = xn[:, ch * s + t0: ch * s + t0 + TW]
            kb.op("vector", lambda e, xc=xc, xnc=xnc: e.tensor_tensor(
                xnc, xc, pb[2], ALU.mult))
        kb.drain("vector")
        for ch in range(NC):
            xnc = xn[:, ch * s + t0: ch * s + t0 + TW]
            kb.op("vector", lambda e, xnc=xnc: e.tensor_tensor(
                xnc, xnc, pb[3], ALU.subtract))
        vv = kb.commit("vector", "V")
        kb.wait("gpsimd", "V", vv)
        kb.dma_group("gpsimd",
                     [(st["xn_scr"][ch * 128:(ch + 1) * 128, t0:t0 + TW],
                       xn[:, ch * s + t0: ch * s + t0 + TW])
                      for ch in range(NC)], None)
        kb.wait("tensor", "V", vv)



def rms_scale_bcast(st, t0):
    kb, s = st["kb"], st["s"]
    pb, tb, tmpsq, c = st["pb"], st["tb"], st["tmpsq"], st["consts"]
    kb.guard("scalar", "T")
    tvals = {}
    for ch in range(NC):
        xc = st["xn"][:, ch * s + t0: ch * s + t0 + TW]
        buf = tmpsq if ch % 2 == 0 else st["tmp2"]
        if ch >= 2:
            kb.wait("scalar", "T", tvals[ch - 2])
        sq = kb.op("scalar", lambda e, xc=xc, buf=buf: e.activation(
            buf[:, :], xc, AF.Square), inc="SC")
        kb.wait("tensor", "SC", sq)
        tvals[ch] = kb.op("tensor", lambda e, buf=buf, f=(ch == 0),
                          g=(ch == NC - 1): e.matmul(
            pb[0][0:1, :], tb["onesq"][:, 0:1], buf[:, :], start=f, stop=g),
            inc="T")
    red = kb.val("T")
    if DEBUG_STAGE == "rmsA":
        return red
    kb.wait("vector", "T", red)
    kb.op("vector", lambda e: e.tensor_scalar(
        R(st, 0), pb[0][0:1, :], 1.0 / D, None, ALU.mult))
    vv = kb.commit("vector", "V")
    kb.wait("scalar", "V", vv)
    kb.op("scalar", lambda e: e.activation(
        R(st, 2), R(st, 0), AF.Ln, bias=c[0:1, 2:3]))
    kb.drain("scalar")
    sq2 = kb.op("scalar", lambda e: e.activation(
        R(st, 0), R(st, 2), AF.Exp, bias=0.0, scale=-0.5), inc="SC")
    if DEBUG_STAGE == "rmsB":
        return kb.val("T")
    kb.wait("tensor", "SC", sq2)
    return kb.op("tensor", lambda e: e.matmul(
        pb[1], tb["onesq"][0:1, :], R(st, 0), start=True, stop=True), inc="T")


def norm_tile(st, t0, dst):
    kb, s = st["kb"], st["s"]
    c = st["consts"]
    for ch in range(NC):
        src_c = st["xn"][:, ch * s + t0: ch * s + t0 + TW]
        dst_c = dst[:, ch * TW:(ch + 1) * TW]
        kb.op("vector", lambda e, a=src_c, b=dst_c, ch=ch: e.scalar_tensor_tensor(
            b, a, c[:, 8 + ch:9 + ch], st["pb"][1], ALU.mult, ALU.mult))
    return kb.commit("vector", "V")


def rope_qknorm(st, src_psum, t0, out_region, ncols=TW, deps=None):
    kb = st["kb"]
    kb.op("vector", lambda e, o=out_region: e.tensor_copy(o, src_psum))
    vv = kb.commit("vector", "V")
    if deps is not None:
        deps = dict(deps, chunk_v=vv)
    return rope_qknorm_sb(st, t0, out_region, ncols, deps=deps)


def rope_qknorm_sb(st, t0, chunk, ncols=TW, deps=None):
    kb = st["kb"]
    pb, tb, c = st["pb"], st["tb"], st["consts"]
    tmp1, tmp2, tmpsq = st["tmp1"], st["tmp2"], st["tmpsq"]
    CSc, CSs = st["CSc"], st["CSs"]

    vv = kb.val("V")
    if deps is None:
        kb.guard("scalar", "T", "V")
    else:
        if deps.get("t_ind2") is not None:
            kb.wait("scalar", "T", deps["t_ind2"])
        kb.wait("scalar", "V", deps["chunk_v"])
    sq = kb.op("scalar", lambda e, ch=chunk: e.activation(
        tmpsq[:, 0:ncols], ch, AF.Square), inc="SC")
    if deps is not None and deps.get("v_final") is not None:
        kb.wait("tensor", "V", deps["v_final"])
    kb.wait("tensor", "V", vv if deps is None else deps["chunk_v"])
    kb.op("tensor", lambda e, ch=chunk: e.matmul(
        pb[6][:, 0:ncols], tb["swap"][:, :], ch, start=True, stop=True))
    kb.wait("tensor", "SC", sq)
    tt = kb.op("tensor", lambda e: e.matmul(
        pb[7][0:2, 0:ncols], tb["ind2"][:, :], tmpsq[:, 0:ncols],
        start=True, stop=True), inc="T")
    kb.wait("vector", "T", tt)
    kb.op("vector", lambda e: e.tensor_scalar(
        R2(st, 1, ncols), pb[7][0:2, 0:ncols], 1.0 / HD, None, ALU.mult))
    vv = kb.commit("vector", "V")
    kb.wait("scalar", "V", vv)
    kb.op("scalar", lambda e: e.activation(
        R2(st, 2, ncols), R2(st, 1, ncols), AF.Ln, bias=c[0:2, 2:3]))
    kb.drain("scalar")
    sq2 = kb.op("scalar", lambda e: e.activation(
        R2(st, 1, ncols), R2(st, 2, ncols), AF.Exp, bias=0.0, scale=-0.5),
        inc="SC")
    kb.wait("tensor", "SC", sq2)
    tt = kb.op("tensor", lambda e: e.matmul(
        pb[7][:, 0:ncols], tb["ind2T"][0:2, :], R2(st, 1, ncols),
        start=True, stop=True), inc="T")
    Ct = CSc[:, t0:t0 + ncols]
    Stt = CSs[:, t0:t0 + ncols]
    kb.op("vector", lambda e, ch=chunk, Ct=Ct: e.tensor_tensor(
        tmp1[:, 0:ncols], ch, Ct, ALU.mult))
    kb.wait("vector", "T", tt)
    kb.op("vector", lambda e, Stt=Stt: e.tensor_tensor(
        tmp2[:, 0:ncols], pb[6][:, 0:ncols], Stt, ALU.mult))
    kb.drain("vector")
    kb.op("vector", lambda e: e.tensor_tensor(
        tmp1[:, 0:ncols], tmp1[:, 0:ncols], tmp2[:, 0:ncols], ALU.add))
    kb.drain("vector")
    kb.op("vector", lambda e, ch=chunk: e.tensor_tensor(
        ch, tmp1[:, 0:ncols], pb[7][:, 0:ncols], ALU.mult))
    vfin = kb.commit("vector", "V")
    return {"t_ind2": tt, "v_final": vfin,
            "chunk_v": (deps["chunk_v"] if deps is not None else vfin)}


def band_load(st, dst, src, n=0):
    kb = st["kb"]
    kb.guard("sync", "T")
    return kb.dma("sync", dst, src, None)


def emit_attn(st, glob):
    kb, s, nt = st["kb"], st["s"], st["nt"]
    w, tb, c, pb = st["w"], st["tb"], st["consts"], st["pb"]
    xn, KG, VG, QT, AO, lat, wst, xst = (st["xn"], st["KG"], st["VG"], st["QT"],
                                         st["AO"], st["lat"], st["wst"], st["xst"])

    wq = w["g_wq" if glob else "l_wq"]
    wo = w["g_wo" if glob else "l_wo"]
    nw = w["norm1_w" if glob else "norm2_w"]

    kb.guard("sync", "V")
    tl = kb.dma_group("sync", [
        (st["CSc"][:, :], st["tab"]["Cg" if glob else "Cl"][:, :]),
        (st["CSs"][:, :], st["tab"]["Sg" if glob else "Sl"][:, :]),
        (ap_of(st["consts"], 8, [[16, 128], [1, NC]]), nw[:, :]),
    ], None)
    for e in ("vector", "tensor", "scalar"):
        kb.wait_tok(e, tl)

    if DEBUG_STAGE in ("xn_pre", "xn_pre2", "xn_pre3", "xn_pre4", "xn_pre5"):
        return
    for t in range(nt):
        t0 = t * TW
        bc = rms_scale_bcast(st, t0)
        kb.wait("vector", "T", bc)
        kb.guard("vector", "T")
        if DEBUG_STAGE in ("rms0", "rmsA", "rmsB"):
            break
        nt_done = norm_tile(st, t0, xst)
        if DEBUG_STAGE in ("norm", "norm_nodma"):
            break

        if glob:
            kv_down = w["g_kv_down"]
            kup = w["g_k_up"]
            vup = w["g_v_up"]
            kb.wait("tensor", "V", nt_done)
            toksA = {}
            tA = {}
            kb.guard("sync", "T")
            toksA[0] = kb.dma("sync", wst[:, 0:LAT], kv_down[0:128, :], None)
            for kc in range(NC):
                if kc + 1 < NC:
                    if kc >= 1:
                        kb.wait("sync", "T", tA[kc - 1])
                    toksA[kc + 1] = kb.dma(
                        "sync",
                        wst[:, ((kc + 1) % 2) * 1024:((kc + 1) % 2) * 1024 + LAT],
                        kv_down[(kc + 1) * 128:(kc + 2) * 128, :], None)
                kb.wait_tok("tensor", toksA[kc])
                buf = wst[:, (kc % 2) * 1024:(kc % 2) * 1024 + LAT]
                for lc in range(2):
                    kb.op("tensor", lambda e, buf=buf, lc=lc, kc=kc, f=(kc == 0),
                          g=(kc == NC - 1):
                          e.matmul(pb[4 + lc],
                                   buf[:, lc * 128:(lc + 1) * 128],
                                   xst[:, kc * TW:(kc + 1) * TW],
                                   start=f, stop=g),
                          inc=("T" if lc == 1 else None))
                tA[kc] = kb.val("T")
            tt = kb.val("T")
            kb.wait("sync", "T", tt)
            wlk = kb.dma_group("sync", [
                (wst[:, 0:LAT], kup[0:128, :]),
                (wst[:, 1024:1024 + LAT], kup[128:256, :]),
                (wst[:, 256:256 + LAT], vup[0:128, :]),
                (wst[:, 1280:1280 + LAT], vup[128:256, :])], None)
            kb.wait("vector", "T", tt)
            kb.guard("vector", "T")
            for lc in range(2):
                kb.op("vector", lambda e, lc=lc, t0=t0: e.tensor_copy(
                    lat[:, lc * s + t0: lc * s + t0 + TW], pb[4 + lc]))
            lat_done = kb.commit("vector", "V")
            if DEBUG_STAGE == "lat":
                continue

            kb.wait_tok("tensor", wlk)
            kb.wait("tensor", "V", lat_done)
            for oc in range(2):
                for lc in range(2):
                    kb.op("tensor", lambda e, oc=oc, lc=lc, t0=t0, f=(lc == 0),
                          g=(lc == 1): e.matmul(
                        pb[4 + oc],
                        wst[:, lc * 1024 + oc * 128: lc * 1024 + (oc + 1) * 128],
                        lat[:, lc * s + t0: lc * s + t0 + TW], start=f, stop=g),
                        inc=("T" if lc == 1 else None))
            kb.wait("vector", "T", kb.val("T"))
            kdeps = {"t_ind2": None, "v_final": None, "chunk_v": None}
            for oc in range(2):
                kdeps = rope_qknorm(st, pb[4 + oc], t0,
                                    KG[:, oc * s + t0: oc * s + t0 + TW],
                                    deps=kdeps)
            kb.wait("tensor", "V", kb.val("V"))

            if DEBUG_STAGE == "k":
                continue
            emit_v_proj(st, t0, nk=2,
                        lhs_fn=lambda kc, tc4, t0=t0: lat[:, kc * s + t0 + tc4 * 128:
                                                          kc * s + t0 + (tc4 + 1) * 128],
                        rhs_fn=lambda kc: wst[:, kc * 1024 + 256: kc * 1024 + 256 + LAT])
        else:
            lwk = w["l_wk"]
            kb.wait("tensor", "V", nt_done)
            toksA = {}
            tA = {}
            kb.guard("sync", "T")
            toksA[0] = kb.dma("sync", wst[:, 0:LAT], lwk[0:128, :], None)
            for kc in range(NC):
                if kc + 1 < NC:
                    if kc >= 1:
                        kb.wait("sync", "T", tA[kc - 1])
                    toksA[kc + 1] = kb.dma(
                        "sync",
                        wst[:, ((kc + 1) % 2) * 1024:((kc + 1) % 2) * 1024 + LAT],
                        lwk[(kc + 1) * 128:(kc + 2) * 128, :], None)
                kb.wait_tok("tensor", toksA[kc])
                buf = wst[:, (kc % 2) * 1024:(kc % 2) * 1024 + LAT]
                for oc in range(2):
                    kb.op("tensor", lambda e, buf=buf, oc=oc, kc=kc, f=(kc == 0),
                          g=(kc == NC - 1):
                          e.matmul(pb[4 + oc], buf[:, oc * 128:(oc + 1) * 128],
                                   xst[:, kc * TW:(kc + 1) * TW], start=f, stop=g),
                          inc=("T" if oc == 1 else None))
                tA[kc] = kb.val("T")
            tt = kb.val("T")
            kb.wait("vector", "T", tt)
            kdeps = {"t_ind2": None, "v_final": None, "chunk_v": None}
            for oc in range(2):
                kdeps = rope_qknorm(st, pb[4 + oc], t0,
                                    KG[:, oc * s + t0: oc * s + t0 + TW],
                                    deps=kdeps)
            kb.wait("tensor", "V", kb.val("V"))

            emit_v_proj_local(st, t0, w["l_wv"])

    if DEBUG_STAGE in ("lat", "k", "kv") and glob:
        return
    for t in range(nt):
        t0 = t * TW
        bc = rms_scale_bcast(st, t0)
        kb.wait("vector", "T", bc)
        kb.guard("vector", "T")
        nt_done = norm_tile(st, t0, xst)
        kb.wait("tensor", "V", nt_done)

        t_after = {}
        toks = {}
        kb.guard("gpsimd", "T")
        toks[0] = kb.dma("gpsimd", wst[:, 0:1024], wq[0:128, :], None)
        for kc in range(NC):
            if kc + 1 < NC:
                if kc >= 1:
                    kb.wait("gpsimd", "T", t_after[kc - 1])
                toks[kc + 1] = kb.dma(
                    "gpsimd", wst[:, ((kc + 1) % 2) * 1024:((kc + 1) % 2) * 1024 + 1024],
                    wq[(kc + 1) * 128:(kc + 2) * 128, :], None)
            kb.wait_tok("tensor", toks[kc])
            buf = wst[:, (kc % 2) * 1024:(kc % 2) * 1024 + 1024]
            for oc in range(NC):
                kb.op("tensor", lambda e, buf=buf, oc=oc, kc=kc, f=(kc == 0),
                      g=(kc == NC - 1):
                      e.matmul(pb[oc], buf[:, oc * 128:(oc + 1) * 128],
                               xst[:, kc * TW:(kc + 1) * TW], start=f, stop=g),
                      inc=("T" if oc == NC - 1 else None))
            t_after[kc] = kb.val("T")
        kb.wait("vector", "T", kb.val("T"))
        for oc in range(NC):
            kb.op("vector", lambda e, oc=oc: e.tensor_copy(
                QT[:, oc * TW:(oc + 1) * TW], pb[oc]))
        qv = kb.commit("vector", "V")
        kb.guard("sync", "T")
        wo_toks = {0: kb.dma("sync", wst[:, 0:1024], wo[0:128, :], None),
                   1: kb.dma("sync", wst[:, 1024:2048], wo[128:256, :], None)}
        st["sdpa_sbv"] = {b: qv for b in (0, 1, 6, 7)}
        st["sdpa_pvv"] = {j: qv for j in range(4)}
        deps = {"t_ind2": None, "v_final": None, "chunk_v": qv}
        for oc in range(4):
            deps = rope_qknorm_sb(st, t0, QT[:, oc * TW:(oc + 1) * TW],
                                  deps=deps)
        emit_sdpa_kv(st, glob, t, 0, deep=False, qready=deps["v_final"])
        for oc in range(4, NC):
            deps = rope_qknorm_sb(st, t0, QT[:, oc * TW:(oc + 1) * TW],
                                  deps=deps)
        st["sdpa_sbv"][6] = st["sdpa_sbv"][7] = deps["v_final"]
        emit_sdpa_kv(st, glob, t, 1, qready=deps["v_final"])
        emit_sdpa_kv(st, glob, t, 2)
        emit_sdpa_kv(st, glob, t, 3)

        t_after = {}
        toks = wo_toks
        for tok in st.pop("ao_lift", []):
            kb.wait_tok("tensor", tok)
        kb.guard("tensor", "V")
        for kc in range(NC):
            kb.wait_tok("tensor", toks[kc])
            buf = wst[:, (kc % 2) * 1024:(kc % 2) * 1024 + 1024]
            for oc in range(NC):
                kb.op("tensor", lambda e, buf=buf, oc=oc, kc=kc, f=(kc == 0),
                      g=(kc == NC - 1):
                      e.matmul(pb[oc], buf[:, oc * 128:(oc + 1) * 128],
                               AO[:, kc * TW:(kc + 1) * TW], start=f, stop=g),
                      inc=("T" if oc == NC - 1 else None))
            t_after[kc] = kb.val("T")
            if kc + 2 < NC:
                kb.wait("sync", "T", t_after[kc])
                toks[kc + 2] = kb.dma(
                    "sync", wst[:, (kc % 2) * 1024:(kc % 2) * 1024 + 1024],
                    wo[(kc + 2) * 128:(kc + 3) * 128, :], None)
        tt = kb.val("T")
        kb.wait("vector", "T", tt)
        for oc in range(NC):
            dstc = xn[:, oc * s + t0: oc * s + t0 + TW]
            kb.op("vector", lambda e, oc=oc, dstc=dstc: e.tensor_tensor(
                dstc, dstc, pb[oc], ALU.add))
        vv = kb.commit("vector", "V")
        kb.wait("tensor", "V", vv)


def emit_sdpa_kv(st, glob, t, kv, deep=True, qready=None):
    kb, s = st["kb"], st["s"]
    pb, pbh, tb, Pt, VG, KG, QT, AO = (st["pb"], st["pbh"], st["tb"], st["Pt"],
                                       st["VG"], st["KG"], st["QT"], st["AO"])
    Pt2 = st["lat"]
    c = st["consts"]
    t0 = t * TW
    nblk = 4 * 65
    vg_free = (s // 128) * nblk
    rb = 64 * (kv % 2)
    kcn = kv // 2
    sbanks = [0, 1, 6, 7] if deep else [0, 1, 0, 1]
    sbv = st["sdpa_sbv"]
    pvv = st["sdpa_pvv"]
    lastpv = st.setdefault("sdpa_lastpv", {})

    if qready is not None:
        kb.wait("tensor", "V", qready)

    pos = []
    for qc, (ha, hb) in enumerate(Q_CHUNK_HEADS):
        if ha // 4 == kv:
            pos.append((qc, 0))
        if hb // 4 == kv:
            pos.append((qc, 1))
    assert len(pos) == 4

    chunks = []
    if glob:
        for j in range(0, t * 4 + 4):
            k0 = j * 128
            qoff = max(0, k0 - t0)
            subs = [(qoff, "d")] if k0 >= t0 else []
            chunks.append((k0, qoff, TW - qoff, subs))
    else:
        for j in range(-2, 4):
            k0 = t0 + j * 128
            if k0 < 0:
                continue
            lo = max(0, j) * 128
            hi = min(4, j + 3) * 128
            subs = []
            if j >= 0:
                subs.append((j * 128, "d"))
            if j + 2 <= 3:
                subs.append(((j + 2) * 128, "u"))
            chunks.append((k0, lo, hi - lo, subs))
    L = len(chunks)

    def pt_ap(j, ci, a, n):
        buf = Pt if ci % 2 == 0 else Pt2
        return buf[:, j * TW + a: j * TW + a + n]

    exp_sc = {}
    pv_t = {}
    for ci, (k0, qoff, N, subs) in enumerate(chunks):
        last = (ci == L - 1)
        lhsK = KG[rb:rb + 64, kcn * s + k0: kcn * s + k0 + 128]
        lhsV = ap_of(VG, (k0 // 128) * nblk + 65 * kv, [[vg_free, 128], [1, 65]])
        for j, (qc, half) in enumerate(pos):
            qb = 64 * half
            scb = pbh[sbanks[j]]
            rhsQ = QT[qb:qb + 64, qc * TW + qoff: qc * TW + qoff + N]
            if ci == 0:
                v = sbv.get(sbanks[j])
                if v is not None and (deep or j < 2):
                    kb.wait("tensor", "V", v)
            if deep:
                prev = (ci - 1, j)
            else:
                prev = (ci, j - 2) if j >= 2 else (ci - 1, j + 2)
            if prev in exp_sc:
                kb.wait("tensor", "SC", exp_sc[prev])
            kb.op("tensor", lambda e, lhsK=lhsK, rhsQ=rhsQ, qoff=qoff, N=N,
                  b=scb: e.matmul(b[0:128, qoff:qoff + N], lhsK, rhsQ,
                                  start=True, stop=True), inc="T")
            tt = kb.val("T")
            kb.wait("scalar", "T", tt)
            if (ci - 2, j) in pv_t:
                kb.wait("scalar", "T", pv_t[(ci - 2, j)])
            elif ci < 2 and (j, ci % 2) in lastpv:
                kb.wait("scalar", "T", lastpv[(j, ci % 2)])
            sc = kb.op("scalar", lambda e, qoff=qoff, N=N, b=scb, j=j, ci=ci:
                       e.activation(pt_ap(j, ci, qoff, N),
                                    b[0:128, qoff:qoff + N], AF.Exp,
                                    bias=c[:, 0:1], scale=0.125), inc="SC")
            exp_sc[(ci, j)] = sc
        for j, (qc, half) in enumerate(pos):
            pvb = pbh[2 + j]
            sc = exp_sc[(ci, j)]
            if subs:
                kb.wait("vector", "SC", sc)
                for (soff, kind) in subs:
                    mk = tb["mask_diag"] if kind == "d" else tb["mask_su"]
                    kb.op("vector", lambda e, soff=soff, mk=mk, j=j, ci=ci:
                          e.tensor_tensor(
                              pt_ap(j, ci, soff, 128),
                              pt_ap(j, ci, soff, 128),
                              mk[:, :], ALU.mult))
                mdone = kb.commit("vector", "V")
                kb.wait("tensor", "V", mdone)
            else:
                kb.wait("tensor", "SC", sc)
            if ci == 0:
                v = pvv.get(j)
                if v is not None:
                    kb.wait("tensor", "V", v)
            if glob:
                kb.op("tensor", lambda e, lhsV=lhsV, qoff=qoff, N=N, b=pvb,
                      j=j, ci=ci, f=(ci == 0), g=last: e.matmul(
                          b[0:65, qoff:qoff + N], lhsV,
                          pt_ap(j, ci, qoff, N),
                          start=f, stop=g), inc="T")
            else:
                nsub = N // 128
                for sbt in range(nsub):
                    so = qoff + sbt * 128
                    kb.op("tensor", lambda e, lhsV=lhsV, so=so, b=pvb, j=j,
                          ci=ci,
                          f=(ci == 0 and sbt == 0),
                          g=(last and sbt == nsub - 1): e.matmul(
                              b[0:65, so:so + 128], lhsV,
                              pt_ap(j, ci, so, 128),
                              start=f, stop=g),
                          inc=("T" if sbt == nsub - 1 else None))
            pv_t[(ci, j)] = kb.val("T")

    for j in range(4):
        lastpv[(j, (L - 1) % 2)] = pv_t[(L - 1, j)]
        if L >= 2:
            lastpv[(j, (L - 2) % 2)] = pv_t[(L - 2, j)]

    xstb = st["xst"]
    Cslots = [R64(st, 0), R64(st, 1), R64(st, 2), xstb[64:65, 2 * TW:3 * TW]]
    rws = st["rows"]
    Lslots = [rws[32:33, 0:TW], rws[32:33, TW:2 * TW], rws[32:33, 2 * TW:3 * TW],
              rws[96:97, 0:TW]]
    kb.wait("vector", "T", pv_t[(L - 1, 3)])
    for j in range(4):
        kb.op("vector", lambda e, j=j: e.tensor_copy(
            Cslots[j], pbh[2 + j][64:65, 0:TW]))
    cv = kb.commit("vector", "V")
    kb.wait("scalar", "V", cv)
    for j in range(4):
        kb.op("scalar", lambda e, j=j: e.activation(Lslots[j], Cslots[j], AF.Ln))
    kb.drain("scalar")
    sqv = {}
    for j in range(4):
        sqv[j] = kb.op("scalar", lambda e, j=j: e.activation(
            Cslots[j], Lslots[j], AF.Exp, bias=0.0, scale=-1.0), inc="SC")
    stgs = [st["xst"][0:64, 0:TW], st["xst"][0:64, TW:2 * TW]]
    ttv = None
    for pair in ((0, 1), (2, 3)):
        for j in pair:
            kb.wait("tensor", "SC", sqv[j])
            if ttv is not None and not deep and j == pair[0]:
                kb.wait("tensor", "V", ttv)
            lx = (L - 1, 2 + (j % 2)) if not deep else (L - 1, j)
            kb.wait("tensor", "SC", exp_sc[lx])
            kb.op("tensor", lambda e, j=j, b=sbanks[j]: e.matmul(
                pbh[b][0:64, 0:TW], tb["onesq"][64:65, 0:64],
                Cslots[j], start=True, stop=True), inc="T")
        tt = kb.val("T")
        kb.wait("vector", "T", tt)
        bufs = {pair[0]: xstb[0:64, 2 * TW:3 * TW],
                pair[1]: xstb[0:64, 3 * TW:4 * TW]}
        for j in pair:
            kb.op("vector", lambda e, j=j, b=bufs[j]: e.tensor_copy(
                b, pbh[sbanks[j]][0:64, 0:TW]))
        kb.drain("vector")
        for j in pair:
            qc, half = pos[j]
            if half == 0:
                dst = AO[0:64, qc * TW:(qc + 1) * TW]
                kb.op("vector", lambda e, j=j, b=bufs[j], dst=dst: e.tensor_tensor(
                    dst, pbh[2 + j][0:64, 0:TW], b, ALU.mult))
            else:
                stg = stgs[j % 2]
                ltok = st.setdefault("lift_tok_slot", {}).get(j % 2)
                if ltok is not None:
                    kb.wait_tok("vector", ltok)
                kb.op("vector", lambda e, j=j, b=bufs[j], stg=stg: e.tensor_tensor(
                    stg, pbh[2 + j][0:64, 0:TW], b, ALU.mult))
                vvl = kb.commit("vector", "V")
                kb.wait("gpsimd", "V", vvl)
                tok = kb.dma("gpsimd", AO[64:128, qc * TW:(qc + 1) * TW], stg, None)
                st.setdefault("ao_lift", []).append(tok)
                st["lift_tok_slot"][j % 2] = tok
        ttv = kb.commit("vector", "V")
        for j in pair:
            sbv[sbanks[j]] = ttv
            pvv[j] = ttv


def emit_v_proj(st, t0, nk, lhs_fn, rhs_fn):
    kb = st["kb"]
    pb, VG = st["pb"], st["VG"]
    nblk = 4 * 65
    vg_free = (st["s"] // 128) * nblk
    for tc4 in range(TW // 128):
        tca = (t0 // 128) + tc4
        kb.guard("tensor", "V")
        for kc in range(nk):
            kb.op("tensor", lambda e, kc=kc, tc4=tc4, f=(kc == 0),
                  g=(kc == nk - 1): e.matmul(
                st["pbh"][4][0:128, 0:256], lhs_fn(kc, tc4), rhs_fn(kc),
                start=f, stop=g),
                inc=("T" if kc == nk - 1 else None))
        tt = kb.val("T")
        kb.wait("vector", "T", tt)
        out_ap = ap_of(VG, tca * nblk, [[vg_free, 128], [65, 4], [1, 64]])
        in_ap = ap_of(st["pbh"][4], 0, [[512, 128], [64, 4], [1, 64]])
        kb.op("vector", lambda e, o=out_ap, i=in_ap: e.tensor_copy(o, i))
        kb.commit("vector", "V")
    ones_ap = ap_of(VG, (t0 // 128) * nblk + 64,
                    [[vg_free, 128], [nblk, TW // 128], [65, 4], [1, 1]])
    kb.op("vector", lambda e, o=ones_ap: e.memset(o, 1.0))
    kb.commit("vector", "V")


def emit_v_proj_local(st, t0, lwv):
    kb = st["kb"]
    wst, xst = st["wst"], st["xst"]
    nblk = 4 * 65
    vg_free = (st["s"] // 128) * nblk

    def acc_ap(tc4, dims):
        return ap_of(st["pbh"][4 + (tc4 // 2)], (tc4 % 2) * 256, dims)

    kb.guard("tensor", "V")
    for kc in range(NC):
        buf = wst[:, (kc % 2) * 1024:(kc % 2) * 1024 + LAT]
        wl = band_load(st, buf, lwv[kc * 128:(kc + 1) * 128, :], LAT)
        kb.wait_tok("tensor", wl)
        for tc4 in range(TW // 128):
            lhs = xst[:, kc * TW + tc4 * 128: kc * TW + (tc4 + 1) * 128]
            out = acc_ap(tc4, [[512, 128], [1, 256]])
            kb.op("tensor", lambda e, lhs=lhs, out=out, buf=buf, kc=kc,
                  f=(kc == 0 and tc4 % 2 == 0),
                  g=(kc == NC - 1 and tc4 % 2 == 1):
                  e.matmul(out, lhs, buf, start=f, stop=g),
                  inc=("T" if tc4 == TW // 128 - 1 else None))
    tt = kb.val("T")
    kb.wait("vector", "T", tt)
    for tc4 in range(TW // 128):
        tca = (t0 // 128) + tc4
        out_ap = ap_of(st["VG"], tca * nblk, [[vg_free, 128], [65, 4], [1, 64]])
        in_ap = acc_ap(tc4, [[512, 128], [64, 4], [1, 64]])
        kb.op("vector", lambda e, o=out_ap, i=in_ap: e.tensor_copy(o, i))
        kb.drain("vector")
    kb.commit("vector", "V")
    ones_ap = ap_of(st["VG"], (t0 // 128) * nblk + 64,
                    [[vg_free, 128], [nblk, TW // 128], [65, 4], [1, 1]])
    kb.op("vector", lambda e, o=ones_ap: e.memset(o, 1.0))
    kb.commit("vector", "V")


def emit_store_s(st):
    kb, s = st["kb"], st["s"]
    kb.guard("gpsimd", "V")
    kb.dma_group("gpsimd",
                 [(st["s_scr"][ch * 128:(ch + 1) * 128, :],
                   st["xn"][:, ch * s:(ch + 1) * s]) for ch in range(NC)], None)



def emit_wquant(st):
    kb = st["kb"]
    c, pb, pbh, tb, cs, mw = (st["consts"], st["pb"], st["pbh"], st["tb"],
                              st["cs"], st["mw"])
    wf, wq16, sgn2 = st["wf"], st["wq16"], st["sgn2"]
    tmp1, tmp2 = st["tmp1"], st["tmp2"]

    specs = [
        ("w_gate", st["wgq_scr"], INNER, 0, 0),
        ("w_up", st["wuq_scr"], INNER, 32, 1),
        ("w_down", st["wdq_scr"], D, 64, 2),
    ]
    for (wname, scr, nout, cs_off, mw_i) in specs:
        src = st["w"][wname]
        nin_ch = src.tensor.shape[0] // 128

        def wfs(ic, nout=nout):
            sl = ic % 2
            return wf[:, sl * INNER: sl * INNER + nout]

        toks = {}
        rv = {}
        kb.guard("vector", "SC")
        kb.guard("sync", "V", "SC")
        toks[0] = kb.dma("sync", wfs(0), src[0:128, :], None)
        for ic in range(nin_ch):
            if ic + 1 < nin_ch:
                if ic >= 1:
                    kb.wait("sync", "V", rv[ic - 1])
                toks[ic + 1] = kb.dma("sync", wfs(ic + 1),
                                      src[(ic + 1) * 128:(ic + 2) * 128, :], None)
            kb.wait_tok("vector", toks[ic])
            rv[ic] = kb.op("vector", lambda e, ic=ic, wfs=wfs: e.reduce_sum(
                tmp1[:, ic:ic + 1], wfs(ic), AX.X, apply_absolute_value=True),
                inc="V")
        kb.drain("vector")
        kb.op("vector", lambda e, n=nin_ch: e.reduce_sum(
            tmp2[:, 0:1], tmp1[:, 0:n], AX.X))
        va = kb.commit("vector", "V")
        kb.wait("tensor", "V", va)
        tt = kb.op("tensor", lambda e: e.matmul(
            pbh[5][:, 0:1], tb["onesq"][:, :], tmp2[:, 0:1], start=True, stop=True),
            inc="T")
        kb.wait("vector", "T", tt)
        kb.op("vector", lambda e, mw_i=mw_i: e.tensor_scalar(
            mw[:, mw_i:mw_i + 1], pbh[5][:, 0:1], 1.0 / (1 << 22), 1e-5,
            ALU.mult, ALU.max))
        kb.drain("vector")
        kb.op("vector", lambda e, mw_i=mw_i: e.tensor_scalar(
            tmp1[:, 0:1], mw[:, mw_i:mw_i + 1], 0.5, None, ALU.mult))
        kb.drain("vector")
        kb.op("vector", lambda e, mw_i=mw_i: e.tensor_scalar(
            tmp1[:, 1:2], mw[:, mw_i:mw_i + 1], -0.5, None, ALU.mult))
        vb = kb.commit("vector", "V")
        kb.wait("scalar", "V", vb)

        def wqs(ic, nout=nout):
            sl = ic % 2
            return wq16[:, sl * INNER: sl * INNER + nout]

        def sgs(ic, nout=nout):
            sl = ic % 2
            return sgn2[:, sl * INNER: sl * INNER + nout]

        slot_store = st.setdefault("wq_store_toks", {})
        colsum_t = {}
        toksB = {}
        sc_after = {}
        kb.guard("sync", "SC", "V", "T")
        toksB[0] = kb.dma("sync", wfs(0), src[0:128, :], None)
        for ic in range(nin_ch):
            sl = ic % 2
            if ic + 1 < nin_ch:
                if ic >= 1:
                    kb.wait("sync", "SC", sc_after[ic - 1])
                toksB[ic + 1] = kb.dma("sync", wfs(ic + 1),
                                       src[(ic + 1) * 128:(ic + 2) * 128, :], None)
            kb.wait_tok("scalar", toksB[ic])
            if sl in colsum_t:
                kb.wait("scalar", "T", colsum_t[sl])
            if slot_store.get(sl) is not None:
                kb.wait_tok("scalar", slot_store[sl])
            kb.op("scalar", lambda e, ic=ic, wqs=wqs, wfs=wfs: e.activation(
                wqs(ic), wfs(ic), AF.Sign, bias=tmp1[:, 1:2]))
            s2 = kb.op("scalar", lambda e, ic=ic, sgs=sgs, wfs=wfs: e.activation(
                sgs(ic), wfs(ic), AF.Sign, bias=tmp1[:, 0:1]), inc="SC")
            sc_after[ic] = s2
            kb.wait("vector", "SC", s2)
            kb.op("vector", lambda e, ic=ic, wqs=wqs, sgs=sgs: e.tensor_tensor(
                wqs(ic), wqs(ic), sgs(ic), ALU.add))
            vq = kb.commit("vector", "V")
            kb.wait("gpsimd", "V", vq)
            slot_store[sl] = kb.dma("gpsimd", scr[ic * 128:(ic + 1) * 128, :],
                                    wqs(ic), None)
            kb.wait("tensor", "V", vq)
            for os_ in range(nout // 128):
                o0 = sl * INNER + os_ * 128
                kb.op("tensor", lambda e, os_=os_, ic=ic, cs_off=cs_off, o0=o0,
                      f=(ic == 0 and os_ == 0),
                      g=(ic == nin_ch - 1 and os_ == nout // 128 - 1): e.matmul(
                          pbh[6][:, cs_off + os_: cs_off + os_ + 1],
                          wq16[:, o0:o0 + 128],
                          tb["ones16"][:, 0:1],
                          start=f, stop=g),
                      inc=("T" if os_ == nout // 128 - 1 else None))
            colsum_t[sl] = kb.val("T")
        tt = kb.val("T")
        kb.wait("vector", "T", tt)
        kb.op("vector", lambda e, cs_off=cs_off, n=nout // 128: e.tensor_scalar(
            cs[:, cs_off:cs_off + n], pbh[6][:, cs_off:cs_off + n], MAGIC, None,
            ALU.mult))
        kb.commit("vector", "V")



def bcast_row(st, row_ap, dst_sb):
    kb, tb = st["kb"], st["tb"]
    kb.guard("tensor", "V")
    tt = kb.op("tensor", lambda e: e.matmul(
        st["pbh"][7][:, :], tb["onesq"][0:1, :], row_ap, start=True, stop=True),
        inc="T")
    kb.wait("vector", "T", tt)
    kb.op("vector", lambda e: e.tensor_copy(dst_sb[:, :], st["pbh"][7][:, :]))
    return kb.commit("vector", "V")


def emit_amax(st, chunk_fn, nch, dst_row):
    kb = st["kb"]
    accs = [st["tmp1"], st["tmp2"], st["tmpsq"], st["tmp4"]]
    na = min(len(accs), max(1, nch // 2))
    if st.get("tmps_sc") is not None:
        kb.wait("vector", "SC", st["tmps_sc"])
    for i in range(na):
        kb.op("vector", lambda e, i=i: e.tensor_tensor(
            accs[i][:, :], chunk_fn(i), chunk_fn(na + i), ALU.max))
    kb.drain("vector")
    nxt = 2 * na
    while nxt < nch:
        for i in range(na):
            if nxt + i < nch:
                kb.op("vector", lambda e, i=i, j=nxt + i: e.tensor_tensor(
                    accs[i][:, :], accs[i][:, :], chunk_fn(j), ALU.max))
        kb.drain("vector")
        nxt += na
    done = 0
    while done < nch:
        for i in range(na):
            if done + i < nch:
                kb.op("vector", lambda e, i=i, j=done + i: e.scalar_tensor_tensor(
                    accs[i][:, :], chunk_fn(j), -1.0, accs[i][:, :],
                    ALU.mult, ALU.max))
        kb.drain("vector")
        done += na
    step = na // 2
    while step >= 1:
        for i in range(step):
            kb.op("vector", lambda e, i=i, s2=step: e.tensor_tensor(
                accs[i][:, :], accs[i][:, :], accs[i + s2][:, :], ALU.max))
        kb.drain("vector")
        step //= 2
    vv = kb.commit("vector", "V")
    kb.wait("gpsimd", "V", vv)
    kb.op("gpsimd", lambda e: e.reduce_max(R(st, 0), accs[0][:, :], AX.C))
    gg = kb.commit("gpsimd", "G")
    kb.wait("vector", "G", gg)
    kb.op("vector", lambda e, d=dst_row: e.tensor_scalar(
        d, R(st, 0), 1e-5, None, ALU.max))
    return kb.commit("vector", "V")


def emit_ffn(st):
    kb, s, nt = st["kb"], st["s"], st["nt"]
    pb, pbh, tb, c, cs, mw = (st["pb"], st["pbh"], st["tb"], st["consts"],
                              st["cs"], st["mw"])
    arena, z1, z2, prod, fw = st["arena"], st["z1"], st["z2"], st["prod"], st["fw"]
    Pt = st["Pt"]
    sxb, tgb, tub, tdb = st["sxb"], st["tgb"], st["tub"], st["tdb"]
    tmp1, tmp2, tmpsq = st["tmp1"], st["tmp2"], st["tmpsq"]
    tmps4 = [tmp1, tmp2, tmpsq, st["tmp4"]]
    xoc = st["xoc"]
    XNT = 0
    S0 = NC * TW
    bankv = st.setdefault("ffn_bankv", {})

    def bank_wait(banks):
        for b in banks:
            v = bankv.get(b)
            if v is not None:
                kb.wait("tensor", "V", v)

    def A(base, ch):
        return arena[:, base + ch * TW: base + (ch + 1) * TW]

    STT_ = NC * TW

    def emit_q1g(t):
        t0 = t * TW
        kb.guard("gpsimd", "V")
        pairs = []
        for ch in range(NC):
            pairs.append((A(XNT, ch),
                          st["xn_scr"][ch * 128:(ch + 1) * 128, t0:t0 + TW]))
        tl = kb.dma_group("gpsimd", pairs, None)
        kb.wait_tok("vector", tl)

        av = emit_amax(st, lambda ch: A(XNT, ch), NC, R(st, 2))
        kb.op("vector", lambda e: e.tensor_scalar(
            R(st, 0), R(st, 2), mw[0:1, 0:1], 1.0 / 254.0, ALU.mult, ALU.mult))
        kb.op("vector", lambda e: e.tensor_scalar(
            R(st, 1), R(st, 2), mw[0:1, 1:2], 1.0 / 254.0, ALU.mult, ALU.mult))
        vv2 = kb.commit("vector", "V")
        kb.wait("scalar", "V", vv2)
        kb.op("scalar", lambda e: e.activation(
            st["rows"][32:33, TW:2 * TW], R(st, 2), AF.Ln))
        kb.drain("scalar")
        sxv = kb.op("scalar", lambda e: e.activation(
            R(st, 2), st["rows"][32:33, TW:2 * TW], AF.Exp,
            bias=c[0:1, 3:4], scale=-1.0), inc="SC")
        kb.guard("tensor", "V")
        kb.wait("tensor", "V", vv2)
        t1 = kb.op("tensor", lambda e: e.matmul(
            pbh[7][:, :], tb["onesq"][0:1, :], R(st, 0), start=True, stop=True))
        t2 = kb.op("tensor", lambda e: e.matmul(
            pbh[6][:, :], tb["onesq"][0:1, :], R(st, 1), start=True, stop=True),
            inc="T")
        kb.wait("vector", "T", kb.val("T"))
        kb.op("vector", lambda e: e.tensor_copy(tgb[:, :], pbh[7][:, :]))
        kb.op("vector", lambda e: e.tensor_copy(tub[:, :], pbh[6][:, :]))
        cbv = kb.commit("vector", "V")
        bankv[6] = cbv
        kb.wait("tensor", "SC", sxv)
        kb.wait("tensor", "V", cbv)
        kb.op("tensor", lambda e: e.matmul(
            pbh[7][:, :], tb["onesq"][0:1, :], R(st, 2), start=True, stop=True),
            inc="T")
        kb.wait("vector", "T", kb.val("T"))
        kb.op("vector", lambda e: e.tensor_copy(sxb[:, :], pbh[7][:, :]))
        bankv[7] = kb.commit("vector", "V")

        prevsc = st.get("tmps_sc")
        for b0 in (0, 4):
            if prevsc is not None:
                kb.wait("vector", "SC", prevsc)
            for i in range(4):
                ch = b0 + i
                kb.op("vector", lambda e, ch=ch, i=i: e.tensor_tensor(
                    tmps4[i][:, :], A(XNT, ch), sxb[:, :], ALU.mult))
            zv = kb.commit("vector", "V")
            kb.wait("scalar", "V", zv)
            if b0 == 0 and st.get("ffn_gu_last_t") is not None:
                kb.wait("scalar", "T", st["ffn_gu_last_t"])
            for i in range(4):
                ch = b0 + i
                kb.op("scalar", lambda e, ch=ch, i=i: e.activation(
                    z1[:, ch * TW:(ch + 1) * TW], tmps4[i][:, :], AF.Relu,
                    bias=c[:, 4:5], scale=1.0), inc=("SC" if i == 3 else None))
            prevsc = kb.val("SC")
            st["tmps_sc"] = prevsc
        st["ffn_z1v"] = prevsc

    def gu_load(og, slot):
        return kb.dma_group("sync" if og % 2 == 0 else "gpsimd", [
            (ap_of(fw, slot * S0, [[3 * NC * TW, 128], [256, NC], [1, 256]]),
             ap_of(st["wgq_scr"].tensor, og * 256,
                   [[INNER, 128], [128 * INNER, NC], [1, 256]])),
            (ap_of(fw, slot * S0 + NC * 256,
                   [[3 * NC * TW, 128], [256, NC], [1, 256]]),
             ap_of(st["wuq_scr"].tensor, og * 256,
                   [[INNER, 128], [128 * INNER, NC], [1, 256]])),
        ], None)

    def emit_gu(t):
        kb.wait("tensor", "SC", st["ffn_z1v"])
        gu_t_after = {}
        gu_toks = {}
        kb.guard("sync", "T")
        kb.guard("gpsimd", "T")
        gu_toks[0] = gu_load(0, 0)
        for og in range(16):
            slot = og % 2
            bs = [0, 1, 2, 3] if og % 2 == 0 else [4, 5, 6, 7]
            if og + 1 < 16:
                if og >= 1:
                    kb.wait("sync" if (og + 1) % 2 == 0 else "gpsimd",
                            "T", gu_t_after[og - 1])
                gu_toks[og + 1] = gu_load(og + 1, (og + 1) % 2)
            kb.wait_tok("tensor", gu_toks[og])
            bank_wait(bs)
            for ic in range(NC):
                for oo in range(2):
                    off = slot * S0 + ic * 256 + oo * 128
                    kb.op("tensor", lambda e, off=off, oo=oo, ic=ic, bs=bs,
                          f=(ic == 0), g=(ic == NC - 1): e.matmul(
                        pb[bs[oo]], fw[:, off:off + 128],
                        z1[:, ic * TW:(ic + 1) * TW], start=f, stop=g))
                for oo in range(2):
                    off = slot * S0 + NC * 256 + ic * 256 + oo * 128
                    kb.op("tensor", lambda e, off=off, oo=oo, ic=ic, bs=bs,
                          f=(ic == 0), g=(ic == NC - 1): e.matmul(
                        pb[bs[2 + oo]], fw[:, off:off + 128],
                        z1[:, ic * TW:(ic + 1) * TW], start=f, stop=g),
                        inc=("T" if (ic == NC - 1 and oo == 1) else None))
            gu_t_after[og] = kb.val("T")
            tt = kb.val("T")
            kb.wait("vector", "T", tt)
            if st.get("tmps_sc") is not None:
                kb.wait("vector", "SC", st["tmps_sc"])
            for oo in range(2):
                oc = og * 2 + oo
                kb.op("vector", lambda e, oo=oo, oc=oc, bs=bs:
                      e.scalar_tensor_tensor(
                          tmps4[oo][:, :], pb[bs[oo]], cs[:, oc:oc + 1],
                          tgb[:, :], ALU.subtract, ALU.mult))
            cv = kb.commit("vector", "V")
            kb.wait("scalar", "V", cv)
            for oo in range(2):
                kb.op("scalar", lambda e, oo=oo: e.activation(
                    Pt[:, oo * TW:(oo + 1) * TW], tmps4[oo][:, :], AF.Silu),
                    inc=("SC" if oo == 1 else None))
            sg = kb.val("SC")
            st["tmps_sc"] = sg
            kb.wait("vector", "SC", sg)
            for oo in range(2):
                oc = og * 2 + oo
                kb.op("vector", lambda e, oo=oo, oc=oc, bs=bs:
                      e.scalar_tensor_tensor(
                          tmps4[2 + oo][:, :], pb[bs[2 + oo]],
                          cs[:, 32 + oc:33 + oc], tub[:, :],
                          ALU.subtract, ALU.mult))
            kb.drain("vector")
            for oo in range(2):
                oc = og * 2 + oo
                kb.op("vector", lambda e, oo=oo, oc=oc: e.tensor_tensor(
                    prod[:, oc * TW:(oc + 1) * TW], Pt[:, oo * TW:(oo + 1) * TW],
                    tmps4[2 + oo][:, :], ALU.mult))
            ev = kb.commit("vector", "V")
            for b in bs:
                bankv[b] = ev
        st["ffn_gu_last_t"] = kb.val("T")

    def emit_q2(t):
        av = emit_amax(st, lambda chh: prod[:, chh * TW:(chh + 1) * TW], 32,
                       R(st, 2))
        kb.op("vector", lambda e: e.tensor_scalar(
            R(st, 1), R(st, 2), mw[0:1, 2:3], 1.0 / 254.0, ALU.mult, ALU.mult))
        vv2 = kb.commit("vector", "V")
        kb.wait("scalar", "V", vv2)
        kb.op("scalar", lambda e: e.activation(
            st["rows"][32:33, TW:2 * TW], R(st, 2), AF.Ln))
        kb.drain("scalar")
        sxv = kb.op("scalar", lambda e: e.activation(
            R(st, 2), st["rows"][32:33, TW:2 * TW], AF.Exp,
            bias=c[0:1, 3:4], scale=-1.0), inc="SC")
        kb.guard("tensor", "V")
        kb.wait("tensor", "V", vv2)
        kb.op("tensor", lambda e: e.matmul(
            pbh[6][:, :], tb["onesq"][0:1, :], R(st, 1), start=True, stop=True))
        kb.wait("tensor", "SC", sxv)
        kb.op("tensor", lambda e: e.matmul(
            pbh[7][:, :], tb["onesq"][0:1, :], R(st, 2), start=True, stop=True),
            inc="T")
        kb.wait("vector", "T", kb.val("T"))
        kb.op("vector", lambda e: e.tensor_copy(tdb[:, :], pbh[6][:, :]))
        kb.op("vector", lambda e: e.tensor_copy(sxb[:, :], pbh[7][:, :]))
        cbv = kb.commit("vector", "V")
        bankv[6] = cbv
        bankv[7] = cbv
        prevsc = st.get("tmps_sc")
        for b0 in range(0, 32, 4):
            if prevsc is not None:
                kb.wait("vector", "SC", prevsc)
            for i in range(4):
                chh = b0 + i
                kb.op("vector", lambda e, chh=chh, i=i: e.tensor_tensor(
                    tmps4[i][:, :], prod[:, chh * TW:(chh + 1) * TW], sxb[:, :],
                    ALU.mult))
            zv = kb.commit("vector", "V")
            kb.wait("scalar", "V", zv)
            if b0 == 0 and st.get("ffn_dn_last_t") is not None:
                kb.wait("scalar", "T", st["ffn_dn_last_t"])
            for i in range(4):
                chh = b0 + i
                kb.op("scalar", lambda e, chh=chh, i=i: e.activation(
                    z2[:, chh * TW:(chh + 1) * TW], tmps4[i][:, :], AF.Relu,
                    bias=c[:, 4:5], scale=1.0), inc=("SC" if i == 3 else None))
            prevsc = kb.val("SC")
            st["tmps_sc"] = prevsc
        st["ffn_z2v"] = prevsc

    def dn_load(oc):
        return kb.dma("sync", ap_of(fw, 2 * S0,
                                    [[3 * NC * TW, 128], [128, 32], [1, 128]]),
                      ap_of(st["wdq_scr"].tensor, oc * 128,
                            [[D, 128], [128 * D, 32], [1, 128]]), None)

    def emit_dn(t):
        t0 = t * TW
        kb.guard("gpsimd", "V")
        stl = kb.dma_group("gpsimd", [
            (A(STT_, ch), st["s_scr"][ch * 128:(ch + 1) * 128, t0:t0 + TW])
            for ch in range(NC)], None)
        kb.wait("tensor", "SC", st["ffn_z2v"])
        dn_t_after = {}
        dn_toks = {}
        xtoks = {}
        xv_after = {}
        dn_ev = {}
        kb.guard("sync", "T")
        dn_toks[0] = dn_load(0)
        kb.guard("gpsimd", "V")
        xtoks[0] = kb.dma("gpsimd", xoc[:, 0:512],
                          st["xT"][0:128, t0:t0 + TW], None)
        for oc in range(NC):
            bank = 4 + oc % 2
            if oc + 1 < NC:
                if oc >= 1:
                    kb.wait("gpsimd", "V", xv_after[oc - 1])
                xtoks[oc + 1] = kb.dma(
                    "gpsimd",
                    xoc[:, ((oc + 1) % 2) * 512:((oc + 1) % 2) * 512 + 512],
                    st["xT"][(oc + 1) * 128:(oc + 2) * 128, t0:t0 + TW], None)
            kb.wait_tok("tensor", dn_toks[oc])
            bank_wait([bank])
            for ic in range(32):
                off = 2 * S0 + ic * 128
                kb.op("tensor", lambda e, ic=ic, off=off, bank=bank,
                      f=(ic == 0), g=(ic == 31): e.matmul(
                    pb[bank], fw[:, off:off + 128],
                    z2[:, ic * TW:(ic + 1) * TW], start=f, stop=g),
                    inc=("T" if ic == 31 else None))
            dn_t_after[oc] = kb.val("T")
            if oc + 1 < NC:
                kb.wait("sync", "T", dn_t_after[oc])
                dn_toks[oc + 1] = dn_load(oc + 1)
            tt = kb.val("T")
            kb.wait("vector", "T", tt)
            if st.get("tmps_sc") is not None:
                kb.wait("vector", "SC", st["tmps_sc"])
            kb.op("vector", lambda e, bank=bank, oc=oc: e.scalar_tensor_tensor(
                tmp1[:, :], pb[bank], cs[:, 64 + oc:65 + oc], tdb[:, :],
                ALU.subtract, ALU.mult))
            vv = kb.commit("vector", "V")
            kb.wait("scalar", "V", vv)
            sg = kb.op("scalar", lambda e: e.activation(
                tmpsq[:, :], tmp1[:, :], AF.Silu), inc="SC")
            st["tmps_sc"] = sg
            kb.wait("vector", "SC", sg)
            if oc == 0:
                kb.wait_tok("vector", stl)
            kb.op("vector", lambda e, oc=oc: e.tensor_tensor(
                tmp2[:, :], tmpsq[:, :], A(STT_, oc), ALU.mult))
            kb.drain("vector")
            kb.wait_tok("vector", xtoks[oc])
            kb.op("vector", lambda e, oc=oc: e.tensor_tensor(
                tmp2[:, :], tmp2[:, :],
                xoc[:, (oc % 2) * 512:(oc % 2) * 512 + 512], ALU.add))
            vv = kb.commit("vector", "V")
            xv_after[oc] = vv
            bankv[bank] = vv
            kb.wait("gpsimd", "V", vv)
            otok = kb.dma("gpsimd",
                          st["outT"][oc * 128:(oc + 1) * 128, t0:t0 + TW],
                          tmp2[:, :], None)
            kb.wait_tok("vector", otok)
        st["ffn_dn_last_t"] = kb.val("T")

    emit_q1g(0)
    emit_gu(0)
    for t in range(1, nt):
        emit_q1g(t)
        emit_q2(t - 1)
        emit_gu(t)
        emit_dn(t - 1)
    emit_q2(nt - 1)
    emit_dn(nt - 1)



def kernel(**inputs):
    ins = {k: np.asarray(v) for k, v in inputs.items()}
    x = ins["L0_attn_hidden"]
    xTs = [np.ascontiguousarray(x[b].T) for b in range(x.shape[0])]

    qp = _q_col_perm()
    kp = _k_col_perm()
    wop = _wo_row_perm()
    wmap = {
        "g_wq": ins["L0_attn_g_wq"][:, qp],
        "g_kv_down": ins["L0_attn_g_kv_down"],
        "g_k_up": ins["L0_attn_g_k_up"][:, kp],
        "g_v_up": ins["L0_attn_g_v_up"],
        "g_wo": ins["L0_attn_g_wo"][wop, :],
        "l_wq": ins["L0_attn_l_wq"][:, qp],
        "l_wk": ins["L0_attn_l_wk"][:, kp],
        "l_wv": ins["L0_attn_l_wv"],
        "l_wo": ins["L0_attn_l_wo"][wop, :],
        "w_gate": ins["L0_ffn_w_gate"].T,
        "w_up": ins["L0_ffn_w_up"].T,
        "w_down": ins["L0_ffn_w_down"].T,
        "norm1_w": np.ascontiguousarray(ins["L0_attn_norm1_w"].reshape(NC, 128).T),
        "norm2_w": np.ascontiguousarray(ins["L0_attn_norm2_w"].reshape(NC, 128).T),
    }
    wmap = {k: np.ascontiguousarray(np.asarray(v, dtype=np.float32))
            for k, v in wmap.items()}
    tmap = host_tables(S)

    prog = build_program()
    in_maps = []
    for b in range(x.shape[0]):
        m = {"xT": xTs[b]}
        m.update(wmap)
        m.update(tmap)
        in_maps.append(m)
    res = run_bass_kernel_spmd(prog, in_maps, core_ids=list(range(len(in_maps))))
    out = np.stack([np.ascontiguousarray(r["outT"].T) for r in res.results])
    return out


# version 14
# speedup vs baseline: 1.0513x; 1.0226x over previous
import numpy as np

import concourse.bass as bass
import concourse.mybir as mybir
from concourse.bass_utils import run_bass_kernel_spmd

F32 = mybir.dt.float32
F16 = mybir.dt.float16
BF16 = mybir.dt.bfloat16
AF = mybir.ActivationFunctionType
ALU = mybir.AluOpType
AX = mybir.AxisListType

D = 1024
S = 2048
B = 8
H = 16
KVH = 4
HD = 64
LAT = 256
WIN = 256
INNER = 4096
NC = D // 128
TW = 512
MAGIC = 1536.0

Q_CHUNK_HEADS = [(0, 4), (1, 5), (2, 6), (3, 7), (8, 12), (9, 13), (10, 14), (11, 15)]
ENGS = ("sync", "tensor", "vector", "scalar", "gpsimd")

DEBUG_STAGE = None
KEEP_SCOPES = False
SKIP_TABS = False
DEBUG_ROPE = None



def _rope_perm64():
    return np.concatenate([np.arange(0, 64, 2), np.arange(1, 64, 2)])


def _q_col_perm():
    rp = _rope_perm64()
    cols = []
    for (ha, hb) in Q_CHUNK_HEADS:
        cols.append(64 * ha + rp)
        cols.append(64 * hb + rp)
    return np.concatenate(cols)


def _k_col_perm():
    rp = _rope_perm64()
    return np.concatenate([64 * kv + rp for kv in range(KVH)])


def _wo_row_perm():
    rows = []
    for (ha, hb) in Q_CHUNK_HEADS:
        rows.append(64 * ha + np.arange(64))
        rows.append(64 * hb + np.arange(64))
    return np.concatenate(rows)


def _rope_tables(base, s):
    inv = np.power(float(base), -np.arange(0, 32, dtype=np.float64) / 32.0)
    th = np.arange(s, dtype=np.float64)[:, None] * inv[None, :]
    cos = np.cos(th).T
    sin = np.sin(th).T
    C = np.empty((128, s), dtype=np.float32)
    Sg = np.empty((128, s), dtype=np.float32)
    for blk in range(4):
        r = 32 * blk
        C[r:r + 32] = cos.astype(np.float32)
        Sg[r:r + 32] = (sin if blk % 2 else -sin).astype(np.float32)
    return C, Sg


def _swap_mat():
    P = np.zeros((128, 128), dtype=np.float32)
    for p in range(128):
        blk = (p // 64) * 64
        o = p - blk
        P[p, blk + (o + 32) % 64] = 1.0
    return P


def host_tables(s):
    kk = np.arange(128)[:, None]
    qq = np.arange(128)[None, :]
    Cg, Sgt = _rope_tables(1e6, s)
    Cl, Slt = _rope_tables(1e4, s)
    ind2 = np.zeros((128, 2), dtype=np.float32)
    ind2[:64, 0] = 1.0
    ind2[64:, 1] = 1.0
    t = {
        "Cg": Cg, "Sg": Sgt, "Cl": Cl, "Sl": Slt,
        "mask_diag": (kk <= qq).astype(np.float32),
        "mask_su": (kk > qq).astype(np.float32),
        "swap": _swap_mat(),
        "onesq": np.ones((128, 128), dtype=np.float32),
        "ind2": ind2,
        "ind2T": np.ascontiguousarray(ind2.T),
        "ones16": np.ones((128, 1), dtype=np.float16),
    }
    return {k: np.ascontiguousarray(v) for k, v in t.items()}



class KB:
    def __init__(self, nc):
        self.nc = nc
        self.steps = {e: [] for e in ENGS}
        self.counts = {}
        self.sems = {}
        self._ctx = []
        self._scopes = []

    def sem(self, name):
        cm = self.nc.semaphore(name)
        h = cm.__enter__()
        self._ctx.append(cm)
        self.sems[name] = h
        self.counts[name] = 0
        return h

    def sbuf(self, name, shape, dt, scoped=False):
        cm = self.nc.sbuf_tensor(name, shape, dt)
        h = cm.__enter__()
        (self._scopes[-1] if scoped else self._ctx).append(cm)
        return h

    def psum(self, name, shape, dt=F32):
        cm = self.nc.psum_tensor(name, shape, dt)
        h = cm.__enter__()
        self._ctx.append(cm)
        return h

    def push_scope(self):
        self._scopes.append([])

    def pop_scope(self):
        for cm in reversed(self._scopes.pop()):
            cm.__exit__(None, None, None)

    def close(self):
        while self._scopes:
            self.pop_scope()
        while self._ctx:
            self._ctx.pop().__exit__(None, None, None)

    def _push(self, eng, fn):
        self.steps[eng].append(fn)

    def wait(self, eng, sem_name, value):
        h = self.sems[sem_name]
        v = int(value)
        if v > 0:
            self._push(eng, lambda e: e.wait_ge(h, v))

    def guard(self, eng, *sem_names):
        for sn in sem_names:
            self.wait(eng, sn, self.counts[sn])

    def op(self, eng, make, inc=None):
        if inc is None:
            self._push(eng, lambda e: make(e))
            return None
        h = self.sems[inc]
        self.counts[inc] += 1
        self._push(eng, lambda e: make(e).then_inc(h, 1))
        return self.counts[inc]

    def drain(self, eng):
        self._push(eng, lambda e: e.drain())

    def commit(self, eng, sem_name):
        self.drain(eng)
        if eng in ("vector", "gpsimd"):
            return self.op(eng, lambda e: e.engine_nop(), inc=sem_name)
        return self.op(eng, lambda e: e.nop(), inc=sem_name)

    def dma(self, eng, out, in_, sem_name):
        return self.dma_group(eng, [(out, in_)], sem_name)

    def dma_group(self, eng, pairs, sem_name):
        pool = self.dma_pools[eng]
        idx = self.dma_rot[eng] % len(pool)
        self.dma_rot[eng] += 1
        sn = pool[idx]
        self.wait(eng, sn, self.counts[sn])
        h = self.sems[sn]
        for (out, in_) in pairs:
            self.counts[sn] += 16
            self._push(eng, lambda e, o=out, i=in_: e.dma_start(out=o, in_=i)
                       .then_inc(h, 16))
        return (sn, self.counts[sn])

    def wait_tok(self, eng, tok):
        self.wait(eng, tok[0], tok[1])

    def val(self, sem_name):
        return self.counts[sem_name]

    def barrier(self):
        for e in ENGS:
            for sn in list(self.counts):
                self.wait(e, sn, self.counts[sn])


def ap_of(handle, off, dims):
    return bass.AP(handle, off, dims)



def build_program():
    s = S
    nt = s // TW

    nc = bass.Bass("TRN2", target_bir_lowering=False)
    kb = KB(nc)

    xT = nc.dram_tensor("xT", [D, s], F32, kind="ExternalInput").ap()
    w = {}
    for name, shape in [
        ("g_wq", [D, D]), ("g_kv_down", [D, LAT]), ("g_k_up", [LAT, LAT]),
        ("g_v_up", [LAT, LAT]), ("g_wo", [D, D]),
        ("l_wq", [D, D]), ("l_wk", [D, LAT]), ("l_wv", [D, LAT]), ("l_wo", [D, D]),
        ("w_gate", [D, INNER]), ("w_up", [D, INNER]), ("w_down", [INNER, D]),
        ("norm1_w", [128, NC]), ("norm2_w", [128, NC]),
    ]:
        w[name] = nc.dram_tensor(name, shape, F32, kind="ExternalInput").ap()
    tab = {}
    for name, shape, dt in [
        ("Cg", [128, s], F32), ("Sg", [128, s], F32),
        ("Cl", [128, s], F32), ("Sl", [128, s], F32),
        ("mask_diag", [128, 128], F32), ("mask_su", [128, 128], F32),
        ("swap", [128, 128], F32), ("onesq", [128, 128], F32),
        ("ind2", [128, 2], F32), ("ind2T", [2, 128], F32),
        ("ones16", [128, 1], F16),
    ]:
        tab[name] = nc.dram_tensor(name, shape, dt, kind="ExternalInput").ap()
    outT = nc.dram_tensor("outT", [D, s], F32, kind="ExternalOutput").ap()
    dbgT = None
    if DEBUG_STAGE is not None:
        dbgT = nc.dram_tensor("dbgT", [D, s], F32, kind="ExternalOutput").ap()

    xn_scr = nc.dram_tensor("xn_scr", [D, s], F32).ap()
    s_scr = nc.dram_tensor("s_scr", [D, s], F32).ap()
    wgq_scr = nc.dram_tensor("wgq_scr", [D, INNER], F16).ap()
    wuq_scr = nc.dram_tensor("wuq_scr", [D, INNER], F16).ap()
    wdq_scr = nc.dram_tensor("wdq_scr", [INNER, D], F16).ap()

    for sn in ("T", "V", "SC", "G"):
        kb.sem(sn)
    kb.dma_pools = {"sync": [], "scalar": [], "gpsimd": []}
    kb.dma_rot = {"sync": 0, "scalar": 0, "gpsimd": 0}
    for q, pref in (("sync", "LW"), ("scalar", "LS"), ("gpsimd", "LG")):
        for i in range(4):
            sn = f"{pref}{i}"
            kb.sem(sn)
            kb.dma_pools[q].append(sn)

    consts = kb.sbuf("consts", [128, 16], F32)
    rows = kb.sbuf("rows", [128, 3 * TW], F32)
    tmp1 = kb.sbuf("tmp1", [128, TW], F32)
    tmp2 = kb.sbuf("tmp2", [128, TW], F32)
    tmpsq = kb.sbuf("tmpsq", [128, TW], F32)
    Pt = kb.sbuf("Pt", [128, 4 * TW], F32)
    cs = kb.sbuf("cs_sb", [128, 72], F32)
    mw = kb.sbuf("mw_sb", [128, 4], F32)
    tb = {}
    for k in ("mask_diag", "mask_su", "swap", "onesq", "ind2", "ind2T"):
        tb[k] = kb.sbuf("tb_" + k, list(tab[k].tensor.shape), F32)
    tb["ones16"] = kb.sbuf("tb_ones16", [128, 1], F16)

    pbh = [kb.psum(f"pb{i}", [128, 512]) for i in range(8)]
    pb = [t[:, :] for t in pbh]

    st = dict(nc=nc, kb=kb, s=s, nt=nt, w=w, tab=tab, tb=tb,
              consts=consts, rows=rows, tmp1=tmp1, tmp2=tmp2, tmpsq=tmpsq,
              Pt=Pt, cs=cs, mw=mw, pb=pb, pbh=pbh,
              xT=xT, outT=outT, dbgT=dbgT,
              xn_scr=xn_scr, s_scr=s_scr,
              wgq_scr=wgq_scr, wuq_scr=wuq_scr, wdq_scr=wdq_scr)

    emit_all(st)

    blk = nc.Block()
    block = blk.__enter__()
    kb._ctx.append(blk)
    for eng_name in ENGS:
        fns = kb.steps[eng_name]

        def body(e, fns=fns):
            for f in fns:
                f(e)
        getattr(block, eng_name)(body)
    kb.close()
    return nc


def R(st, k, n=TW):
    return st["rows"][0:1, k * TW:k * TW + n]


def R2(st, k, n=TW):
    return st["rows"][0:2, k * TW:k * TW + n]


def R64(st, k, n=TW):
    return st["rows"][64:65, k * TW:k * TW + n]


def R96(st, k, n=TW):
    return st["rows"][96:97, k * TW:k * TW + n]


def emit_all(st):
    kb = st["kb"]

    st["tabs_tok"] = kb.dma_group(
        "sync", [(st["tb"][k][:, :], st["tab"][k][:, :])
                 for k in ("mask_diag", "mask_su", "swap", "onesq",
                           "ind2", "ind2T", "ones16")], None)

    c = st["consts"]
    kb.op("vector", lambda e: e.memset(c[:, 0:1], -8.0))
    kb.op("vector", lambda e: e.memset(c[:, 1:2], 1e-5))
    kb.op("vector", lambda e: e.memset(c[:, 2:3], 1e-6))
    kb.op("vector", lambda e: e.memset(c[:, 3:4], float(np.log(127.0))))
    kb.op("vector", lambda e: e.memset(c[:, 4:5], MAGIC))
    kb.commit("vector", "V")
    st["consts_done"] = kb.val("V")

    kb.push_scope()
    st["xn"] = kb.sbuf("xn", [128, NC * st["s"]], F32, scoped=True)
    st["xst"] = kb.sbuf("xst", [128, NC * TW], F32, scoped=True)
    st["KGh"] = kb.sbuf("KGh", [128, 2 * st["s"]], BF16, scoped=True)
    st["KGl"] = kb.sbuf("KGl", [128, 2 * st["s"]], BF16, scoped=True)
    st["VG"] = kb.sbuf("VG", [128, (st["s"] // 128) * 4 * 65], F32, scoped=True)
    st["QTh"] = kb.sbuf("QTh", [128, NC * TW], BF16, scoped=True)
    st["QTl"] = kb.sbuf("QTl", [128, NC * TW], BF16, scoped=True)
    st["AO"] = kb.sbuf("AO", [128, NC * TW], F32, scoped=True)
    st["lat"] = kb.sbuf("lat", [128, 2 * st["s"]], F32, scoped=True)
    st["wst"] = kb.sbuf("wst", [128, 2 * 1024], F32, scoped=True)
    st["CSc"] = kb.sbuf("CSc", [128, st["s"]], F32, scoped=True)
    st["CSs"] = kb.sbuf("CSs", [128, st["s"]], F32, scoped=True)

    emit_ln(st)
    if DEBUG_STAGE == "xn_plus":
        rms_scale_bcast(st, 0)
        emit_debug_xn(st, st["xn"])
        emit_tail(st)
        return
    if DEBUG_STAGE == "xn_barrier":
        kb.barrier()
        emit_debug_xn(st, st["xn"])
        emit_tail(st)
        return
    if DEBUG_STAGE == "wq":
        kb.barrier()
        if not KEEP_SCOPES:
            kb.pop_scope()
        kb.push_scope()
        st["wf"] = kb.sbuf("wf", [128, 2 * INNER], F32, scoped=True)
        st["wq16"] = kb.sbuf("wq16", [128, 2 * INNER], F16, scoped=True)
        st["sgn2"] = kb.sbuf("sgn2", [128, 2 * INNER], F16, scoped=True)
        emit_wquant(st)
        kb.barrier()
        kb.guard("gpsimd", "V", "T", "SC")
        kb.dma("gpsimd", st["dbgT"][0:128, 0:72], st["cs"][:, :], None)
        kb.dma("gpsimd", st["outT"][0:128, 0:72], st["cs"][:, :], None)
        emit_tail(st)
        return
    if DEBUG_STAGE == "xn":
        emit_debug_xn(st, st["xn"])
        emit_tail(st)
        return
    emit_attn(st, glob=True)
    if DEBUG_STAGE in ("q0", "sdpa1"):
        kb.barrier()
        kb.guard("gpsimd", "V", "T", "SC")
        if DEBUG_STAGE == "q0":
            pairs = [(st["outT"][ch * 128:(ch + 1) * 128, 0:TW],
                      st["QTh"][:, ch * TW:(ch + 1) * TW]) for ch in range(NC)]
        else:
            pairs = [(st["outT"][0:128, 0:TW], st["AO"][:, 0:TW])]
        kb.dma_group("gpsimd", pairs, None)
        emit_tail(st)
        return
    if DEBUG_STAGE in ("norm", "norm_nodma", "rms0", "rmsA", "rmsB"):
        kb.barrier()
        emit_debug_xn(st, st["xn"])
        emit_tail(st)
        return
    if DEBUG_STAGE in ("lat", "k"):
        kb.barrier()
        kb.guard("gpsimd", "V", "T", "SC")
        if DEBUG_STAGE == "lat":
            pairs = [(st["outT"][0:128, :], st["lat"][:, 0:st["s"]]),
                     (st["outT"][128:256, :], st["lat"][:, st["s"]:2 * st["s"]])]
        else:
            pairs = [(st["outT"][0:128, :], st["KGh"][:, 0:st["s"]]),
                     (st["outT"][128:256, :], st["KGh"][:, st["s"]:2 * st["s"]])]
        kb.dma_group("gpsimd", pairs, None)
        emit_tail(st)
        return
    if DEBUG_STAGE in ("kv", "q0"):
        kb.barrier()
        kb.guard("gpsimd", "V", "T", "SC")
        if DEBUG_STAGE == "kv":
            pairs = [(st["outT"][0:128, :], st["KGh"][:, 0:st["s"]]),
                     (st["outT"][128:256, :], st["KGh"][:, st["s"]:2 * st["s"]])]
            vgf = (st["s"] // 128) * 260
            off = 0
            row = 256
            while off < vgf:
                wdt = min(st["s"], vgf - off)
                pairs.append((st["outT"][row:row + 128, 0:wdt],
                              st["VG"][:, off:off + wdt]))
                off += wdt
                row += 128
        else:
            pairs = [(st["outT"][ch * 128:(ch + 1) * 128, 0:TW],
                      st["QTh"][:, ch * TW:(ch + 1) * TW]) for ch in range(NC)]
        kb.dma_group("gpsimd", pairs, None)
        emit_tail(st)
        return
    if DEBUG_STAGE == "s1":
        emit_debug_xn(st, st["xn"])
        emit_tail(st)
        return
    emit_attn(st, glob=False)
    emit_store_s(st)
    if DEBUG_STAGE == "s":
        emit_debug_xn(st, st["xn"])
        emit_tail(st)
        return
    kb.barrier()
    if not KEEP_SCOPES:
        kb.pop_scope()

    kb.push_scope()
    st["wf"] = kb.sbuf("wf", [128, 2 * INNER], F32, scoped=True)
    st["wq16"] = kb.sbuf("wq16", [128, 2 * INNER], F16, scoped=True)
    st["sgn2"] = kb.sbuf("sgn2", [128, 2 * INNER], F16, scoped=True)
    emit_wquant(st)
    kb.barrier()
    if not KEEP_SCOPES:
        kb.pop_scope()

    kb.push_scope()
    st["arena"] = kb.sbuf("arena", [128, 2 * NC * TW], F32, scoped=True)
    st["xoc"] = kb.sbuf("xoc", [128, 2 * 512], F32, scoped=True)
    st["z1"] = kb.sbuf("z1", [128, NC * TW], F16, scoped=True)
    st["z2"] = kb.sbuf("z2", [128, 32 * TW], F16, scoped=True)
    st["prod"] = kb.sbuf("prod", [128, 32 * TW], F32, scoped=True)
    st["fw"] = kb.sbuf("fw", [128, 3 * NC * TW], F16, scoped=True)
    st["sxb"] = kb.sbuf("sxb", [128, TW], F32, scoped=True)
    st["tgb"] = kb.sbuf("tgb", [128, TW], F32, scoped=True)
    st["tub"] = kb.sbuf("tub", [128, TW], F32, scoped=True)
    st["tdb"] = kb.sbuf("tdb", [128, TW], F32, scoped=True)
    st["tmp4"] = kb.sbuf("tmp4", [128, TW], F32, scoped=True)
    emit_ffn(st)
    kb.pop_scope()

    emit_tail(st)


def emit_tail(st):
    kb = st["kb"]
    for q in ("sync", "scalar", "gpsimd"):
        for sn in kb.dma_pools[q]:
            kb.wait(q, sn, kb.val(sn))


def emit_debug_xn(st, src):
    kb, s = st["kb"], st["s"]
    kb.guard("gpsimd", "V", "T", "SC")
    pairs = []
    for ch in range(NC):
        pairs.append((st["outT"][ch * 128:(ch + 1) * 128, :],
                      src[:, ch * s:(ch + 1) * s]))
        pairs.append((st["dbgT"][ch * 128:(ch + 1) * 128, :],
                      src[:, ch * s:(ch + 1) * s]))
    kb.dma_group("gpsimd", pairs, None)



def emit_ln(st):
    kb, s, nt = st["kb"], st["s"], st["nt"]
    xn, xst, tmpsq, c, pb, tb = (st["xn"], st["xst"], st["tmpsq"], st["consts"],
                                 st["pb"], st["tb"])

    kb.wait_tok("scalar", st["tabs_tok"])
    kb.wait_tok("tensor", st["tabs_tok"])
    kb.wait("vector", "V", st["consts_done"])

    for t in range(nt):
        t0 = t * TW
        kb.guard("sync", "T", "V", "SC")
        xl = kb.dma_group(
            "sync", [(xst[:, ch * TW:(ch + 1) * TW],
                      st["xT"][ch * 128:(ch + 1) * 128, t0:t0 + TW])
                     for ch in range(NC)], None)
        kb.wait_tok("scalar", xl)
        kb.wait_tok("tensor", xl)

        kb.guard("scalar", "T")
        tvals = {}
        for ch in range(NC):
            xc = xst[:, ch * TW:(ch + 1) * TW]
            buf = tmpsq if ch % 2 == 0 else st["tmp2"]
            if ch >= 2:
                kb.wait("scalar", "T", tvals[ch - 2])
            sq = kb.op("scalar", lambda e, xc=xc, buf=buf: e.activation(
                buf[:, :], xc, AF.Square), inc="SC")
            kb.op("tensor", lambda e, xc=xc, f=(ch == 0), g=(ch == NC - 1):
                  e.matmul(pb[0][0:1, :], tb["onesq"][:, 0:1], xc,
                           start=f, stop=g))
            kb.wait("tensor", "SC", sq)
            tvals[ch] = kb.op("tensor", lambda e, buf=buf, f=(ch == 0),
                              g=(ch == NC - 1): e.matmul(
                pb[1][0:1, :], tb["onesq"][:, 0:1], buf[:, :],
                start=f, stop=g), inc="T")
        red = kb.val("T")

        kb.wait("vector", "T", red)
        kb.op("vector", lambda e: e.tensor_scalar(
            R(st, 0), pb[0][0:1, :], 1.0 / D, None, ALU.mult))
        kb.drain("vector")
        kb.op("vector", lambda e: e.tensor_tensor(
            R(st, 1), R(st, 0), R(st, 0), ALU.mult))
        kb.drain("vector")
        vv = kb.commit("vector", "V")
        kb.op("vector", lambda e: e.scalar_tensor_tensor(
            R(st, 1), pb[1][0:1, :], 1.0 / D, R(st, 1), ALU.mult, ALU.subtract))
        vv = kb.commit("vector", "V")
        kb.wait("scalar", "V", vv)
        kb.op("scalar", lambda e: e.activation(
            R(st, 2), R(st, 1), AF.Ln, bias=c[0:1, 1:2]))
        kb.drain("scalar")
        sq2 = kb.op("scalar", lambda e: e.activation(
            R(st, 1), R(st, 2), AF.Exp, bias=0.0, scale=-0.5), inc="SC")
        kb.wait("vector", "SC", sq2)
        kb.op("vector", lambda e: e.tensor_tensor(
            R(st, 2), R(st, 0), R(st, 1), ALU.mult))
        vv = kb.commit("vector", "V")

        kb.wait("tensor", "V", vv)
        kb.op("tensor", lambda e: e.matmul(
            pb[2], tb["onesq"][0:1, :], R(st, 1), start=True, stop=True))
        tbc = kb.op("tensor", lambda e: e.matmul(
            pb[3], tb["onesq"][0:1, :], R(st, 2), start=True, stop=True), inc="T")

        kb.wait("vector", "T", tbc)
        for ch in range(NC):
            xc = xst[:, ch * TW:(ch + 1) * TW]
            xnc = xn[:, ch * s + t0: ch * s + t0 + TW]
            kb.op("vector", lambda e, xc=xc, xnc=xnc: e.tensor_tensor(
                xnc, xc, pb[2], ALU.mult))
        kb.drain("vector")
        for ch in range(NC):
            xnc = xn[:, ch * s + t0: ch * s + t0 + TW]
            kb.op("vector", lambda e, xnc=xnc: e.tensor_tensor(
                xnc, xnc, pb[3], ALU.subtract))
        vv = kb.commit("vector", "V")
        kb.wait("gpsimd", "V", vv)
        kb.dma_group("gpsimd",
                     [(st["xn_scr"][ch * 128:(ch + 1) * 128, t0:t0 + TW],
                       xn[:, ch * s + t0: ch * s + t0 + TW])
                      for ch in range(NC)], None)
        kb.wait("tensor", "V", vv)



def rms_scale_bcast(st, t0):
    kb, s = st["kb"], st["s"]
    pb, tb, tmpsq, c = st["pb"], st["tb"], st["tmpsq"], st["consts"]
    kb.guard("scalar", "T")
    tvals = {}
    for ch in range(NC):
        xc = st["xn"][:, ch * s + t0: ch * s + t0 + TW]
        buf = tmpsq if ch % 2 == 0 else st["tmp2"]
        if ch >= 2:
            kb.wait("scalar", "T", tvals[ch - 2])
        sq = kb.op("scalar", lambda e, xc=xc, buf=buf: e.activation(
            buf[:, :], xc, AF.Square), inc="SC")
        kb.wait("tensor", "SC", sq)
        tvals[ch] = kb.op("tensor", lambda e, buf=buf, f=(ch == 0),
                          g=(ch == NC - 1): e.matmul(
            pb[0][0:1, :], tb["onesq"][:, 0:1], buf[:, :], start=f, stop=g),
            inc="T")
    red = kb.val("T")
    if DEBUG_STAGE == "rmsA":
        return red
    kb.wait("vector", "T", red)
    kb.op("vector", lambda e: e.tensor_scalar(
        R(st, 0), pb[0][0:1, :], 1.0 / D, None, ALU.mult))
    vv = kb.commit("vector", "V")
    kb.wait("scalar", "V", vv)
    kb.op("scalar", lambda e: e.activation(
        R(st, 2), R(st, 0), AF.Ln, bias=c[0:1, 2:3]))
    kb.drain("scalar")
    sq2 = kb.op("scalar", lambda e: e.activation(
        R(st, 0), R(st, 2), AF.Exp, bias=0.0, scale=-0.5), inc="SC")
    if DEBUG_STAGE == "rmsB":
        return kb.val("T")
    kb.wait("tensor", "SC", sq2)
    return kb.op("tensor", lambda e: e.matmul(
        pb[1], tb["onesq"][0:1, :], R(st, 0), start=True, stop=True), inc="T")


def norm_tile(st, t0, dst):
    kb, s = st["kb"], st["s"]
    c = st["consts"]
    for ch in range(NC):
        src_c = st["xn"][:, ch * s + t0: ch * s + t0 + TW]
        dst_c = dst[:, ch * TW:(ch + 1) * TW]
        kb.op("vector", lambda e, a=src_c, b=dst_c, ch=ch: e.scalar_tensor_tensor(
            b, a, c[:, 8 + ch:9 + ch], st["pb"][1], ALU.mult, ALU.mult))
    return kb.commit("vector", "V")


def rope_qknorm(st, src_psum, t0, out_region, ncols=TW, deps=None,
                split_out=None):
    kb = st["kb"]
    kb.op("vector", lambda e, o=out_region: e.tensor_copy(o, src_psum))
    vv = kb.commit("vector", "V")
    if deps is not None:
        deps = dict(deps, chunk_v=vv)
    return rope_qknorm_sb(st, t0, out_region, ncols, deps=deps,
                          split_out=split_out)


def rope_qknorm_sb(st, t0, chunk, ncols=TW, deps=None, split_out=None):
    kb = st["kb"]
    pb, tb, c = st["pb"], st["tb"], st["consts"]
    tmp1, tmp2, tmpsq = st["tmp1"], st["tmp2"], st["tmpsq"]
    CSc, CSs = st["CSc"], st["CSs"]

    vv = kb.val("V")
    if deps is None:
        kb.guard("scalar", "T", "V")
    else:
        if deps.get("t_ind2") is not None:
            kb.wait("scalar", "T", deps["t_ind2"])
        kb.wait("scalar", "V", deps["chunk_v"])
    sq = kb.op("scalar", lambda e, ch=chunk: e.activation(
        tmpsq[:, 0:ncols], ch, AF.Square), inc="SC")
    if deps is not None and deps.get("v_final") is not None:
        kb.wait("tensor", "V", deps["v_final"])
    kb.wait("tensor", "V", vv if deps is None else deps["chunk_v"])
    kb.op("tensor", lambda e, ch=chunk: e.matmul(
        pb[6][:, 0:ncols], tb["swap"][:, :], ch, start=True, stop=True))
    kb.wait("tensor", "SC", sq)
    tt = kb.op("tensor", lambda e: e.matmul(
        pb[7][0:2, 0:ncols], tb["ind2"][:, :], tmpsq[:, 0:ncols],
        start=True, stop=True), inc="T")
    kb.wait("vector", "T", tt)
    kb.op("vector", lambda e: e.tensor_scalar(
        R2(st, 1, ncols), pb[7][0:2, 0:ncols], 1.0 / HD, None, ALU.mult))
    vv = kb.commit("vector", "V")
    kb.wait("scalar", "V", vv)
    kb.op("scalar", lambda e: e.activation(
        R2(st, 2, ncols), R2(st, 1, ncols), AF.Ln, bias=c[0:2, 2:3]))
    kb.drain("scalar")
    sq2 = kb.op("scalar", lambda e: e.activation(
        R2(st, 1, ncols), R2(st, 2, ncols), AF.Exp, bias=0.0, scale=-0.5),
        inc="SC")
    kb.wait("tensor", "SC", sq2)
    tt = kb.op("tensor", lambda e: e.matmul(
        pb[7][:, 0:ncols], tb["ind2T"][0:2, :], R2(st, 1, ncols),
        start=True, stop=True), inc="T")
    Ct = CSc[:, t0:t0 + ncols]
    Stt = CSs[:, t0:t0 + ncols]
    kb.op("vector", lambda e, ch=chunk, Ct=Ct: e.tensor_tensor(
        tmp1[:, 0:ncols], ch, Ct, ALU.mult))
    kb.wait("vector", "T", tt)
    kb.op("vector", lambda e, Stt=Stt: e.tensor_tensor(
        tmp2[:, 0:ncols], pb[6][:, 0:ncols], Stt, ALU.mult))
    kb.drain("vector")
    kb.op("vector", lambda e: e.tensor_tensor(
        tmp1[:, 0:ncols], tmp1[:, 0:ncols], tmp2[:, 0:ncols], ALU.add))
    kb.drain("vector")
    kb.op("vector", lambda e, ch=chunk: e.tensor_tensor(
        ch, tmp1[:, 0:ncols], pb[7][:, 0:ncols], ALU.mult))
    vfin = kb.commit("vector", "V")
    if split_out is not None:
        hi, lo = split_out
        kb.op("vector", lambda e, ch=chunk, hi=hi: e.tensor_copy(hi, ch))
        kb.drain("vector")
        kb.op("vector", lambda e, ch=chunk, hi=hi, lo=lo: e.tensor_tensor(
            lo, ch, hi, ALU.subtract))
        spv = kb.commit("vector", "V")
        return {"t_ind2": tt, "v_final": vfin, "split_v": spv,
                "chunk_v": (deps["chunk_v"] if deps is not None else vfin)}
    return {"t_ind2": tt, "v_final": vfin,
            "chunk_v": (deps["chunk_v"] if deps is not None else vfin)}


def band_load(st, dst, src, n=0):
    kb = st["kb"]
    kb.guard("sync", "T")
    return kb.dma("sync", dst, src, None)


def emit_attn(st, glob):
    kb, s, nt = st["kb"], st["s"], st["nt"]
    w, tb, c, pb = st["w"], st["tb"], st["consts"], st["pb"]
    xn, VG, AO, lat, wst, xst = (st["xn"], st["VG"], st["AO"], st["lat"],
                                 st["wst"], st["xst"])
    KGh, KGl, QTh, QTl = st["KGh"], st["KGl"], st["QTh"], st["QTl"]

    wq = w["g_wq" if glob else "l_wq"]
    wo = w["g_wo" if glob else "l_wo"]
    nw = w["norm1_w" if glob else "norm2_w"]

    kb.guard("sync", "V")
    tl = kb.dma_group("sync", [
        (st["CSc"][:, :], st["tab"]["Cg" if glob else "Cl"][:, :]),
        (st["CSs"][:, :], st["tab"]["Sg" if glob else "Sl"][:, :]),
        (ap_of(st["consts"], 8, [[16, 128], [1, NC]]), nw[:, :]),
    ], None)
    for e in ("vector", "tensor", "scalar"):
        kb.wait_tok(e, tl)

    if DEBUG_STAGE in ("xn_pre", "xn_pre2", "xn_pre3", "xn_pre4", "xn_pre5"):
        return
    for t in range(nt):
        t0 = t * TW
        bc = rms_scale_bcast(st, t0)
        kb.wait("vector", "T", bc)
        kb.guard("vector", "T")
        if DEBUG_STAGE in ("rms0", "rmsA", "rmsB"):
            break
        nt_done = norm_tile(st, t0, xst)
        if DEBUG_STAGE in ("norm", "norm_nodma"):
            break

        if glob:
            kv_down = w["g_kv_down"]
            kup = w["g_k_up"]
            vup = w["g_v_up"]
            kb.wait("tensor", "V", nt_done)
            toksA = {}
            tA = {}
            kb.guard("sync", "T")
            toksA[0] = kb.dma("sync", wst[:, 0:LAT], kv_down[0:128, :], None)
            for kc in range(NC):
                if kc + 1 < NC:
                    if kc >= 1:
                        kb.wait("sync", "T", tA[kc - 1])
                    toksA[kc + 1] = kb.dma(
                        "sync",
                        wst[:, ((kc + 1) % 2) * 1024:((kc + 1) % 2) * 1024 + LAT],
                        kv_down[(kc + 1) * 128:(kc + 2) * 128, :], None)
                kb.wait_tok("tensor", toksA[kc])
                buf = wst[:, (kc % 2) * 1024:(kc % 2) * 1024 + LAT]
                for lc in range(2):
                    kb.op("tensor", lambda e, buf=buf, lc=lc, kc=kc, f=(kc == 0),
                          g=(kc == NC - 1):
                          e.matmul(pb[4 + lc],
                                   buf[:, lc * 128:(lc + 1) * 128],
                                   xst[:, kc * TW:(kc + 1) * TW],
                                   start=f, stop=g),
                          inc=("T" if lc == 1 else None))
                tA[kc] = kb.val("T")
            tt = kb.val("T")
            kb.wait("sync", "T", tt)
            wlk = kb.dma_group("sync", [
                (wst[:, 0:LAT], kup[0:128, :]),
                (wst[:, 1024:1024 + LAT], kup[128:256, :]),
                (wst[:, 256:256 + LAT], vup[0:128, :]),
                (wst[:, 1280:1280 + LAT], vup[128:256, :])], None)
            kb.wait("vector", "T", tt)
            kb.guard("vector", "T")
            for lc in range(2):
                kb.op("vector", lambda e, lc=lc, t0=t0: e.tensor_copy(
                    lat[:, lc * s + t0: lc * s + t0 + TW], pb[4 + lc]))
            lat_done = kb.commit("vector", "V")
            if DEBUG_STAGE == "lat":
                continue

            kb.wait_tok("tensor", wlk)
            kb.wait("tensor", "V", lat_done)
            for oc in range(2):
                for lc in range(2):
                    kb.op("tensor", lambda e, oc=oc, lc=lc, t0=t0, f=(lc == 0),
                          g=(lc == 1): e.matmul(
                        pb[4 + oc],
                        wst[:, lc * 1024 + oc * 128: lc * 1024 + (oc + 1) * 128],
                        lat[:, lc * s + t0: lc * s + t0 + TW], start=f, stop=g),
                        inc=("T" if lc == 1 else None))
            kb.wait("vector", "T", kb.val("T"))
            kdeps = {"t_ind2": None, "v_final": None, "chunk_v": None}
            for oc in range(2):
                kdeps = rope_qknorm(
                    st, pb[4 + oc], t0, AO[:, oc * TW:(oc + 1) * TW],
                    deps=kdeps,
                    split_out=(KGh[:, oc * s + t0: oc * s + t0 + TW],
                               KGl[:, oc * s + t0: oc * s + t0 + TW]))
            kb.wait("tensor", "V", kb.val("V"))

            if DEBUG_STAGE == "k":
                continue
            emit_v_proj(st, t0, nk=2,
                        lhs_fn=lambda kc, tc4, t0=t0: lat[:, kc * s + t0 + tc4 * 128:
                                                          kc * s + t0 + (tc4 + 1) * 128],
                        rhs_fn=lambda kc: wst[:, kc * 1024 + 256: kc * 1024 + 256 + LAT])
        else:
            lwk = w["l_wk"]
            kb.wait("tensor", "V", nt_done)
            toksA = {}
            tA = {}
            kb.guard("sync", "T")
            toksA[0] = kb.dma("sync", wst[:, 0:LAT], lwk[0:128, :], None)
            for kc in range(NC):
                if kc + 1 < NC:
                    if kc >= 1:
                        kb.wait("sync", "T", tA[kc - 1])
                    toksA[kc + 1] = kb.dma(
                        "sync",
                        wst[:, ((kc + 1) % 2) * 1024:((kc + 1) % 2) * 1024 + LAT],
                        lwk[(kc + 1) * 128:(kc + 2) * 128, :], None)
                kb.wait_tok("tensor", toksA[kc])
                buf = wst[:, (kc % 2) * 1024:(kc % 2) * 1024 + LAT]
                for oc in range(2):
                    kb.op("tensor", lambda e, buf=buf, oc=oc, kc=kc, f=(kc == 0),
                          g=(kc == NC - 1):
                          e.matmul(pb[4 + oc], buf[:, oc * 128:(oc + 1) * 128],
                                   xst[:, kc * TW:(kc + 1) * TW], start=f, stop=g),
                          inc=("T" if oc == 1 else None))
                tA[kc] = kb.val("T")
            tt = kb.val("T")
            kb.wait("vector", "T", tt)
            kdeps = {"t_ind2": None, "v_final": None, "chunk_v": None}
            for oc in range(2):
                kdeps = rope_qknorm(
                    st, pb[4 + oc], t0, AO[:, oc * TW:(oc + 1) * TW],
                    deps=kdeps,
                    split_out=(KGh[:, oc * s + t0: oc * s + t0 + TW],
                               KGl[:, oc * s + t0: oc * s + t0 + TW]))
            kb.wait("tensor", "V", kb.val("V"))

            emit_v_proj_local(st, t0, w["l_wv"])

    if DEBUG_STAGE in ("lat", "k", "kv") and glob:
        return
    for t in range(nt):
        t0 = t * TW
        bc = rms_scale_bcast(st, t0)
        kb.wait("vector", "T", bc)
        kb.guard("vector", "T")
        nt_done = norm_tile(st, t0, xst)
        kb.wait("tensor", "V", nt_done)

        t_after = {}
        toks = {}
        kb.guard("gpsimd", "T")
        toks[0] = kb.dma("gpsimd", wst[:, 0:1024], wq[0:128, :], None)
        for kc in range(NC):
            if kc + 1 < NC:
                if kc >= 1:
                    kb.wait("gpsimd", "T", t_after[kc - 1])
                toks[kc + 1] = kb.dma(
                    "gpsimd", wst[:, ((kc + 1) % 2) * 1024:((kc + 1) % 2) * 1024 + 1024],
                    wq[(kc + 1) * 128:(kc + 2) * 128, :], None)
            kb.wait_tok("tensor", toks[kc])
            buf = wst[:, (kc % 2) * 1024:(kc % 2) * 1024 + 1024]
            for oc in range(NC):
                kb.op("tensor", lambda e, buf=buf, oc=oc, kc=kc, f=(kc == 0),
                      g=(kc == NC - 1):
                      e.matmul(pb[oc], buf[:, oc * 128:(oc + 1) * 128],
                               xst[:, kc * TW:(kc + 1) * TW], start=f, stop=g),
                      inc=("T" if oc == NC - 1 else None))
            t_after[kc] = kb.val("T")
        kb.wait("vector", "T", kb.val("T"))
        kb.guard("sync", "T")
        wo_toks = {0: kb.dma("sync", wst[:, 0:1024], wo[0:128, :], None),
                   1: kb.dma("sync", wst[:, 1024:2048], wo[128:256, :], None)}
        for oc in range(NC):
            kb.op("vector", lambda e, oc=oc: e.tensor_copy(
                AO[:, oc * TW:(oc + 1) * TW], pb[oc]))
        qv = kb.commit("vector", "V")
        st["sdpa_sbv"] = {b: qv for b in (0, 1, 6, 7)}
        st["sdpa_pvv"] = {j: qv for j in range(4)}
        deps = {"t_ind2": None, "v_final": None, "chunk_v": qv}
        for oc in range(4):
            deps = rope_qknorm_sb(
                st, t0, AO[:, oc * TW:(oc + 1) * TW], deps=deps,
                split_out=(QTh[:, oc * TW:(oc + 1) * TW],
                           QTl[:, oc * TW:(oc + 1) * TW]))
        emit_sdpa_kv(st, glob, t, 0, deep=False, qready=deps["split_v"])
        for oc in range(4, NC):
            deps = rope_qknorm_sb(
                st, t0, AO[:, oc * TW:(oc + 1) * TW], deps=deps,
                split_out=(QTh[:, oc * TW:(oc + 1) * TW],
                           QTl[:, oc * TW:(oc + 1) * TW]))
        st["sdpa_sbv"][6] = st["sdpa_sbv"][7] = deps["v_final"]
        emit_sdpa_kv(st, glob, t, 1, qready=deps["split_v"])
        emit_sdpa_kv(st, glob, t, 2)
        emit_sdpa_kv(st, glob, t, 3)

        t_after = {}
        toks = wo_toks
        for tok in st.pop("ao_lift", []):
            kb.wait_tok("tensor", tok)
        kb.guard("tensor", "V")
        for kc in range(NC):
            kb.wait_tok("tensor", toks[kc])
            buf = wst[:, (kc % 2) * 1024:(kc % 2) * 1024 + 1024]
            for oc in range(NC):
                kb.op("tensor", lambda e, buf=buf, oc=oc, kc=kc, f=(kc == 0),
                      g=(kc == NC - 1):
                      e.matmul(pb[oc], buf[:, oc * 128:(oc + 1) * 128],
                               AO[:, kc * TW:(kc + 1) * TW], start=f, stop=g),
                      inc=("T" if oc == NC - 1 else None))
            t_after[kc] = kb.val("T")
            if kc + 2 < NC:
                kb.wait("sync", "T", t_after[kc])
                toks[kc + 2] = kb.dma(
                    "sync", wst[:, (kc % 2) * 1024:(kc % 2) * 1024 + 1024],
                    wo[(kc + 2) * 128:(kc + 3) * 128, :], None)
        tt = kb.val("T")
        kb.wait("vector", "T", tt)
        for oc in range(NC):
            dstc = xn[:, oc * s + t0: oc * s + t0 + TW]
            kb.op("vector", lambda e, oc=oc, dstc=dstc: e.tensor_tensor(
                dstc, dstc, pb[oc], ALU.add))
        vv = kb.commit("vector", "V")
        kb.wait("tensor", "V", vv)


def emit_sdpa_kv(st, glob, t, kv, deep=True, qready=None):
    kb, s = st["kb"], st["s"]
    pb, pbh, tb, Pt, VG, AO = (st["pb"], st["pbh"], st["tb"], st["Pt"],
                               st["VG"], st["AO"])
    KGh, KGl, QTh, QTl = st["KGh"], st["KGl"], st["QTh"], st["QTl"]
    Pt2 = st["lat"]
    c = st["consts"]
    t0 = t * TW
    nblk = 4 * 65
    vg_free = (s // 128) * nblk
    rb = 64 * (kv % 2)
    kcn = kv // 2
    sbanks = [0, 1, 6, 7] if deep else [0, 1, 0, 1]
    sbv = st["sdpa_sbv"]
    pvv = st["sdpa_pvv"]
    lastpv = st.setdefault("sdpa_lastpv", {})

    if qready is not None:
        kb.wait("tensor", "V", qready)

    pos = []
    for qc, (ha, hb) in enumerate(Q_CHUNK_HEADS):
        if ha // 4 == kv:
            pos.append((qc, 0))
        if hb // 4 == kv:
            pos.append((qc, 1))
    assert len(pos) == 4

    chunks = []
    if glob:
        for j in range(0, t * 4 + 4):
            k0 = j * 128
            qoff = max(0, k0 - t0)
            subs = [(qoff, "d")] if k0 >= t0 else []
            chunks.append((k0, qoff, TW - qoff, subs))
    else:
        for j in range(-2, 4):
            k0 = t0 + j * 128
            if k0 < 0:
                continue
            lo = max(0, j) * 128
            hi = min(4, j + 3) * 128
            subs = []
            if j >= 0:
                subs.append((j * 128, "d"))
            if j + 2 <= 3:
                subs.append(((j + 2) * 128, "u"))
            chunks.append((k0, lo, hi - lo, subs))
    L = len(chunks)

    def pt_ap(j, ci, a, n):
        buf = Pt if ci % 2 == 0 else Pt2
        return buf[:, j * TW + a: j * TW + a + n]

    exp_sc = {}
    pv_t = {}
    for ci, (k0, qoff, N, subs) in enumerate(chunks):
        last = (ci == L - 1)
        lhsV = ap_of(VG, (k0 // 128) * nblk + 65 * kv, [[vg_free, 128], [1, 65]])
        lhsKh = KGh[rb:rb + 64, kcn * s + k0: kcn * s + k0 + 128]
        lhsKl = KGl[rb:rb + 64, kcn * s + k0: kcn * s + k0 + 128]
        for j, (qc, half) in enumerate(pos):
            qb = 64 * half
            scb = pbh[sbanks[j]]
            rQh = QTh[qb:qb + 64, qc * TW + qoff: qc * TW + qoff + N]
            rQl = QTl[qb:qb + 64, qc * TW + qoff: qc * TW + qoff + N]
            if ci == 0:
                v = sbv.get(sbanks[j])
                if v is not None and (deep or j < 2):
                    kb.wait("tensor", "V", v)
            if deep:
                prev = (ci - 1, j)
            else:
                prev = (ci, j - 2) if j >= 2 else (ci - 1, j + 2)
            if prev in exp_sc:
                kb.wait("tensor", "SC", exp_sc[prev])
            kb.op("tensor", lambda e, lhsKh=lhsKh, rQh=rQh, qoff=qoff, N=N,
                  b=scb: e.matmul(b[0:128, qoff:qoff + N], lhsKh, rQh,
                                  start=True, stop=False))
            kb.op("tensor", lambda e, lhsKl=lhsKl, rQh=rQh, qoff=qoff, N=N,
                  b=scb: e.matmul(b[0:128, qoff:qoff + N], lhsKl, rQh,
                                  start=False, stop=False))
            kb.op("tensor", lambda e, lhsKh=lhsKh, rQl=rQl, qoff=qoff, N=N,
                  b=scb: e.matmul(b[0:128, qoff:qoff + N], lhsKh, rQl,
                                  start=False, stop=True), inc="T")
            tt = kb.val("T")
            kb.wait("scalar", "T", tt)
            if (ci - 2, j) in pv_t:
                kb.wait("scalar", "T", pv_t[(ci - 2, j)])
            elif ci < 2 and (j, ci % 2) in lastpv:
                kb.wait("scalar", "T", lastpv[(j, ci % 2)])
            sc = kb.op("scalar", lambda e, qoff=qoff, N=N, b=scb, j=j, ci=ci:
                       e.activation(pt_ap(j, ci, qoff, N),
                                    b[0:128, qoff:qoff + N], AF.Exp,
                                    bias=c[:, 0:1], scale=0.125), inc="SC")
            exp_sc[(ci, j)] = sc
        for j, (qc, half) in enumerate(pos):
            pvb = pbh[2 + j]
            sc = exp_sc[(ci, j)]
            if subs:
                kb.wait("vector", "SC", sc)
                for (soff, kind) in subs:
                    mk = tb["mask_diag"] if kind == "d" else tb["mask_su"]
                    kb.op("vector", lambda e, soff=soff, mk=mk, j=j, ci=ci:
                          e.tensor_tensor(
                              pt_ap(j, ci, soff, 128),
                              pt_ap(j, ci, soff, 128),
                              mk[:, :], ALU.mult))
                mdone = kb.commit("vector", "V")
                kb.wait("tensor", "V", mdone)
            else:
                kb.wait("tensor", "SC", sc)
            if ci == 0:
                v = pvv.get(j)
                if v is not None:
                    kb.wait("tensor", "V", v)
            if glob:
                kb.op("tensor", lambda e, lhsV=lhsV, qoff=qoff, N=N, b=pvb,
                      j=j, ci=ci, f=(ci == 0), g=last: e.matmul(
                          b[0:65, qoff:qoff + N], lhsV,
                          pt_ap(j, ci, qoff, N),
                          start=f, stop=g), inc="T")
            else:
                nsub = N // 128
                for sbt in range(nsub):
                    so = qoff + sbt * 128
                    kb.op("tensor", lambda e, lhsV=lhsV, so=so, b=pvb, j=j,
                          ci=ci,
                          f=(ci == 0 and sbt == 0),
                          g=(last and sbt == nsub - 1): e.matmul(
                              b[0:65, so:so + 128], lhsV,
                              pt_ap(j, ci, so, 128),
                              start=f, stop=g),
                          inc=("T" if sbt == nsub - 1 else None))
            pv_t[(ci, j)] = kb.val("T")

    for j in range(4):
        lastpv[(j, (L - 1) % 2)] = pv_t[(L - 1, j)]
        if L >= 2:
            lastpv[(j, (L - 2) % 2)] = pv_t[(L - 2, j)]

    xstb = st["xst"]
    Cslots = [R64(st, 0), R64(st, 1), R64(st, 2), xstb[64:65, 2 * TW:3 * TW]]
    rws = st["rows"]
    Lslots = [rws[32:33, 0:TW], rws[32:33, TW:2 * TW], rws[32:33, 2 * TW:3 * TW],
              rws[96:97, 0:TW]]
    kb.wait("vector", "T", pv_t[(L - 1, 3)])
    for j in range(4):
        kb.op("vector", lambda e, j=j: e.tensor_copy(
            Cslots[j], pbh[2 + j][64:65, 0:TW]))
    cv = kb.commit("vector", "V")
    kb.wait("scalar", "V", cv)
    for j in range(4):
        kb.op("scalar", lambda e, j=j: e.activation(Lslots[j], Cslots[j], AF.Ln))
    kb.drain("scalar")
    sqv = {}
    for j in range(4):
        sqv[j] = kb.op("scalar", lambda e, j=j: e.activation(
            Cslots[j], Lslots[j], AF.Exp, bias=0.0, scale=-1.0), inc="SC")
    stgs = [st["xst"][0:64, 0:TW], st["xst"][0:64, TW:2 * TW]]
    ttv = None
    for pair in ((0, 1), (2, 3)):
        for j in pair:
            kb.wait("tensor", "SC", sqv[j])
            if ttv is not None and not deep and j == pair[0]:
                kb.wait("tensor", "V", ttv)
            lx = (L - 1, 2 + (j % 2)) if not deep else (L - 1, j)
            kb.wait("tensor", "SC", exp_sc[lx])
            kb.op("tensor", lambda e, j=j, b=sbanks[j]: e.matmul(
                pbh[b][0:64, 0:TW], tb["onesq"][64:65, 0:64],
                Cslots[j], start=True, stop=True), inc="T")
        tt = kb.val("T")
        kb.wait("vector", "T", tt)
        bufs = {pair[0]: xstb[0:64, 2 * TW:3 * TW],
                pair[1]: xstb[0:64, 3 * TW:4 * TW]}
        for j in pair:
            kb.op("vector", lambda e, j=j, b=bufs[j]: e.tensor_copy(
                b, pbh[sbanks[j]][0:64, 0:TW]))
        kb.drain("vector")
        for j in pair:
            qc, half = pos[j]
            if half == 0:
                dst = AO[0:64, qc * TW:(qc + 1) * TW]
                kb.op("vector", lambda e, j=j, b=bufs[j], dst=dst: e.tensor_tensor(
                    dst, pbh[2 + j][0:64, 0:TW], b, ALU.mult))
            else:
                stg = stgs[j % 2]
                ltok = st.setdefault("lift_tok_slot", {}).get(j % 2)
                if ltok is not None:
                    kb.wait_tok("vector", ltok)
                kb.op("vector", lambda e, j=j, b=bufs[j], stg=stg: e.tensor_tensor(
                    stg, pbh[2 + j][0:64, 0:TW], b, ALU.mult))
                vvl = kb.commit("vector", "V")
                kb.wait("gpsimd", "V", vvl)
                tok = kb.dma("gpsimd", AO[64:128, qc * TW:(qc + 1) * TW], stg, None)
                st.setdefault("ao_lift", []).append(tok)
                st["lift_tok_slot"][j % 2] = tok
        ttv = kb.commit("vector", "V")
        for j in pair:
            sbv[sbanks[j]] = ttv
            pvv[j] = ttv


def emit_v_proj(st, t0, nk, lhs_fn, rhs_fn):
    kb = st["kb"]
    pb, VG = st["pb"], st["VG"]
    nblk = 4 * 65
    vg_free = (st["s"] // 128) * nblk
    for tc4 in range(TW // 128):
        tca = (t0 // 128) + tc4
        kb.guard("tensor", "V")
        for kc in range(nk):
            kb.op("tensor", lambda e, kc=kc, tc4=tc4, f=(kc == 0),
                  g=(kc == nk - 1): e.matmul(
                st["pbh"][4][0:128, 0:256], lhs_fn(kc, tc4), rhs_fn(kc),
                start=f, stop=g),
                inc=("T" if kc == nk - 1 else None))
        tt = kb.val("T")
        kb.wait("vector", "T", tt)
        out_ap = ap_of(VG, tca * nblk, [[vg_free, 128], [65, 4], [1, 64]])
        in_ap = ap_of(st["pbh"][4], 0, [[512, 128], [64, 4], [1, 64]])
        kb.op("vector", lambda e, o=out_ap, i=in_ap: e.tensor_copy(o, i))
        kb.commit("vector", "V")
    ones_ap = ap_of(VG, (t0 // 128) * nblk + 64,
                    [[vg_free, 128], [nblk, TW // 128], [65, 4], [1, 1]])
    kb.op("vector", lambda e, o=ones_ap: e.memset(o, 1.0))
    kb.commit("vector", "V")


def emit_v_proj_local(st, t0, lwv):
    kb = st["kb"]
    wst, xst = st["wst"], st["xst"]
    nblk = 4 * 65
    vg_free = (st["s"] // 128) * nblk

    def acc_ap(tc4, dims):
        return ap_of(st["pbh"][4 + (tc4 // 2)], (tc4 % 2) * 256, dims)

    kb.guard("tensor", "V")
    for kc in range(NC):
        buf = wst[:, (kc % 2) * 1024:(kc % 2) * 1024 + LAT]
        wl = band_load(st, buf, lwv[kc * 128:(kc + 1) * 128, :], LAT)
        kb.wait_tok("tensor", wl)
        for tc4 in range(TW // 128):
            lhs = xst[:, kc * TW + tc4 * 128: kc * TW + (tc4 + 1) * 128]
            out = acc_ap(tc4, [[512, 128], [1, 256]])
            kb.op("tensor", lambda e, lhs=lhs, out=out, buf=buf, kc=kc,
                  f=(kc == 0 and tc4 % 2 == 0),
                  g=(kc == NC - 1 and tc4 % 2 == 1):
                  e.matmul(out, lhs, buf, start=f, stop=g),
                  inc=("T" if tc4 == TW // 128 - 1 else None))
    tt = kb.val("T")
    kb.wait("vector", "T", tt)
    for tc4 in range(TW // 128):
        tca = (t0 // 128) + tc4
        out_ap = ap_of(st["VG"], tca * nblk, [[vg_free, 128], [65, 4], [1, 64]])
        in_ap = acc_ap(tc4, [[512, 128], [64, 4], [1, 64]])
        kb.op("vector", lambda e, o=out_ap, i=in_ap: e.tensor_copy(o, i))
        kb.drain("vector")
    kb.commit("vector", "V")
    ones_ap = ap_of(st["VG"], (t0 // 128) * nblk + 64,
                    [[vg_free, 128], [nblk, TW // 128], [65, 4], [1, 1]])
    kb.op("vector", lambda e, o=ones_ap: e.memset(o, 1.0))
    kb.commit("vector", "V")


def emit_store_s(st):
    kb, s = st["kb"], st["s"]
    kb.guard("gpsimd", "V")
    kb.dma_group("gpsimd",
                 [(st["s_scr"][ch * 128:(ch + 1) * 128, :],
                   st["xn"][:, ch * s:(ch + 1) * s]) for ch in range(NC)], None)



def emit_wquant(st):
    kb = st["kb"]
    c, pb, pbh, tb, cs, mw = (st["consts"], st["pb"], st["pbh"], st["tb"],
                              st["cs"], st["mw"])
    wf, wq16, sgn2 = st["wf"], st["wq16"], st["sgn2"]
    tmp1, tmp2 = st["tmp1"], st["tmp2"]

    specs = [
        ("w_gate", st["wgq_scr"], INNER, 0, 0),
        ("w_up", st["wuq_scr"], INNER, 32, 1),
        ("w_down", st["wdq_scr"], D, 64, 2),
    ]
    for (wname, scr, nout, cs_off, mw_i) in specs:
        src = st["w"][wname]
        nin_ch = src.tensor.shape[0] // 128

        def wfs(ic, nout=nout):
            sl = ic % 2
            return wf[:, sl * INNER: sl * INNER + nout]

        toks = {}
        rv = {}
        kb.guard("vector", "SC")
        kb.guard("sync", "V", "SC")
        toks[0] = kb.dma("sync", wfs(0), src[0:128, :], None)
        for ic in range(nin_ch):
            if ic + 1 < nin_ch:
                if ic >= 1:
                    kb.wait("sync", "V", rv[ic - 1])
                toks[ic + 1] = kb.dma("sync", wfs(ic + 1),
                                      src[(ic + 1) * 128:(ic + 2) * 128, :], None)
            kb.wait_tok("vector", toks[ic])
            rv[ic] = kb.op("vector", lambda e, ic=ic, wfs=wfs: e.reduce_sum(
                tmp1[:, ic:ic + 1], wfs(ic), AX.X, apply_absolute_value=True),
                inc="V")
        kb.drain("vector")
        kb.op("vector", lambda e, n=nin_ch: e.reduce_sum(
            tmp2[:, 0:1], tmp1[:, 0:n], AX.X))
        va = kb.commit("vector", "V")
        kb.wait("tensor", "V", va)
        tt = kb.op("tensor", lambda e: e.matmul(
            pbh[5][:, 0:1], tb["onesq"][:, :], tmp2[:, 0:1], start=True, stop=True),
            inc="T")
        kb.wait("vector", "T", tt)
        kb.op("vector", lambda e, mw_i=mw_i: e.tensor_scalar(
            mw[:, mw_i:mw_i + 1], pbh[5][:, 0:1], 1.0 / (1 << 22), 1e-5,
            ALU.mult, ALU.max))
        kb.drain("vector")
        kb.op("vector", lambda e, mw_i=mw_i: e.tensor_scalar(
            tmp1[:, 0:1], mw[:, mw_i:mw_i + 1], 0.5, None, ALU.mult))
        kb.drain("vector")
        kb.op("vector", lambda e, mw_i=mw_i: e.tensor_scalar(
            tmp1[:, 1:2], mw[:, mw_i:mw_i + 1], -0.5, None, ALU.mult))
        vb = kb.commit("vector", "V")
        kb.wait("scalar", "V", vb)

        def wqs(ic, nout=nout):
            sl = ic % 2
            return wq16[:, sl * INNER: sl * INNER + nout]

        def sgs(ic, nout=nout):
            sl = ic % 2
            return sgn2[:, sl * INNER: sl * INNER + nout]

        slot_store = st.setdefault("wq_store_toks", {})
        colsum_t = {}
        toksB = {}
        sc_after = {}
        kb.guard("sync", "SC", "V", "T")
        toksB[0] = kb.dma("sync", wfs(0), src[0:128, :], None)
        for ic in range(nin_ch):
            sl = ic % 2
            if ic + 1 < nin_ch:
                if ic >= 1:
                    kb.wait("sync", "SC", sc_after[ic - 1])
                toksB[ic + 1] = kb.dma("sync", wfs(ic + 1),
                                       src[(ic + 1) * 128:(ic + 2) * 128, :], None)
            kb.wait_tok("scalar", toksB[ic])
            if sl in colsum_t:
                kb.wait("scalar", "T", colsum_t[sl])
            if slot_store.get(sl) is not None:
                kb.wait_tok("scalar", slot_store[sl])
            kb.op("scalar", lambda e, ic=ic, wqs=wqs, wfs=wfs: e.activation(
                wqs(ic), wfs(ic), AF.Sign, bias=tmp1[:, 1:2]))
            s2 = kb.op("scalar", lambda e, ic=ic, sgs=sgs, wfs=wfs: e.activation(
                sgs(ic), wfs(ic), AF.Sign, bias=tmp1[:, 0:1]), inc="SC")
            sc_after[ic] = s2
            kb.wait("vector", "SC", s2)
            kb.op("vector", lambda e, ic=ic, wqs=wqs, sgs=sgs: e.tensor_tensor(
                wqs(ic), wqs(ic), sgs(ic), ALU.add))
            vq = kb.commit("vector", "V")
            kb.wait("gpsimd", "V", vq)
            slot_store[sl] = kb.dma("gpsimd", scr[ic * 128:(ic + 1) * 128, :],
                                    wqs(ic), None)
            kb.wait("tensor", "V", vq)
            for os_ in range(nout // 128):
                o0 = sl * INNER + os_ * 128
                kb.op("tensor", lambda e, os_=os_, ic=ic, cs_off=cs_off, o0=o0,
                      f=(ic == 0 and os_ == 0),
                      g=(ic == nin_ch - 1 and os_ == nout // 128 - 1): e.matmul(
                          pbh[6][:, cs_off + os_: cs_off + os_ + 1],
                          wq16[:, o0:o0 + 128],
                          tb["ones16"][:, 0:1],
                          start=f, stop=g),
                      inc=("T" if os_ == nout // 128 - 1 else None))
            colsum_t[sl] = kb.val("T")
        tt = kb.val("T")
        kb.wait("vector", "T", tt)
        kb.op("vector", lambda e, cs_off=cs_off, n=nout // 128: e.tensor_scalar(
            cs[:, cs_off:cs_off + n], pbh[6][:, cs_off:cs_off + n], MAGIC, None,
            ALU.mult))
        kb.commit("vector", "V")



def bcast_row(st, row_ap, dst_sb):
    kb, tb = st["kb"], st["tb"]
    kb.guard("tensor", "V")
    tt = kb.op("tensor", lambda e: e.matmul(
        st["pbh"][7][:, :], tb["onesq"][0:1, :], row_ap, start=True, stop=True),
        inc="T")
    kb.wait("vector", "T", tt)
    kb.op("vector", lambda e: e.tensor_copy(dst_sb[:, :], st["pbh"][7][:, :]))
    return kb.commit("vector", "V")


def emit_amax(st, chunk_fn, nch, dst_row):
    kb = st["kb"]
    accs = [st["tmp1"], st["tmp2"], st["tmpsq"], st["tmp4"]]
    na = min(len(accs), max(1, nch // 2))
    if st.get("tmps_sc") is not None:
        kb.wait("vector", "SC", st["tmps_sc"])
    for i in range(na):
        kb.op("vector", lambda e, i=i: e.tensor_tensor(
            accs[i][:, :], chunk_fn(i), chunk_fn(na + i), ALU.max))
    kb.drain("vector")
    nxt = 2 * na
    while nxt < nch:
        for i in range(na):
            if nxt + i < nch:
                kb.op("vector", lambda e, i=i, j=nxt + i: e.tensor_tensor(
                    accs[i][:, :], accs[i][:, :], chunk_fn(j), ALU.max))
        kb.drain("vector")
        nxt += na
    done = 0
    while done < nch:
        for i in range(na):
            if done + i < nch:
                kb.op("vector", lambda e, i=i, j=done + i: e.scalar_tensor_tensor(
                    accs[i][:, :], chunk_fn(j), -1.0, accs[i][:, :],
                    ALU.mult, ALU.max))
        kb.drain("vector")
        done += na
    step = na // 2
    while step >= 1:
        for i in range(step):
            kb.op("vector", lambda e, i=i, s2=step: e.tensor_tensor(
                accs[i][:, :], accs[i][:, :], accs[i + s2][:, :], ALU.max))
        kb.drain("vector")
        step //= 2
    vv = kb.commit("vector", "V")
    kb.wait("gpsimd", "V", vv)
    kb.op("gpsimd", lambda e: e.reduce_max(R(st, 0), accs[0][:, :], AX.C))
    gg = kb.commit("gpsimd", "G")
    kb.wait("vector", "G", gg)
    kb.op("vector", lambda e, d=dst_row: e.tensor_scalar(
        d, R(st, 0), 1e-5, None, ALU.max))
    return kb.commit("vector", "V")


def emit_ffn(st):
    kb, s, nt = st["kb"], st["s"], st["nt"]
    pb, pbh, tb, c, cs, mw = (st["pb"], st["pbh"], st["tb"], st["consts"],
                              st["cs"], st["mw"])
    arena, z1, z2, prod, fw = st["arena"], st["z1"], st["z2"], st["prod"], st["fw"]
    Pt = st["Pt"]
    sxb, tgb, tub, tdb = st["sxb"], st["tgb"], st["tub"], st["tdb"]
    tmp1, tmp2, tmpsq = st["tmp1"], st["tmp2"], st["tmpsq"]
    tmps4 = [tmp1, tmp2, tmpsq, st["tmp4"]]
    xoc = st["xoc"]
    XNT = 0
    S0 = NC * TW
    bankv = st.setdefault("ffn_bankv", {})

    def bank_wait(banks):
        for b in banks:
            v = bankv.get(b)
            if v is not None:
                kb.wait("tensor", "V", v)

    def A(base, ch):
        return arena[:, base + ch * TW: base + (ch + 1) * TW]

    STT_ = NC * TW

    def emit_q1g(t):
        t0 = t * TW
        kb.guard("gpsimd", "V")
        pairs = []
        for ch in range(NC):
            pairs.append((A(XNT, ch),
                          st["xn_scr"][ch * 128:(ch + 1) * 128, t0:t0 + TW]))
        tl = kb.dma_group("gpsimd", pairs, None)
        kb.wait_tok("vector", tl)

        av = emit_amax(st, lambda ch: A(XNT, ch), NC, R(st, 2))
        kb.op("vector", lambda e: e.tensor_scalar(
            R(st, 0), R(st, 2), mw[0:1, 0:1], 1.0 / 254.0, ALU.mult, ALU.mult))
        kb.op("vector", lambda e: e.tensor_scalar(
            R(st, 1), R(st, 2), mw[0:1, 1:2], 1.0 / 254.0, ALU.mult, ALU.mult))
        vv2 = kb.commit("vector", "V")
        kb.wait("scalar", "V", vv2)
        kb.op("scalar", lambda e: e.activation(
            st["rows"][32:33, TW:2 * TW], R(st, 2), AF.Ln))
        kb.drain("scalar")
        sxv = kb.op("scalar", lambda e: e.activation(
            R(st, 2), st["rows"][32:33, TW:2 * TW], AF.Exp,
            bias=c[0:1, 3:4], scale=-1.0), inc="SC")
        kb.guard("tensor", "V")
        kb.wait("tensor", "V", vv2)
        t1 = kb.op("tensor", lambda e: e.matmul(
            pbh[7][:, :], tb["onesq"][0:1, :], R(st, 0), start=True, stop=True))
        t2 = kb.op("tensor", lambda e: e.matmul(
            pbh[6][:, :], tb["onesq"][0:1, :], R(st, 1), start=True, stop=True),
            inc="T")
        kb.wait("vector", "T", kb.val("T"))
        kb.op("vector", lambda e: e.tensor_copy(tgb[:, :], pbh[7][:, :]))
        kb.op("vector", lambda e: e.tensor_copy(tub[:, :], pbh[6][:, :]))
        cbv = kb.commit("vector", "V")
        bankv[6] = cbv
        kb.wait("tensor", "SC", sxv)
        kb.wait("tensor", "V", cbv)
        kb.op("tensor", lambda e: e.matmul(
            pbh[7][:, :], tb["onesq"][0:1, :], R(st, 2), start=True, stop=True),
            inc="T")
        kb.wait("vector", "T", kb.val("T"))
        kb.op("vector", lambda e: e.tensor_copy(sxb[:, :], pbh[7][:, :]))
        bankv[7] = kb.commit("vector", "V")

        prevsc = st.get("tmps_sc")
        for b0 in (0, 4):
            if prevsc is not None:
                kb.wait("vector", "SC", prevsc)
            for i in range(4):
                ch = b0 + i
                kb.op("vector", lambda e, ch=ch, i=i: e.tensor_tensor(
                    tmps4[i][:, :], A(XNT, ch), sxb[:, :], ALU.mult))
            zv = kb.commit("vector", "V")
            kb.wait("scalar", "V", zv)
            if b0 == 0 and st.get("ffn_gu_last_t") is not None:
                kb.wait("scalar", "T", st["ffn_gu_last_t"])
            for i in range(4):
                ch = b0 + i
                kb.op("scalar", lambda e, ch=ch, i=i: e.activation(
                    z1[:, ch * TW:(ch + 1) * TW], tmps4[i][:, :], AF.Relu,
                    bias=c[:, 4:5], scale=1.0), inc=("SC" if i == 3 else None))
            prevsc = kb.val("SC")
            st["tmps_sc"] = prevsc
        st["ffn_z1v"] = prevsc

    def gu_load(og, slot):
        return kb.dma_group("sync" if og % 2 == 0 else "gpsimd", [
            (ap_of(fw, slot * S0, [[3 * NC * TW, 128], [256, NC], [1, 256]]),
             ap_of(st["wgq_scr"].tensor, og * 256,
                   [[INNER, 128], [128 * INNER, NC], [1, 256]])),
            (ap_of(fw, slot * S0 + NC * 256,
                   [[3 * NC * TW, 128], [256, NC], [1, 256]]),
             ap_of(st["wuq_scr"].tensor, og * 256,
                   [[INNER, 128], [128 * INNER, NC], [1, 256]])),
        ], None)

    def emit_gu(t):
        kb.wait("tensor", "SC", st["ffn_z1v"])
        gu_t_after = {}
        gu_toks = {}
        kb.guard("sync", "T")
        kb.guard("gpsimd", "T")
        gu_toks[0] = gu_load(0, 0)
        for og in range(16):
            slot = og % 2
            bs = [0, 1, 2, 3] if og % 2 == 0 else [4, 5, 6, 7]
            if og + 1 < 16:
                if og >= 1:
                    kb.wait("sync" if (og + 1) % 2 == 0 else "gpsimd",
                            "T", gu_t_after[og - 1])
                gu_toks[og + 1] = gu_load(og + 1, (og + 1) % 2)
            kb.wait_tok("tensor", gu_toks[og])
            bank_wait(bs)
            for ic in range(NC):
                for oo in range(2):
                    off = slot * S0 + ic * 256 + oo * 128
                    kb.op("tensor", lambda e, off=off, oo=oo, ic=ic, bs=bs,
                          f=(ic == 0), g=(ic == NC - 1): e.matmul(
                        pb[bs[oo]], fw[:, off:off + 128],
                        z1[:, ic * TW:(ic + 1) * TW], start=f, stop=g))
                for oo in range(2):
                    off = slot * S0 + NC * 256 + ic * 256 + oo * 128
                    kb.op("tensor", lambda e, off=off, oo=oo, ic=ic, bs=bs,
                          f=(ic == 0), g=(ic == NC - 1): e.matmul(
                        pb[bs[2 + oo]], fw[:, off:off + 128],
                        z1[:, ic * TW:(ic + 1) * TW], start=f, stop=g),
                        inc=("T" if (ic == NC - 1 and oo == 1) else None))
            gu_t_after[og] = kb.val("T")
            tt = kb.val("T")
            kb.wait("vector", "T", tt)
            if st.get("tmps_sc") is not None:
                kb.wait("vector", "SC", st["tmps_sc"])
            for oo in range(2):
                oc = og * 2 + oo
                kb.op("vector", lambda e, oo=oo, oc=oc, bs=bs:
                      e.scalar_tensor_tensor(
                          tmps4[oo][:, :], pb[bs[oo]], cs[:, oc:oc + 1],
                          tgb[:, :], ALU.subtract, ALU.mult))
            cv = kb.commit("vector", "V")
            kb.wait("scalar", "V", cv)
            for oo in range(2):
                kb.op("scalar", lambda e, oo=oo: e.activation(
                    Pt[:, oo * TW:(oo + 1) * TW], tmps4[oo][:, :], AF.Silu),
                    inc=("SC" if oo == 1 else None))
            sg = kb.val("SC")
            st["tmps_sc"] = sg
            kb.wait("vector", "SC", sg)
            for oo in range(2):
                oc = og * 2 + oo
                kb.op("vector", lambda e, oo=oo, oc=oc, bs=bs:
                      e.scalar_tensor_tensor(
                          tmps4[2 + oo][:, :], pb[bs[2 + oo]],
                          cs[:, 32 + oc:33 + oc], tub[:, :],
                          ALU.subtract, ALU.mult))
            kb.drain("vector")
            for oo in range(2):
                oc = og * 2 + oo
                kb.op("vector", lambda e, oo=oo, oc=oc: e.tensor_tensor(
                    prod[:, oc * TW:(oc + 1) * TW], Pt[:, oo * TW:(oo + 1) * TW],
                    tmps4[2 + oo][:, :], ALU.mult))
            ev = kb.commit("vector", "V")
            for b in bs:
                bankv[b] = ev
        st["ffn_gu_last_t"] = kb.val("T")

    def emit_q2(t):
        av = emit_amax(st, lambda chh: prod[:, chh * TW:(chh + 1) * TW], 32,
                       R(st, 2))
        kb.op("vector", lambda e: e.tensor_scalar(
            R(st, 1), R(st, 2), mw[0:1, 2:3], 1.0 / 254.0, ALU.mult, ALU.mult))
        vv2 = kb.commit("vector", "V")
        kb.wait("scalar", "V", vv2)
        kb.op("scalar", lambda e: e.activation(
            st["rows"][32:33, TW:2 * TW], R(st, 2), AF.Ln))
        kb.drain("scalar")
        sxv = kb.op("scalar", lambda e: e.activation(
            R(st, 2), st["rows"][32:33, TW:2 * TW], AF.Exp,
            bias=c[0:1, 3:4], scale=-1.0), inc="SC")
        kb.guard("tensor", "V")
        kb.wait("tensor", "V", vv2)
        kb.op("tensor", lambda e: e.matmul(
            pbh[6][:, :], tb["onesq"][0:1, :], R(st, 1), start=True, stop=True))
        kb.wait("tensor", "SC", sxv)
        kb.op("tensor", lambda e: e.matmul(
            pbh[7][:, :], tb["onesq"][0:1, :], R(st, 2), start=True, stop=True),
            inc="T")
        kb.wait("vector", "T", kb.val("T"))
        kb.op("vector", lambda e: e.tensor_copy(tdb[:, :], pbh[6][:, :]))
        kb.op("vector", lambda e: e.tensor_copy(sxb[:, :], pbh[7][:, :]))
        cbv = kb.commit("vector", "V")
        bankv[6] = cbv
        bankv[7] = cbv
        prevsc = st.get("tmps_sc")
        for b0 in range(0, 32, 4):
            if prevsc is not None:
                kb.wait("vector", "SC", prevsc)
            for i in range(4):
                chh = b0 + i
                kb.op("vector", lambda e, chh=chh, i=i: e.tensor_tensor(
                    tmps4[i][:, :], prod[:, chh * TW:(chh + 1) * TW], sxb[:, :],
                    ALU.mult))
            zv = kb.commit("vector", "V")
            kb.wait("scalar", "V", zv)
            if b0 == 0 and st.get("ffn_dn_last_t") is not None:
                kb.wait("scalar", "T", st["ffn_dn_last_t"])
            for i in range(4):
                chh = b0 + i
                kb.op("scalar", lambda e, chh=chh, i=i: e.activation(
                    z2[:, chh * TW:(chh + 1) * TW], tmps4[i][:, :], AF.Relu,
                    bias=c[:, 4:5], scale=1.0), inc=("SC" if i == 3 else None))
            prevsc = kb.val("SC")
            st["tmps_sc"] = prevsc
        st["ffn_z2v"] = prevsc

    def dn_load(oc):
        return kb.dma("sync", ap_of(fw, 2 * S0,
                                    [[3 * NC * TW, 128], [128, 32], [1, 128]]),
                      ap_of(st["wdq_scr"].tensor, oc * 128,
                            [[D, 128], [128 * D, 32], [1, 128]]), None)

    def emit_dn(t):
        t0 = t * TW
        kb.guard("gpsimd", "V")
        stl = kb.dma_group("gpsimd", [
            (A(STT_, ch), st["s_scr"][ch * 128:(ch + 1) * 128, t0:t0 + TW])
            for ch in range(NC)], None)
        kb.wait("tensor", "SC", st["ffn_z2v"])
        dn_t_after = {}
        dn_toks = {}
        xtoks = {}
        xv_after = {}
        dn_ev = {}
        kb.guard("sync", "T")
        dn_toks[0] = dn_load(0)
        kb.guard("gpsimd", "V")
        xtoks[0] = kb.dma("gpsimd", xoc[:, 0:512],
                          st["xT"][0:128, t0:t0 + TW], None)
        for oc in range(NC):
            bank = 4 + oc % 2
            if oc + 1 < NC:
                if oc >= 1:
                    kb.wait("gpsimd", "V", xv_after[oc - 1])
                xtoks[oc + 1] = kb.dma(
                    "gpsimd",
                    xoc[:, ((oc + 1) % 2) * 512:((oc + 1) % 2) * 512 + 512],
                    st["xT"][(oc + 1) * 128:(oc + 2) * 128, t0:t0 + TW], None)
            kb.wait_tok("tensor", dn_toks[oc])
            bank_wait([bank])
            for ic in range(32):
                off = 2 * S0 + ic * 128
                kb.op("tensor", lambda e, ic=ic, off=off, bank=bank,
                      f=(ic == 0), g=(ic == 31): e.matmul(
                    pb[bank], fw[:, off:off + 128],
                    z2[:, ic * TW:(ic + 1) * TW], start=f, stop=g),
                    inc=("T" if ic == 31 else None))
            dn_t_after[oc] = kb.val("T")
            if oc + 1 < NC:
                kb.wait("sync", "T", dn_t_after[oc])
                dn_toks[oc + 1] = dn_load(oc + 1)
            tt = kb.val("T")
            kb.wait("vector", "T", tt)
            if st.get("tmps_sc") is not None:
                kb.wait("vector", "SC", st["tmps_sc"])
            kb.op("vector", lambda e, bank=bank, oc=oc: e.scalar_tensor_tensor(
                tmp1[:, :], pb[bank], cs[:, 64 + oc:65 + oc], tdb[:, :],
                ALU.subtract, ALU.mult))
            vv = kb.commit("vector", "V")
            kb.wait("scalar", "V", vv)
            sg = kb.op("scalar", lambda e: e.activation(
                tmpsq[:, :], tmp1[:, :], AF.Silu), inc="SC")
            st["tmps_sc"] = sg
            kb.wait("vector", "SC", sg)
            if oc == 0:
                kb.wait_tok("vector", stl)
            kb.op("vector", lambda e, oc=oc: e.tensor_tensor(
                tmp2[:, :], tmpsq[:, :], A(STT_, oc), ALU.mult))
            kb.drain("vector")
            kb.wait_tok("vector", xtoks[oc])
            kb.op("vector", lambda e, oc=oc: e.tensor_tensor(
                tmp2[:, :], tmp2[:, :],
                xoc[:, (oc % 2) * 512:(oc % 2) * 512 + 512], ALU.add))
            vv = kb.commit("vector", "V")
            xv_after[oc] = vv
            bankv[bank] = vv
            kb.wait("gpsimd", "V", vv)
            otok = kb.dma("gpsimd",
                          st["outT"][oc * 128:(oc + 1) * 128, t0:t0 + TW],
                          tmp2[:, :], None)
            kb.wait_tok("vector", otok)
        st["ffn_dn_last_t"] = kb.val("T")

    emit_q1g(0)
    emit_gu(0)
    for t in range(1, nt):
        emit_q1g(t)
        emit_q2(t - 1)
        emit_gu(t)
        emit_dn(t - 1)
    emit_q2(nt - 1)
    emit_dn(nt - 1)



def kernel(**inputs):
    ins = {k: np.asarray(v) for k, v in inputs.items()}
    x = ins["L0_attn_hidden"]
    xTs = [np.ascontiguousarray(x[b].T) for b in range(x.shape[0])]

    qp = _q_col_perm()
    kp = _k_col_perm()
    wop = _wo_row_perm()
    wmap = {
        "g_wq": ins["L0_attn_g_wq"][:, qp],
        "g_kv_down": ins["L0_attn_g_kv_down"],
        "g_k_up": ins["L0_attn_g_k_up"][:, kp],
        "g_v_up": ins["L0_attn_g_v_up"],
        "g_wo": ins["L0_attn_g_wo"][wop, :],
        "l_wq": ins["L0_attn_l_wq"][:, qp],
        "l_wk": ins["L0_attn_l_wk"][:, kp],
        "l_wv": ins["L0_attn_l_wv"],
        "l_wo": ins["L0_attn_l_wo"][wop, :],
        "w_gate": ins["L0_ffn_w_gate"].T,
        "w_up": ins["L0_ffn_w_up"].T,
        "w_down": ins["L0_ffn_w_down"].T,
        "norm1_w": np.ascontiguousarray(ins["L0_attn_norm1_w"].reshape(NC, 128).T),
        "norm2_w": np.ascontiguousarray(ins["L0_attn_norm2_w"].reshape(NC, 128).T),
    }
    wmap = {k: np.ascontiguousarray(np.asarray(v, dtype=np.float32))
            for k, v in wmap.items()}
    tmap = host_tables(S)

    prog = build_program()
    in_maps = []
    for b in range(x.shape[0]):
        m = {"xT": xTs[b]}
        m.update(wmap)
        m.update(tmap)
        in_maps.append(m)
    res = run_bass_kernel_spmd(prog, in_maps, core_ids=list(range(len(in_maps))))
    out = np.stack([np.ascontiguousarray(r["outT"].T) for r in res.results])
    return out


# version 15 (final)
# speedup vs baseline: 1.0555x; 1.0040x over previous
import numpy as np

import concourse.bass as bass
import concourse.mybir as mybir
from concourse.bass_utils import run_bass_kernel_spmd

F32 = mybir.dt.float32
F16 = mybir.dt.float16
BF16 = mybir.dt.bfloat16
AF = mybir.ActivationFunctionType
ALU = mybir.AluOpType
AX = mybir.AxisListType

D = 1024
S = 2048
B = 8
H = 16
KVH = 4
HD = 64
LAT = 256
WIN = 256
INNER = 4096
NC = D // 128
TW = 512
MAGIC = 1536.0

Q_CHUNK_HEADS = [(0, 4), (1, 5), (2, 6), (3, 7), (8, 12), (9, 13), (10, 14), (11, 15)]
ENGS = ("sync", "tensor", "vector", "scalar", "gpsimd")

DEBUG_STAGE = None
KEEP_SCOPES = False
SKIP_TABS = False
DEBUG_ROPE = None



def _rope_perm64():
    return np.concatenate([np.arange(0, 64, 2), np.arange(1, 64, 2)])


def _q_col_perm():
    rp = _rope_perm64()
    cols = []
    for (ha, hb) in Q_CHUNK_HEADS:
        cols.append(64 * ha + rp)
        cols.append(64 * hb + rp)
    return np.concatenate(cols)


def _k_col_perm():
    rp = _rope_perm64()
    return np.concatenate([64 * kv + rp for kv in range(KVH)])


def _wo_row_perm():
    rows = []
    for (ha, hb) in Q_CHUNK_HEADS:
        rows.append(64 * ha + np.arange(64))
        rows.append(64 * hb + np.arange(64))
    return np.concatenate(rows)


def _rope_tables(base, s):
    inv = np.power(float(base), -np.arange(0, 32, dtype=np.float64) / 32.0)
    th = np.arange(s, dtype=np.float64)[:, None] * inv[None, :]
    cos = np.cos(th).T
    sin = np.sin(th).T
    C = np.empty((128, s), dtype=np.float32)
    Sg = np.empty((128, s), dtype=np.float32)
    for blk in range(4):
        r = 32 * blk
        C[r:r + 32] = cos.astype(np.float32)
        Sg[r:r + 32] = (sin if blk % 2 else -sin).astype(np.float32)
    return C, Sg


def _swap_mat():
    P = np.zeros((128, 128), dtype=np.float32)
    for p in range(128):
        blk = (p // 64) * 64
        o = p - blk
        P[p, blk + (o + 32) % 64] = 1.0
    return P


def host_tables(s):
    kk = np.arange(128)[:, None]
    qq = np.arange(128)[None, :]
    Cg, Sgt = _rope_tables(1e6, s)
    Cl, Slt = _rope_tables(1e4, s)
    ind2 = np.zeros((128, 2), dtype=np.float32)
    ind2[:64, 0] = 1.0
    ind2[64:, 1] = 1.0
    t = {
        "Cg": Cg, "Sg": Sgt, "Cl": Cl, "Sl": Slt,
        "mask_diag": (kk <= qq).astype(np.float32),
        "mask_su": (kk > qq).astype(np.float32),
        "swap": _swap_mat(),
        "onesq": np.ones((128, 128), dtype=np.float32),
        "ind2": ind2,
        "ind2T": np.ascontiguousarray(ind2.T),
        "ones16": np.ones((128, 1), dtype=np.float16),
    }
    return {k: np.ascontiguousarray(v) for k, v in t.items()}



class KB:
    def __init__(self, nc):
        self.nc = nc
        self.steps = {e: [] for e in ENGS}
        self.counts = {}
        self.sems = {}
        self._ctx = []
        self._scopes = []

    def sem(self, name):
        cm = self.nc.semaphore(name)
        h = cm.__enter__()
        self._ctx.append(cm)
        self.sems[name] = h
        self.counts[name] = 0
        return h

    def sbuf(self, name, shape, dt, scoped=False):
        cm = self.nc.sbuf_tensor(name, shape, dt)
        h = cm.__enter__()
        (self._scopes[-1] if scoped else self._ctx).append(cm)
        return h

    def psum(self, name, shape, dt=F32):
        cm = self.nc.psum_tensor(name, shape, dt)
        h = cm.__enter__()
        self._ctx.append(cm)
        return h

    def push_scope(self):
        self._scopes.append([])

    def pop_scope(self):
        for cm in reversed(self._scopes.pop()):
            cm.__exit__(None, None, None)

    def close(self):
        while self._scopes:
            self.pop_scope()
        while self._ctx:
            self._ctx.pop().__exit__(None, None, None)

    def _push(self, eng, fn):
        self.steps[eng].append(fn)

    def wait(self, eng, sem_name, value):
        h = self.sems[sem_name]
        v = int(value)
        if v > 0:
            self._push(eng, lambda e: e.wait_ge(h, v))

    def guard(self, eng, *sem_names):
        for sn in sem_names:
            self.wait(eng, sn, self.counts[sn])

    def op(self, eng, make, inc=None):
        if inc is None:
            self._push(eng, lambda e: make(e))
            return None
        h = self.sems[inc]
        self.counts[inc] += 1
        self._push(eng, lambda e: make(e).then_inc(h, 1))
        return self.counts[inc]

    def drain(self, eng):
        self._push(eng, lambda e: e.drain())

    def commit(self, eng, sem_name):
        self.drain(eng)
        if eng in ("vector", "gpsimd"):
            return self.op(eng, lambda e: e.engine_nop(), inc=sem_name)
        return self.op(eng, lambda e: e.nop(), inc=sem_name)

    def dma(self, eng, out, in_, sem_name):
        return self.dma_group(eng, [(out, in_)], sem_name)

    def dma_group(self, eng, pairs, sem_name):
        pool = self.dma_pools[eng]
        idx = self.dma_rot[eng] % len(pool)
        self.dma_rot[eng] += 1
        sn = pool[idx]
        self.wait(eng, sn, self.counts[sn])
        h = self.sems[sn]
        for (out, in_) in pairs:
            self.counts[sn] += 16
            self._push(eng, lambda e, o=out, i=in_: e.dma_start(out=o, in_=i)
                       .then_inc(h, 16))
        return (sn, self.counts[sn])

    def wait_tok(self, eng, tok):
        self.wait(eng, tok[0], tok[1])

    def val(self, sem_name):
        return self.counts[sem_name]

    def barrier(self):
        for e in ENGS:
            for sn in list(self.counts):
                self.wait(e, sn, self.counts[sn])


def ap_of(handle, off, dims):
    return bass.AP(handle, off, dims)



def build_program():
    s = S
    nt = s // TW

    nc = bass.Bass("TRN2", target_bir_lowering=False)
    kb = KB(nc)

    xT = nc.dram_tensor("xT", [D, s], F32, kind="ExternalInput").ap()
    w = {}
    for name, shape in [
        ("g_wq", [D, D]), ("g_kv_down", [D, LAT]), ("g_k_up", [LAT, LAT]),
        ("g_v_up", [LAT, LAT]), ("g_wo", [D, D]),
        ("l_wq", [D, D]), ("l_wk", [D, LAT]), ("l_wv", [D, LAT]), ("l_wo", [D, D]),
        ("w_gate", [D, INNER]), ("w_up", [D, INNER]), ("w_down", [INNER, D]),
        ("norm1_w", [128, NC]), ("norm2_w", [128, NC]),
    ]:
        w[name] = nc.dram_tensor(name, shape, F32, kind="ExternalInput").ap()
    tab = {}
    for name, shape, dt in [
        ("Cg", [128, s], F32), ("Sg", [128, s], F32),
        ("Cl", [128, s], F32), ("Sl", [128, s], F32),
        ("mask_diag", [128, 128], F32), ("mask_su", [128, 128], F32),
        ("swap", [128, 128], F32), ("onesq", [128, 128], F32),
        ("ind2", [128, 2], F32), ("ind2T", [2, 128], F32),
        ("ones16", [128, 1], F16),
    ]:
        tab[name] = nc.dram_tensor(name, shape, dt, kind="ExternalInput").ap()
    outT = nc.dram_tensor("outT", [D, s], F32, kind="ExternalOutput").ap()
    dbgT = None
    if DEBUG_STAGE is not None:
        dbgT = nc.dram_tensor("dbgT", [D, s], F32, kind="ExternalOutput").ap()

    xn_scr = nc.dram_tensor("xn_scr", [D, s], F32).ap()
    s_scr = nc.dram_tensor("s_scr", [D, s], F32).ap()
    wgq_scr = nc.dram_tensor("wgq_scr", [D, INNER], F16).ap()
    wuq_scr = nc.dram_tensor("wuq_scr", [D, INNER], F16).ap()
    wdq_scr = nc.dram_tensor("wdq_scr", [INNER, D], F16).ap()

    for sn in ("T", "V", "SC", "G"):
        kb.sem(sn)
    kb.dma_pools = {"sync": [], "scalar": [], "gpsimd": []}
    kb.dma_rot = {"sync": 0, "scalar": 0, "gpsimd": 0}
    for q, pref in (("sync", "LW"), ("scalar", "LS"), ("gpsimd", "LG")):
        for i in range(4):
            sn = f"{pref}{i}"
            kb.sem(sn)
            kb.dma_pools[q].append(sn)

    consts = kb.sbuf("consts", [128, 16], F32)
    rows = kb.sbuf("rows", [128, 3 * TW], F32)
    tmp1 = kb.sbuf("tmp1", [128, TW], F32)
    tmp2 = kb.sbuf("tmp2", [128, TW], F32)
    tmpsq = kb.sbuf("tmpsq", [128, TW], F32)
    Pt = kb.sbuf("Pt", [128, 4 * TW], F32)
    cs = kb.sbuf("cs_sb", [128, 72], F32)
    mw = kb.sbuf("mw_sb", [128, 4], F32)
    tb = {}
    for k in ("mask_diag", "mask_su", "swap", "onesq", "ind2", "ind2T"):
        tb[k] = kb.sbuf("tb_" + k, list(tab[k].tensor.shape), F32)
    tb["ones16"] = kb.sbuf("tb_ones16", [128, 1], F16)

    pbh = [kb.psum(f"pb{i}", [128, 512]) for i in range(8)]
    pb = [t[:, :] for t in pbh]

    st = dict(nc=nc, kb=kb, s=s, nt=nt, w=w, tab=tab, tb=tb,
              consts=consts, rows=rows, tmp1=tmp1, tmp2=tmp2, tmpsq=tmpsq,
              Pt=Pt, cs=cs, mw=mw, pb=pb, pbh=pbh,
              xT=xT, outT=outT, dbgT=dbgT,
              xn_scr=xn_scr, s_scr=s_scr,
              wgq_scr=wgq_scr, wuq_scr=wuq_scr, wdq_scr=wdq_scr)

    emit_all(st)

    blk = nc.Block()
    block = blk.__enter__()
    kb._ctx.append(blk)
    for eng_name in ENGS:
        fns = kb.steps[eng_name]

        def body(e, fns=fns):
            for f in fns:
                f(e)
        getattr(block, eng_name)(body)
    kb.close()
    return nc


def R(st, k, n=TW):
    return st["rows"][0:1, k * TW:k * TW + n]


def R2(st, k, n=TW):
    return st["rows"][0:2, k * TW:k * TW + n]


def R64(st, k, n=TW):
    return st["rows"][64:65, k * TW:k * TW + n]


def R96(st, k, n=TW):
    return st["rows"][96:97, k * TW:k * TW + n]


def emit_all(st):
    kb = st["kb"]

    st["tabs_tok"] = kb.dma_group(
        "sync", [(st["tb"][k][:, :], st["tab"][k][:, :])
                 for k in ("mask_diag", "mask_su", "swap", "onesq",
                           "ind2", "ind2T", "ones16")], None)

    c = st["consts"]
    kb.op("vector", lambda e: e.memset(c[:, 0:1], -8.0))
    kb.op("vector", lambda e: e.memset(c[:, 1:2], 1e-5))
    kb.op("vector", lambda e: e.memset(c[:, 2:3], 1e-6))
    kb.op("vector", lambda e: e.memset(c[:, 3:4], float(np.log(127.0))))
    kb.op("vector", lambda e: e.memset(c[:, 4:5], MAGIC))
    kb.commit("vector", "V")
    st["consts_done"] = kb.val("V")

    kb.push_scope()
    st["xn"] = kb.sbuf("xn", [128, NC * st["s"]], F32, scoped=True)
    st["xst"] = kb.sbuf("xst", [128, NC * TW], F32, scoped=True)
    st["KGh"] = kb.sbuf("KGh", [128, 2 * st["s"]], BF16, scoped=True)
    st["KGl"] = kb.sbuf("KGl", [128, 2 * st["s"]], BF16, scoped=True)
    st["VG"] = kb.sbuf("VG", [128, (st["s"] // 128) * 4 * 65], F32, scoped=True)
    st["QTh"] = kb.sbuf("QTh", [128, NC * TW], BF16, scoped=True)
    st["QTl"] = kb.sbuf("QTl", [128, NC * TW], BF16, scoped=True)
    st["AO"] = kb.sbuf("AO", [128, NC * TW], F32, scoped=True)
    st["lat"] = kb.sbuf("lat", [128, 2 * st["s"]], F32, scoped=True)
    st["wst"] = kb.sbuf("wst", [128, 2 * 1024], F32, scoped=True)
    st["CSc"] = kb.sbuf("CSc", [128, st["s"]], F32, scoped=True)
    st["CSs"] = kb.sbuf("CSs", [128, st["s"]], F32, scoped=True)

    emit_ln(st)
    if DEBUG_STAGE == "xn_plus":
        rms_scale_bcast(st, 0)
        emit_debug_xn(st, st["xn"])
        emit_tail(st)
        return
    if DEBUG_STAGE == "xn_barrier":
        kb.barrier()
        emit_debug_xn(st, st["xn"])
        emit_tail(st)
        return
    if DEBUG_STAGE == "wq":
        kb.barrier()
        if not KEEP_SCOPES:
            kb.pop_scope()
        kb.push_scope()
        st["wf"] = kb.sbuf("wf", [128, 2 * INNER], F32, scoped=True)
        st["wq16"] = kb.sbuf("wq16", [128, 2 * INNER], F16, scoped=True)
        st["sgn2"] = kb.sbuf("sgn2", [128, 2 * INNER], F16, scoped=True)
        emit_wquant(st)
        kb.barrier()
        kb.guard("gpsimd", "V", "T", "SC")
        kb.dma("gpsimd", st["dbgT"][0:128, 0:72], st["cs"][:, :], None)
        kb.dma("gpsimd", st["outT"][0:128, 0:72], st["cs"][:, :], None)
        emit_tail(st)
        return
    if DEBUG_STAGE == "xn":
        emit_debug_xn(st, st["xn"])
        emit_tail(st)
        return
    emit_attn(st, glob=True)
    if DEBUG_STAGE in ("q0", "sdpa1"):
        kb.barrier()
        kb.guard("gpsimd", "V", "T", "SC")
        if DEBUG_STAGE == "q0":
            pairs = [(st["outT"][ch * 128:(ch + 1) * 128, 0:TW],
                      st["QTh"][:, ch * TW:(ch + 1) * TW]) for ch in range(NC)]
        else:
            pairs = [(st["outT"][0:128, 0:TW], st["AO"][:, 0:TW])]
        kb.dma_group("gpsimd", pairs, None)
        emit_tail(st)
        return
    if DEBUG_STAGE in ("norm", "norm_nodma", "rms0", "rmsA", "rmsB"):
        kb.barrier()
        emit_debug_xn(st, st["xn"])
        emit_tail(st)
        return
    if DEBUG_STAGE in ("lat", "k"):
        kb.barrier()
        kb.guard("gpsimd", "V", "T", "SC")
        if DEBUG_STAGE == "lat":
            pairs = [(st["outT"][0:128, :], st["lat"][:, 0:st["s"]]),
                     (st["outT"][128:256, :], st["lat"][:, st["s"]:2 * st["s"]])]
        else:
            pairs = [(st["outT"][0:128, :], st["KGh"][:, 0:st["s"]]),
                     (st["outT"][128:256, :], st["KGh"][:, st["s"]:2 * st["s"]])]
        kb.dma_group("gpsimd", pairs, None)
        emit_tail(st)
        return
    if DEBUG_STAGE in ("kv", "q0"):
        kb.barrier()
        kb.guard("gpsimd", "V", "T", "SC")
        if DEBUG_STAGE == "kv":
            pairs = [(st["outT"][0:128, :], st["KGh"][:, 0:st["s"]]),
                     (st["outT"][128:256, :], st["KGh"][:, st["s"]:2 * st["s"]])]
            vgf = (st["s"] // 128) * 260
            off = 0
            row = 256
            while off < vgf:
                wdt = min(st["s"], vgf - off)
                pairs.append((st["outT"][row:row + 128, 0:wdt],
                              st["VG"][:, off:off + wdt]))
                off += wdt
                row += 128
        else:
            pairs = [(st["outT"][ch * 128:(ch + 1) * 128, 0:TW],
                      st["QTh"][:, ch * TW:(ch + 1) * TW]) for ch in range(NC)]
        kb.dma_group("gpsimd", pairs, None)
        emit_tail(st)
        return
    if DEBUG_STAGE == "s1":
        emit_debug_xn(st, st["xn"])
        emit_tail(st)
        return
    emit_attn(st, glob=False)
    emit_store_s(st)
    if DEBUG_STAGE == "s":
        emit_debug_xn(st, st["xn"])
        emit_tail(st)
        return
    kb.barrier()
    if not KEEP_SCOPES:
        kb.pop_scope()

    kb.push_scope()
    st["wf"] = kb.sbuf("wf", [128, 2 * INNER], F32, scoped=True)
    st["wq16"] = kb.sbuf("wq16", [128, 2 * INNER], F16, scoped=True)
    st["sgn2"] = kb.sbuf("sgn2", [128, 2 * INNER], F16, scoped=True)
    emit_wquant(st)
    kb.barrier()
    if not KEEP_SCOPES:
        kb.pop_scope()

    kb.push_scope()
    st["arena"] = kb.sbuf("arena", [128, 2 * NC * TW], F32, scoped=True)
    st["xoc"] = kb.sbuf("xoc", [128, 2 * 512], F32, scoped=True)
    st["z1"] = kb.sbuf("z1", [128, NC * TW], F16, scoped=True)
    st["z2"] = kb.sbuf("z2", [128, 32 * TW], F16, scoped=True)
    st["prod"] = kb.sbuf("prod", [128, 32 * TW], F32, scoped=True)
    st["fw"] = kb.sbuf("fw", [128, 3 * NC * TW], F16, scoped=True)
    st["sxb"] = kb.sbuf("sxb", [128, TW], F32, scoped=True)
    st["tgb"] = kb.sbuf("tgb", [128, TW], F32, scoped=True)
    st["tub"] = kb.sbuf("tub", [128, TW], F32, scoped=True)
    st["tdb"] = kb.sbuf("tdb", [128, TW], F32, scoped=True)
    st["tmp4"] = kb.sbuf("tmp4", [128, TW], F32, scoped=True)
    emit_ffn(st)
    kb.pop_scope()

    emit_tail(st)


def emit_tail(st):
    kb = st["kb"]
    for q in ("sync", "scalar", "gpsimd"):
        for sn in kb.dma_pools[q]:
            kb.wait(q, sn, kb.val(sn))


def emit_debug_xn(st, src):
    kb, s = st["kb"], st["s"]
    kb.guard("gpsimd", "V", "T", "SC")
    pairs = []
    for ch in range(NC):
        pairs.append((st["outT"][ch * 128:(ch + 1) * 128, :],
                      src[:, ch * s:(ch + 1) * s]))
        pairs.append((st["dbgT"][ch * 128:(ch + 1) * 128, :],
                      src[:, ch * s:(ch + 1) * s]))
    kb.dma_group("gpsimd", pairs, None)



def emit_ln(st):
    kb, s, nt = st["kb"], st["s"], st["nt"]
    xn, xst, tmpsq, c, pb, tb = (st["xn"], st["xst"], st["tmpsq"], st["consts"],
                                 st["pb"], st["tb"])

    kb.wait_tok("scalar", st["tabs_tok"])
    kb.wait_tok("tensor", st["tabs_tok"])
    kb.wait("vector", "V", st["consts_done"])

    for t in range(nt):
        t0 = t * TW
        kb.guard("sync", "T", "V", "SC")
        xl = kb.dma_group(
            "sync", [(xst[:, ch * TW:(ch + 1) * TW],
                      st["xT"][ch * 128:(ch + 1) * 128, t0:t0 + TW])
                     for ch in range(NC)], None)
        kb.wait_tok("scalar", xl)
        kb.wait_tok("tensor", xl)

        kb.guard("scalar", "T")
        tvals = {}
        for ch in range(NC):
            xc = xst[:, ch * TW:(ch + 1) * TW]
            buf = tmpsq if ch % 2 == 0 else st["tmp2"]
            if ch >= 2:
                kb.wait("scalar", "T", tvals[ch - 2])
            sq = kb.op("scalar", lambda e, xc=xc, buf=buf: e.activation(
                buf[:, :], xc, AF.Square), inc="SC")
            kb.op("tensor", lambda e, xc=xc, f=(ch == 0), g=(ch == NC - 1):
                  e.matmul(pb[0][0:1, :], tb["onesq"][:, 0:1], xc,
                           start=f, stop=g))
            kb.wait("tensor", "SC", sq)
            tvals[ch] = kb.op("tensor", lambda e, buf=buf, f=(ch == 0),
                              g=(ch == NC - 1): e.matmul(
                pb[1][0:1, :], tb["onesq"][:, 0:1], buf[:, :],
                start=f, stop=g), inc="T")
        red = kb.val("T")

        kb.wait("vector", "T", red)
        kb.op("vector", lambda e: e.tensor_scalar(
            R(st, 0), pb[0][0:1, :], 1.0 / D, None, ALU.mult))
        kb.drain("vector")
        kb.op("vector", lambda e: e.tensor_tensor(
            R(st, 1), R(st, 0), R(st, 0), ALU.mult))
        kb.drain("vector")
        vv = kb.commit("vector", "V")
        kb.op("vector", lambda e: e.scalar_tensor_tensor(
            R(st, 1), pb[1][0:1, :], 1.0 / D, R(st, 1), ALU.mult, ALU.subtract))
        vv = kb.commit("vector", "V")
        kb.wait("scalar", "V", vv)
        kb.op("scalar", lambda e: e.activation(
            R(st, 2), R(st, 1), AF.Ln, bias=c[0:1, 1:2]))
        kb.drain("scalar")
        sq2 = kb.op("scalar", lambda e: e.activation(
            R(st, 1), R(st, 2), AF.Exp, bias=0.0, scale=-0.5), inc="SC")
        kb.wait("vector", "SC", sq2)
        kb.op("vector", lambda e: e.tensor_tensor(
            R(st, 2), R(st, 0), R(st, 1), ALU.mult))
        vv = kb.commit("vector", "V")

        kb.wait("tensor", "V", vv)
        kb.op("tensor", lambda e: e.matmul(
            pb[2], tb["onesq"][0:1, :], R(st, 1), start=True, stop=True))
        tbc = kb.op("tensor", lambda e: e.matmul(
            pb[3], tb["onesq"][0:1, :], R(st, 2), start=True, stop=True), inc="T")

        kb.wait("vector", "T", tbc)
        for ch in range(NC):
            xc = xst[:, ch * TW:(ch + 1) * TW]
            xnc = xn[:, ch * s + t0: ch * s + t0 + TW]
            kb.op("vector", lambda e, xc=xc, xnc=xnc: e.tensor_tensor(
                xnc, xc, pb[2], ALU.mult))
        kb.drain("vector")
        for ch in range(NC):
            xnc = xn[:, ch * s + t0: ch * s + t0 + TW]
            kb.op("vector", lambda e, xnc=xnc: e.tensor_tensor(
                xnc, xnc, pb[3], ALU.subtract))
        vv = kb.commit("vector", "V")
        kb.wait("gpsimd", "V", vv)
        kb.dma_group("gpsimd",
                     [(st["xn_scr"][ch * 128:(ch + 1) * 128, t0:t0 + TW],
                       xn[:, ch * s + t0: ch * s + t0 + TW])
                      for ch in range(NC)], None)
        kb.wait("tensor", "V", vv)



def rms_scale_bcast(st, t0):
    kb, s = st["kb"], st["s"]
    pb, tb, tmpsq, c = st["pb"], st["tb"], st["tmpsq"], st["consts"]
    kb.guard("scalar", "T")
    tvals = {}
    for ch in range(NC):
        xc = st["xn"][:, ch * s + t0: ch * s + t0 + TW]
        buf = tmpsq if ch % 2 == 0 else st["tmp2"]
        if ch >= 2:
            kb.wait("scalar", "T", tvals[ch - 2])
        sq = kb.op("scalar", lambda e, xc=xc, buf=buf: e.activation(
            buf[:, :], xc, AF.Square), inc="SC")
        kb.wait("tensor", "SC", sq)
        tvals[ch] = kb.op("tensor", lambda e, buf=buf, f=(ch == 0),
                          g=(ch == NC - 1): e.matmul(
            pb[0][0:1, :], tb["onesq"][:, 0:1], buf[:, :], start=f, stop=g),
            inc="T")
    red = kb.val("T")
    if DEBUG_STAGE == "rmsA":
        return red
    kb.wait("vector", "T", red)
    kb.op("vector", lambda e: e.tensor_scalar(
        R(st, 0), pb[0][0:1, :], 1.0 / D, None, ALU.mult))
    vv = kb.commit("vector", "V")
    kb.wait("scalar", "V", vv)
    kb.op("scalar", lambda e: e.activation(
        R(st, 2), R(st, 0), AF.Ln, bias=c[0:1, 2:3]))
    kb.drain("scalar")
    sq2 = kb.op("scalar", lambda e: e.activation(
        R(st, 0), R(st, 2), AF.Exp, bias=0.0, scale=-0.5), inc="SC")
    if DEBUG_STAGE == "rmsB":
        return kb.val("T")
    kb.wait("tensor", "SC", sq2)
    return kb.op("tensor", lambda e: e.matmul(
        pb[1], tb["onesq"][0:1, :], R(st, 0), start=True, stop=True), inc="T")


def norm_tile(st, t0, dst):
    kb, s = st["kb"], st["s"]
    c = st["consts"]
    for ch in range(NC):
        src_c = st["xn"][:, ch * s + t0: ch * s + t0 + TW]
        dst_c = dst[:, ch * TW:(ch + 1) * TW]
        kb.op("vector", lambda e, a=src_c, b=dst_c, ch=ch: e.scalar_tensor_tensor(
            b, a, c[:, 8 + ch:9 + ch], st["pb"][1], ALU.mult, ALU.mult))
    return kb.commit("vector", "V")


def rope_qknorm(st, src_psum, t0, out_region, ncols=TW, deps=None,
                split_out=None):
    kb = st["kb"]
    kb.op("vector", lambda e, o=out_region: e.tensor_copy(o, src_psum))
    vv = kb.commit("vector", "V")
    if deps is not None:
        deps = dict(deps, chunk_v=vv)
    return rope_qknorm_sb(st, t0, out_region, ncols, deps=deps,
                          split_out=split_out)


def rope_qknorm_sb(st, t0, chunk, ncols=TW, deps=None, split_out=None):
    kb = st["kb"]
    pb, tb, c = st["pb"], st["tb"], st["consts"]
    tmp1, tmp2, tmpsq = st["tmp1"], st["tmp2"], st["tmpsq"]
    CSc, CSs = st["CSc"], st["CSs"]

    vv = kb.val("V")
    if deps is None:
        kb.guard("scalar", "T", "V")
    else:
        if deps.get("t_ind2") is not None:
            kb.wait("scalar", "T", deps["t_ind2"])
        kb.wait("scalar", "V", deps["chunk_v"])
    sq = kb.op("scalar", lambda e, ch=chunk: e.activation(
        tmpsq[:, 0:ncols], ch, AF.Square), inc="SC")
    if deps is not None and deps.get("v_final") is not None:
        kb.wait("tensor", "V", deps["v_final"])
    kb.wait("tensor", "V", vv if deps is None else deps["chunk_v"])
    kb.op("tensor", lambda e, ch=chunk: e.matmul(
        pb[6][:, 0:ncols], tb["swap"][:, :], ch, start=True, stop=True))
    kb.wait("tensor", "SC", sq)
    tt = kb.op("tensor", lambda e: e.matmul(
        pb[7][0:2, 0:ncols], tb["ind2"][:, :], tmpsq[:, 0:ncols],
        start=True, stop=True), inc="T")
    kb.wait("vector", "T", tt)
    kb.op("vector", lambda e: e.tensor_scalar(
        R2(st, 1, ncols), pb[7][0:2, 0:ncols], 1.0 / HD, None, ALU.mult))
    vv = kb.commit("vector", "V")
    kb.wait("scalar", "V", vv)
    kb.op("scalar", lambda e: e.activation(
        R2(st, 2, ncols), R2(st, 1, ncols), AF.Ln, bias=c[0:2, 2:3]))
    kb.drain("scalar")
    sq2 = kb.op("scalar", lambda e: e.activation(
        R2(st, 1, ncols), R2(st, 2, ncols), AF.Exp, bias=0.0, scale=-0.5),
        inc="SC")
    kb.wait("tensor", "SC", sq2)
    tt = kb.op("tensor", lambda e: e.matmul(
        pb[7][:, 0:ncols], tb["ind2T"][0:2, :], R2(st, 1, ncols),
        start=True, stop=True), inc="T")
    Ct = CSc[:, t0:t0 + ncols]
    Stt = CSs[:, t0:t0 + ncols]
    kb.op("vector", lambda e, ch=chunk, Ct=Ct: e.tensor_tensor(
        tmp1[:, 0:ncols], ch, Ct, ALU.mult))
    kb.wait("vector", "T", tt)
    kb.op("vector", lambda e, Stt=Stt: e.tensor_tensor(
        tmp2[:, 0:ncols], pb[6][:, 0:ncols], Stt, ALU.mult))
    kb.drain("vector")
    kb.op("vector", lambda e: e.tensor_tensor(
        tmp1[:, 0:ncols], tmp1[:, 0:ncols], tmp2[:, 0:ncols], ALU.add))
    kb.drain("vector")
    kb.op("vector", lambda e, ch=chunk: e.tensor_tensor(
        ch, tmp1[:, 0:ncols], pb[7][:, 0:ncols], ALU.mult))
    vfin = kb.commit("vector", "V")
    if split_out is not None:
        hi, lo = split_out
        kb.op("vector", lambda e, ch=chunk, hi=hi: e.tensor_copy(hi, ch))
        kb.drain("vector")
        kb.op("vector", lambda e, ch=chunk, hi=hi, lo=lo: e.tensor_tensor(
            lo, ch, hi, ALU.subtract))
        spv = kb.commit("vector", "V")
        return {"t_ind2": tt, "v_final": vfin, "split_v": spv,
                "chunk_v": (deps["chunk_v"] if deps is not None else vfin)}
    return {"t_ind2": tt, "v_final": vfin,
            "chunk_v": (deps["chunk_v"] if deps is not None else vfin)}


def band_load(st, dst, src, n=0):
    kb = st["kb"]
    kb.guard("sync", "T")
    return kb.dma("sync", dst, src, None)


def emit_attn(st, glob):
    kb, s, nt = st["kb"], st["s"], st["nt"]
    w, tb, c, pb = st["w"], st["tb"], st["consts"], st["pb"]
    xn, VG, AO, lat, wst, xst = (st["xn"], st["VG"], st["AO"], st["lat"],
                                 st["wst"], st["xst"])
    KGh, KGl, QTh, QTl = st["KGh"], st["KGl"], st["QTh"], st["QTl"]

    wq = w["g_wq" if glob else "l_wq"]
    wo = w["g_wo" if glob else "l_wo"]
    nw = w["norm1_w" if glob else "norm2_w"]

    kb.guard("sync", "V")
    tl = kb.dma_group("sync", [
        (st["CSc"][:, :], st["tab"]["Cg" if glob else "Cl"][:, :]),
        (st["CSs"][:, :], st["tab"]["Sg" if glob else "Sl"][:, :]),
        (ap_of(st["consts"], 8, [[16, 128], [1, NC]]), nw[:, :]),
    ], None)
    for e in ("vector", "tensor", "scalar"):
        kb.wait_tok(e, tl)

    if DEBUG_STAGE in ("xn_pre", "xn_pre2", "xn_pre3", "xn_pre4", "xn_pre5"):
        return
    for t in range(nt):
        t0 = t * TW
        bc = rms_scale_bcast(st, t0)
        kb.wait("vector", "T", bc)
        kb.guard("vector", "T")
        if DEBUG_STAGE in ("rms0", "rmsA", "rmsB"):
            break
        nt_done = norm_tile(st, t0, xst)
        if DEBUG_STAGE in ("norm", "norm_nodma"):
            break

        if glob:
            kv_down = w["g_kv_down"]
            kup = w["g_k_up"]
            vup = w["g_v_up"]
            kb.wait("tensor", "V", nt_done)
            toksA = {}
            tA = {}
            kb.guard("sync", "T")
            toksA[0] = kb.dma("sync", wst[:, 0:LAT], kv_down[0:128, :], None)
            for kc in range(NC):
                if kc + 1 < NC:
                    if kc >= 1:
                        kb.wait("sync", "T", tA[kc - 1])
                    toksA[kc + 1] = kb.dma(
                        "sync",
                        wst[:, ((kc + 1) % 2) * 1024:((kc + 1) % 2) * 1024 + LAT],
                        kv_down[(kc + 1) * 128:(kc + 2) * 128, :], None)
                kb.wait_tok("tensor", toksA[kc])
                buf = wst[:, (kc % 2) * 1024:(kc % 2) * 1024 + LAT]
                for lc in range(2):
                    kb.op("tensor", lambda e, buf=buf, lc=lc, kc=kc, f=(kc == 0),
                          g=(kc == NC - 1):
                          e.matmul(pb[4 + lc],
                                   buf[:, lc * 128:(lc + 1) * 128],
                                   xst[:, kc * TW:(kc + 1) * TW],
                                   start=f, stop=g),
                          inc=("T" if lc == 1 else None))
                tA[kc] = kb.val("T")
            tt = kb.val("T")
            kb.wait("sync", "T", tt)
            wlk = kb.dma_group("sync", [
                (wst[:, 0:LAT], kup[0:128, :]),
                (wst[:, 1024:1024 + LAT], kup[128:256, :]),
                (wst[:, 256:256 + LAT], vup[0:128, :]),
                (wst[:, 1280:1280 + LAT], vup[128:256, :])], None)
            kb.wait("vector", "T", tt)
            kb.guard("vector", "T")
            for lc in range(2):
                kb.op("vector", lambda e, lc=lc, t0=t0: e.tensor_copy(
                    lat[:, lc * s + t0: lc * s + t0 + TW], pb[4 + lc]))
            lat_done = kb.commit("vector", "V")
            if DEBUG_STAGE == "lat":
                continue

            kb.wait_tok("tensor", wlk)
            kb.wait("tensor", "V", lat_done)
            for oc in range(2):
                for lc in range(2):
                    kb.op("tensor", lambda e, oc=oc, lc=lc, t0=t0, f=(lc == 0),
                          g=(lc == 1): e.matmul(
                        pb[4 + oc],
                        wst[:, lc * 1024 + oc * 128: lc * 1024 + (oc + 1) * 128],
                        lat[:, lc * s + t0: lc * s + t0 + TW], start=f, stop=g),
                        inc=("T" if lc == 1 else None))
            kb.wait("vector", "T", kb.val("T"))
            kdeps = {"t_ind2": None, "v_final": None, "chunk_v": None}
            for oc in range(2):
                kdeps = rope_qknorm(
                    st, pb[4 + oc], t0, AO[:, oc * TW:(oc + 1) * TW],
                    deps=kdeps,
                    split_out=(KGh[:, oc * s + t0: oc * s + t0 + TW],
                               KGl[:, oc * s + t0: oc * s + t0 + TW]))
            kb.wait("tensor", "V", kb.val("V"))

            if DEBUG_STAGE == "k":
                continue
            emit_v_proj(st, t0, nk=2,
                        lhs_fn=lambda kc, tc4, t0=t0: lat[:, kc * s + t0 + tc4 * 128:
                                                          kc * s + t0 + (tc4 + 1) * 128],
                        rhs_fn=lambda kc: wst[:, kc * 1024 + 256: kc * 1024 + 256 + LAT])
        else:
            lwk = w["l_wk"]
            kb.wait("tensor", "V", nt_done)
            toksA = {}
            tA = {}
            kb.guard("sync", "T")
            toksA[0] = kb.dma("sync", wst[:, 0:LAT], lwk[0:128, :], None)
            for kc in range(NC):
                if kc + 1 < NC:
                    if kc >= 1:
                        kb.wait("sync", "T", tA[kc - 1])
                    toksA[kc + 1] = kb.dma(
                        "sync",
                        wst[:, ((kc + 1) % 2) * 1024:((kc + 1) % 2) * 1024 + LAT],
                        lwk[(kc + 1) * 128:(kc + 2) * 128, :], None)
                kb.wait_tok("tensor", toksA[kc])
                buf = wst[:, (kc % 2) * 1024:(kc % 2) * 1024 + LAT]
                for oc in range(2):
                    kb.op("tensor", lambda e, buf=buf, oc=oc, kc=kc, f=(kc == 0),
                          g=(kc == NC - 1):
                          e.matmul(pb[4 + oc], buf[:, oc * 128:(oc + 1) * 128],
                                   xst[:, kc * TW:(kc + 1) * TW], start=f, stop=g),
                          inc=("T" if oc == 1 else None))
                tA[kc] = kb.val("T")
            tt = kb.val("T")
            kb.wait("vector", "T", tt)
            kdeps = {"t_ind2": None, "v_final": None, "chunk_v": None}
            for oc in range(2):
                kdeps = rope_qknorm(
                    st, pb[4 + oc], t0, AO[:, oc * TW:(oc + 1) * TW],
                    deps=kdeps,
                    split_out=(KGh[:, oc * s + t0: oc * s + t0 + TW],
                               KGl[:, oc * s + t0: oc * s + t0 + TW]))
            kb.wait("tensor", "V", kb.val("V"))

            emit_v_proj_local(st, t0, w["l_wv"])

    if DEBUG_STAGE in ("lat", "k", "kv") and glob:
        return
    for t in range(nt):
        t0 = t * TW
        bc = rms_scale_bcast(st, t0)
        kb.wait("vector", "T", bc)
        kb.guard("vector", "T")
        nt_done = norm_tile(st, t0, xst)
        kb.wait("tensor", "V", nt_done)

        t_after = {}
        toks = {}
        kb.guard("gpsimd", "T")
        toks[0] = kb.dma("gpsimd", wst[:, 0:1024], wq[0:128, :], None)
        for kc in range(NC):
            if kc + 1 < NC:
                if kc >= 1:
                    kb.wait("gpsimd", "T", t_after[kc - 1])
                toks[kc + 1] = kb.dma(
                    "gpsimd", wst[:, ((kc + 1) % 2) * 1024:((kc + 1) % 2) * 1024 + 1024],
                    wq[(kc + 1) * 128:(kc + 2) * 128, :], None)
            kb.wait_tok("tensor", toks[kc])
            buf = wst[:, (kc % 2) * 1024:(kc % 2) * 1024 + 1024]
            for oc in range(NC):
                kb.op("tensor", lambda e, buf=buf, oc=oc, kc=kc, f=(kc == 0),
                      g=(kc == NC - 1):
                      e.matmul(pb[oc], buf[:, oc * 128:(oc + 1) * 128],
                               xst[:, kc * TW:(kc + 1) * TW], start=f, stop=g),
                      inc=("T" if oc == NC - 1 else None))
            t_after[kc] = kb.val("T")
        kb.wait("vector", "T", kb.val("T"))
        kb.guard("sync", "T")
        wo_toks = {0: kb.dma("sync", wst[:, 0:1024], wo[0:128, :], None),
                   1: kb.dma("sync", wst[:, 1024:2048], wo[128:256, :], None)}
        for oc in range(NC):
            kb.op("vector", lambda e, oc=oc: e.tensor_copy(
                AO[:, oc * TW:(oc + 1) * TW], pb[oc]))
        qv = kb.commit("vector", "V")
        st["sdpa_sbv"] = {b: qv for b in (0, 1, 6, 7)}
        st["sdpa_pvv"] = {j: qv for j in range(4)}
        deps = {"t_ind2": None, "v_final": None, "chunk_v": qv}
        for oc in range(4):
            deps = rope_qknorm_sb(
                st, t0, AO[:, oc * TW:(oc + 1) * TW], deps=deps,
                split_out=(QTh[:, oc * TW:(oc + 1) * TW],
                           QTl[:, oc * TW:(oc + 1) * TW]))
        emit_sdpa_kv(st, glob, t, 0, deep=False, qready=deps["split_v"])
        for oc in range(4, NC):
            deps = rope_qknorm_sb(
                st, t0, AO[:, oc * TW:(oc + 1) * TW], deps=deps,
                split_out=(QTh[:, oc * TW:(oc + 1) * TW],
                           QTl[:, oc * TW:(oc + 1) * TW]))
        st["sdpa_sbv"][6] = st["sdpa_sbv"][7] = deps["v_final"]
        emit_sdpa_kv(st, glob, t, 1, qready=deps["split_v"])
        emit_sdpa_kv(st, glob, t, 2)
        emit_sdpa_kv(st, glob, t, 3)

        t_after = {}
        toks = wo_toks
        for tok in st.pop("ao_lift", []):
            kb.wait_tok("tensor", tok)
        kb.guard("tensor", "V")
        for kc in range(NC):
            kb.wait_tok("tensor", toks[kc])
            buf = wst[:, (kc % 2) * 1024:(kc % 2) * 1024 + 1024]
            for oc in range(NC):
                kb.op("tensor", lambda e, buf=buf, oc=oc, kc=kc, f=(kc == 0),
                      g=(kc == NC - 1):
                      e.matmul(pb[oc], buf[:, oc * 128:(oc + 1) * 128],
                               AO[:, kc * TW:(kc + 1) * TW], start=f, stop=g),
                      inc=("T" if oc == NC - 1 else None))
            t_after[kc] = kb.val("T")
            if kc + 2 < NC:
                kb.wait("sync", "T", t_after[kc])
                toks[kc + 2] = kb.dma(
                    "sync", wst[:, (kc % 2) * 1024:(kc % 2) * 1024 + 1024],
                    wo[(kc + 2) * 128:(kc + 3) * 128, :], None)
        tt = kb.val("T")
        kb.wait("vector", "T", tt)
        for oc in range(NC):
            dstc = xn[:, oc * s + t0: oc * s + t0 + TW]
            kb.op("vector", lambda e, oc=oc, dstc=dstc: e.tensor_tensor(
                dstc, dstc, pb[oc], ALU.add))
        vv = kb.commit("vector", "V")
        kb.wait("tensor", "V", vv)


def emit_sdpa_kv(st, glob, t, kv, deep=True, qready=None):
    kb, s = st["kb"], st["s"]
    pb, pbh, tb, Pt, VG, AO = (st["pb"], st["pbh"], st["tb"], st["Pt"],
                               st["VG"], st["AO"])
    KGh, KGl, QTh, QTl = st["KGh"], st["KGl"], st["QTh"], st["QTl"]
    Pt2 = st["lat"]
    c = st["consts"]
    t0 = t * TW
    nblk = 4 * 65
    vg_free = (s // 128) * nblk
    rb = 64 * (kv % 2)
    kcn = kv // 2
    sbanks = [0, 1, 6, 7] if deep else [0, 1, 0, 1]
    sbv = st["sdpa_sbv"]
    pvv = st["sdpa_pvv"]
    lastpv = st.setdefault("sdpa_lastpv", {})

    if qready is not None:
        kb.wait("tensor", "V", qready)

    pos = []
    for qc, (ha, hb) in enumerate(Q_CHUNK_HEADS):
        if ha // 4 == kv:
            pos.append((qc, 0))
        if hb // 4 == kv:
            pos.append((qc, 1))
    assert len(pos) == 4

    chunks = []
    if glob:
        for j in range(0, t * 4 + 4):
            k0 = j * 128
            qoff = max(0, k0 - t0)
            subs = [(qoff, "d")] if k0 >= t0 else []
            chunks.append((k0, qoff, TW - qoff, subs))
    else:
        for j in range(-2, 4):
            k0 = t0 + j * 128
            if k0 < 0:
                continue
            lo = max(0, j) * 128
            hi = min(4, j + 3) * 128
            subs = []
            if j >= 0:
                subs.append((j * 128, "d"))
            if j + 2 <= 3:
                subs.append(((j + 2) * 128, "u"))
            chunks.append((k0, lo, hi - lo, subs))
    L = len(chunks)

    def pt_ap(j, ci, a, n):
        buf = Pt if ci % 2 == 0 else Pt2
        return buf[:, j * TW + a: j * TW + a + n]

    exp_sc = {}
    pv_t = {}
    for ci, (k0, qoff, N, subs) in enumerate(chunks):
        last = (ci == L - 1)
        lhsV = ap_of(VG, (k0 // 128) * nblk + 65 * kv, [[vg_free, 128], [1, 65]])
        lhsKh = KGh[rb:rb + 64, kcn * s + k0: kcn * s + k0 + 128]
        lhsKl = KGl[rb:rb + 64, kcn * s + k0: kcn * s + k0 + 128]
        for j, (qc, half) in enumerate(pos):
            qb = 64 * half
            scb = pbh[sbanks[j]]
            rQh = QTh[qb:qb + 64, qc * TW + qoff: qc * TW + qoff + N]
            rQl = QTl[qb:qb + 64, qc * TW + qoff: qc * TW + qoff + N]
            if ci == 0:
                v = sbv.get(sbanks[j])
                if v is not None and (deep or j < 2):
                    kb.wait("tensor", "V", v)
            if deep:
                prev = (ci - 1, j)
            else:
                prev = (ci, j - 2) if j >= 2 else (ci - 1, j + 2)
            if prev in exp_sc:
                kb.wait("tensor", "SC", exp_sc[prev])
            kb.op("tensor", lambda e, lhsKh=lhsKh, rQh=rQh, qoff=qoff, N=N,
                  b=scb: e.matmul(b[0:128, qoff:qoff + N], lhsKh, rQh,
                                  start=True, stop=False))
            kb.op("tensor", lambda e, lhsKl=lhsKl, rQh=rQh, qoff=qoff, N=N,
                  b=scb: e.matmul(b[0:128, qoff:qoff + N], lhsKl, rQh,
                                  start=False, stop=False))
            kb.op("tensor", lambda e, lhsKh=lhsKh, rQl=rQl, qoff=qoff, N=N,
                  b=scb: e.matmul(b[0:128, qoff:qoff + N], lhsKh, rQl,
                                  start=False, stop=True), inc="T")
            tt = kb.val("T")
            kb.wait("scalar", "T", tt)
            if (ci - 2, j) in pv_t:
                kb.wait("scalar", "T", pv_t[(ci - 2, j)])
            elif ci < 2 and (j, ci % 2) in lastpv:
                kb.wait("scalar", "T", lastpv[(j, ci % 2)])
            sc = kb.op("scalar", lambda e, qoff=qoff, N=N, b=scb, j=j, ci=ci:
                       e.activation(pt_ap(j, ci, qoff, N),
                                    b[0:128, qoff:qoff + N], AF.Exp,
                                    bias=c[:, 0:1], scale=0.125), inc="SC")
            exp_sc[(ci, j)] = sc
        for j, (qc, half) in enumerate(pos):
            pvb = pbh[2 + j]
            sc = exp_sc[(ci, j)]
            if subs:
                kb.wait("vector", "SC", sc)
                for (soff, kind) in subs:
                    mk = tb["mask_diag"] if kind == "d" else tb["mask_su"]
                    kb.op("vector", lambda e, soff=soff, mk=mk, j=j, ci=ci:
                          e.tensor_tensor(
                              pt_ap(j, ci, soff, 128),
                              pt_ap(j, ci, soff, 128),
                              mk[:, :], ALU.mult))
                mdone = kb.commit("vector", "V")
                kb.wait("tensor", "V", mdone)
            else:
                kb.wait("tensor", "SC", sc)
            if ci == 0:
                v = pvv.get(j)
                if v is not None:
                    kb.wait("tensor", "V", v)
            if glob:
                kb.op("tensor", lambda e, lhsV=lhsV, qoff=qoff, N=N, b=pvb,
                      j=j, ci=ci, f=(ci == 0), g=last: e.matmul(
                          b[0:65, qoff:qoff + N], lhsV,
                          pt_ap(j, ci, qoff, N),
                          start=f, stop=g), inc="T")
            else:
                nsub = N // 128
                for sbt in range(nsub):
                    so = qoff + sbt * 128
                    kb.op("tensor", lambda e, lhsV=lhsV, so=so, b=pvb, j=j,
                          ci=ci,
                          f=(ci == 0 and sbt == 0),
                          g=(last and sbt == nsub - 1): e.matmul(
                              b[0:65, so:so + 128], lhsV,
                              pt_ap(j, ci, so, 128),
                              start=f, stop=g),
                          inc=("T" if sbt == nsub - 1 else None))
            pv_t[(ci, j)] = kb.val("T")

    for j in range(4):
        lastpv[(j, (L - 1) % 2)] = pv_t[(L - 1, j)]
        if L >= 2:
            lastpv[(j, (L - 2) % 2)] = pv_t[(L - 2, j)]

    xstb = st["xst"]
    Cslots = [R64(st, 0), R64(st, 1), R64(st, 2), xstb[64:65, 2 * TW:3 * TW]]
    rws = st["rows"]
    Lslots = [rws[32:33, 0:TW], rws[32:33, TW:2 * TW], rws[32:33, 2 * TW:3 * TW],
              rws[96:97, 0:TW]]
    kb.wait("vector", "T", pv_t[(L - 1, 3)])
    for j in range(4):
        kb.op("vector", lambda e, j=j: e.tensor_copy(
            Cslots[j], pbh[2 + j][64:65, 0:TW]))
    cv = kb.commit("vector", "V")
    kb.wait("scalar", "V", cv)
    for j in range(4):
        kb.op("scalar", lambda e, j=j: e.activation(Lslots[j], Cslots[j], AF.Ln))
    kb.drain("scalar")
    sqv = {}
    for j in range(4):
        sqv[j] = kb.op("scalar", lambda e, j=j: e.activation(
            Cslots[j], Lslots[j], AF.Exp, bias=0.0, scale=-1.0), inc="SC")
    stgs = [st["xst"][0:64, 0:TW], st["xst"][0:64, TW:2 * TW]]
    ttv = None
    for pair in ((0, 1), (2, 3)):
        for j in pair:
            kb.wait("tensor", "SC", sqv[j])
            if ttv is not None and not deep and j == pair[0]:
                kb.wait("tensor", "V", ttv)
            lx = (L - 1, 2 + (j % 2)) if not deep else (L - 1, j)
            kb.wait("tensor", "SC", exp_sc[lx])
            kb.op("tensor", lambda e, j=j, b=sbanks[j]: e.matmul(
                pbh[b][0:64, 0:TW], tb["onesq"][64:65, 0:64],
                Cslots[j], start=True, stop=True), inc="T")
        tt = kb.val("T")
        kb.wait("vector", "T", tt)
        bufs = {pair[0]: xstb[0:64, 2 * TW:3 * TW],
                pair[1]: xstb[0:64, 3 * TW:4 * TW]}
        for j in pair:
            kb.op("vector", lambda e, j=j, b=bufs[j]: e.tensor_copy(
                b, pbh[sbanks[j]][0:64, 0:TW]))
        kb.drain("vector")
        for j in pair:
            qc, half = pos[j]
            if half == 0:
                dst = AO[0:64, qc * TW:(qc + 1) * TW]
                kb.op("vector", lambda e, j=j, b=bufs[j], dst=dst: e.tensor_tensor(
                    dst, pbh[2 + j][0:64, 0:TW], b, ALU.mult))
            else:
                stg = stgs[j % 2]
                ltok = st.setdefault("lift_tok_slot", {}).get(j % 2)
                if ltok is not None:
                    kb.wait_tok("vector", ltok)
                kb.op("vector", lambda e, j=j, b=bufs[j], stg=stg: e.tensor_tensor(
                    stg, pbh[2 + j][0:64, 0:TW], b, ALU.mult))
                vvl = kb.commit("vector", "V")
                kb.wait("gpsimd", "V", vvl)
                tok = kb.dma("gpsimd", AO[64:128, qc * TW:(qc + 1) * TW], stg, None)
                st.setdefault("ao_lift", []).append(tok)
                st["lift_tok_slot"][j % 2] = tok
        ttv = kb.commit("vector", "V")
        for j in pair:
            sbv[sbanks[j]] = ttv
            pvv[j] = ttv


def emit_v_proj(st, t0, nk, lhs_fn, rhs_fn):
    kb = st["kb"]
    pb, VG = st["pb"], st["VG"]
    nblk = 4 * 65
    vg_free = (st["s"] // 128) * nblk
    for tc4 in range(TW // 128):
        tca = (t0 // 128) + tc4
        kb.guard("tensor", "V")
        for kc in range(nk):
            kb.op("tensor", lambda e, kc=kc, tc4=tc4, f=(kc == 0),
                  g=(kc == nk - 1): e.matmul(
                st["pbh"][4][0:128, 0:256], lhs_fn(kc, tc4), rhs_fn(kc),
                start=f, stop=g),
                inc=("T" if kc == nk - 1 else None))
        tt = kb.val("T")
        kb.wait("vector", "T", tt)
        out_ap = ap_of(VG, tca * nblk, [[vg_free, 128], [65, 4], [1, 64]])
        in_ap = ap_of(st["pbh"][4], 0, [[512, 128], [64, 4], [1, 64]])
        kb.op("vector", lambda e, o=out_ap, i=in_ap: e.tensor_copy(o, i))
        kb.commit("vector", "V")
    ones_ap = ap_of(VG, (t0 // 128) * nblk + 64,
                    [[vg_free, 128], [nblk, TW // 128], [65, 4], [1, 1]])
    kb.op("vector", lambda e, o=ones_ap: e.memset(o, 1.0))
    kb.commit("vector", "V")


def emit_v_proj_local(st, t0, lwv):
    kb = st["kb"]
    wst, xst = st["wst"], st["xst"]
    nblk = 4 * 65
    vg_free = (st["s"] // 128) * nblk

    def acc_ap(tc4, dims):
        return ap_of(st["pbh"][4 + (tc4 // 2)], (tc4 % 2) * 256, dims)

    kb.guard("tensor", "V")
    for kc in range(NC):
        buf = wst[:, (kc % 2) * 1024:(kc % 2) * 1024 + LAT]
        wl = band_load(st, buf, lwv[kc * 128:(kc + 1) * 128, :], LAT)
        kb.wait_tok("tensor", wl)
        for tc4 in range(TW // 128):
            lhs = xst[:, kc * TW + tc4 * 128: kc * TW + (tc4 + 1) * 128]
            out = acc_ap(tc4, [[512, 128], [1, 256]])
            kb.op("tensor", lambda e, lhs=lhs, out=out, buf=buf, kc=kc,
                  f=(kc == 0 and tc4 % 2 == 0),
                  g=(kc == NC - 1 and tc4 % 2 == 1):
                  e.matmul(out, lhs, buf, start=f, stop=g),
                  inc=("T" if tc4 == TW // 128 - 1 else None))
    tt = kb.val("T")
    kb.wait("vector", "T", tt)
    for tc4 in range(TW // 128):
        tca = (t0 // 128) + tc4
        out_ap = ap_of(st["VG"], tca * nblk, [[vg_free, 128], [65, 4], [1, 64]])
        in_ap = acc_ap(tc4, [[512, 128], [64, 4], [1, 64]])
        kb.op("vector", lambda e, o=out_ap, i=in_ap: e.tensor_copy(o, i))
        kb.drain("vector")
    kb.commit("vector", "V")
    ones_ap = ap_of(st["VG"], (t0 // 128) * nblk + 64,
                    [[vg_free, 128], [nblk, TW // 128], [65, 4], [1, 1]])
    kb.op("vector", lambda e, o=ones_ap: e.memset(o, 1.0))
    kb.commit("vector", "V")


def emit_store_s(st):
    kb, s = st["kb"], st["s"]
    kb.guard("gpsimd", "V")
    kb.dma_group("gpsimd",
                 [(st["s_scr"][ch * 128:(ch + 1) * 128, :],
                   st["xn"][:, ch * s:(ch + 1) * s]) for ch in range(NC)], None)



def emit_wquant(st):
    kb = st["kb"]
    c, pb, pbh, tb, cs, mw = (st["consts"], st["pb"], st["pbh"], st["tb"],
                              st["cs"], st["mw"])
    wf, wq16, sgn2 = st["wf"], st["wq16"], st["sgn2"]
    tmp1, tmp2 = st["tmp1"], st["tmp2"]

    specs = [
        ("w_gate", st["wgq_scr"], INNER, 0, 0),
        ("w_up", st["wuq_scr"], INNER, 32, 1),
        ("w_down", st["wdq_scr"], D, 64, 2),
    ]
    for (wname, scr, nout, cs_off, mw_i) in specs:
        src = st["w"][wname]
        nin_ch = src.tensor.shape[0] // 128

        def wfs(ic, nout=nout):
            sl = ic % 2
            return wf[:, sl * INNER: sl * INNER + nout]

        toks = {}
        rv = {}
        kb.guard("vector", "SC")
        kb.guard("sync", "V", "SC")
        toks[0] = kb.dma("sync", wfs(0), src[0:128, :], None)
        for ic in range(nin_ch):
            if ic + 1 < nin_ch:
                if ic >= 1:
                    kb.wait("sync", "V", rv[ic - 1])
                toks[ic + 1] = kb.dma("sync", wfs(ic + 1),
                                      src[(ic + 1) * 128:(ic + 2) * 128, :], None)
            kb.wait_tok("vector", toks[ic])
            rv[ic] = kb.op("vector", lambda e, ic=ic, wfs=wfs: e.reduce_sum(
                tmp1[:, ic:ic + 1], wfs(ic), AX.X, apply_absolute_value=True),
                inc="V")
        kb.drain("vector")
        kb.op("vector", lambda e, n=nin_ch: e.reduce_sum(
            tmp2[:, 0:1], tmp1[:, 0:n], AX.X))
        va = kb.commit("vector", "V")
        kb.wait("tensor", "V", va)
        tt = kb.op("tensor", lambda e: e.matmul(
            pbh[5][:, 0:1], tb["onesq"][:, :], tmp2[:, 0:1], start=True, stop=True),
            inc="T")
        kb.wait("vector", "T", tt)
        kb.op("vector", lambda e, mw_i=mw_i: e.tensor_scalar(
            mw[:, mw_i:mw_i + 1], pbh[5][:, 0:1], 1.0 / (1 << 22), 1e-5,
            ALU.mult, ALU.max))
        kb.drain("vector")
        kb.op("vector", lambda e, mw_i=mw_i: e.tensor_scalar(
            tmp1[:, 0:1], mw[:, mw_i:mw_i + 1], 0.5, None, ALU.mult))
        kb.drain("vector")
        kb.op("vector", lambda e, mw_i=mw_i: e.tensor_scalar(
            tmp1[:, 1:2], mw[:, mw_i:mw_i + 1], -0.5, None, ALU.mult))
        vb = kb.commit("vector", "V")
        kb.wait("scalar", "V", vb)

        def wqs(ic, nout=nout):
            sl = ic % 2
            return wq16[:, sl * INNER: sl * INNER + nout]

        def sgs(ic, nout=nout):
            sl = ic % 2
            return sgn2[:, sl * INNER: sl * INNER + nout]

        slot_store = st.setdefault("wq_store_toks", {})
        colsum_t = {}
        toksB = {}
        sc_after = {}
        kb.guard("sync", "SC", "V", "T")
        toksB[0] = kb.dma("sync", wfs(0), src[0:128, :], None)
        for ic in range(nin_ch):
            sl = ic % 2
            if ic + 1 < nin_ch:
                if ic >= 1:
                    kb.wait("sync", "SC", sc_after[ic - 1])
                toksB[ic + 1] = kb.dma("sync", wfs(ic + 1),
                                       src[(ic + 1) * 128:(ic + 2) * 128, :], None)
            kb.wait_tok("scalar", toksB[ic])
            if sl in colsum_t:
                kb.wait("scalar", "T", colsum_t[sl])
            if slot_store.get(sl) is not None:
                kb.wait_tok("scalar", slot_store[sl])
            kb.op("scalar", lambda e, ic=ic, wqs=wqs, wfs=wfs: e.activation(
                wqs(ic), wfs(ic), AF.Sign, bias=tmp1[:, 1:2]))
            s2 = kb.op("scalar", lambda e, ic=ic, sgs=sgs, wfs=wfs: e.activation(
                sgs(ic), wfs(ic), AF.Sign, bias=tmp1[:, 0:1]), inc="SC")
            sc_after[ic] = s2
            kb.wait("vector", "SC", s2)
            kb.op("vector", lambda e, ic=ic, wqs=wqs, sgs=sgs: e.tensor_tensor(
                wqs(ic), wqs(ic), sgs(ic), ALU.add))
            vq = kb.commit("vector", "V")
            kb.wait("gpsimd", "V", vq)
            slot_store[sl] = kb.dma("gpsimd", scr[ic * 128:(ic + 1) * 128, :],
                                    wqs(ic), None)
            kb.wait("tensor", "V", vq)
            for os_ in range(nout // 128):
                o0 = sl * INNER + os_ * 128
                kb.op("tensor", lambda e, os_=os_, ic=ic, cs_off=cs_off, o0=o0,
                      f=(ic == 0 and os_ == 0),
                      g=(ic == nin_ch - 1 and os_ == nout // 128 - 1): e.matmul(
                          pbh[6][:, cs_off + os_: cs_off + os_ + 1],
                          wq16[:, o0:o0 + 128],
                          tb["ones16"][:, 0:1],
                          start=f, stop=g),
                      inc=("T" if os_ == nout // 128 - 1 else None))
            colsum_t[sl] = kb.val("T")
        tt = kb.val("T")
        kb.wait("vector", "T", tt)
        kb.op("vector", lambda e, cs_off=cs_off, n=nout // 128: e.tensor_scalar(
            cs[:, cs_off:cs_off + n], pbh[6][:, cs_off:cs_off + n], MAGIC, None,
            ALU.mult))
        kb.commit("vector", "V")



def bcast_row(st, row_ap, dst_sb):
    kb, tb = st["kb"], st["tb"]
    kb.guard("tensor", "V")
    tt = kb.op("tensor", lambda e: e.matmul(
        st["pbh"][7][:, :], tb["onesq"][0:1, :], row_ap, start=True, stop=True),
        inc="T")
    kb.wait("vector", "T", tt)
    kb.op("vector", lambda e: e.tensor_copy(dst_sb[:, :], st["pbh"][7][:, :]))
    return kb.commit("vector", "V")


def emit_amax(st, chunk_fn, nch, dst_row):
    kb = st["kb"]
    accs = [st["tmp1"], st["tmp2"], st["tmpsq"], st["tmp4"]]
    na = min(len(accs), max(1, nch // 2))
    if st.get("tmps_sc") is not None:
        kb.wait("vector", "SC", st["tmps_sc"])
    for i in range(na):
        kb.op("vector", lambda e, i=i: e.tensor_tensor(
            accs[i][:, :], chunk_fn(i), chunk_fn(na + i), ALU.max))
    kb.drain("vector")
    nxt = 2 * na
    while nxt < nch:
        for i in range(na):
            if nxt + i < nch:
                kb.op("vector", lambda e, i=i, j=nxt + i: e.tensor_tensor(
                    accs[i][:, :], accs[i][:, :], chunk_fn(j), ALU.max))
        kb.drain("vector")
        nxt += na
    done = 0
    while done < nch:
        for i in range(na):
            if done + i < nch:
                kb.op("vector", lambda e, i=i, j=done + i: e.scalar_tensor_tensor(
                    accs[i][:, :], chunk_fn(j), -1.0, accs[i][:, :],
                    ALU.mult, ALU.max))
        kb.drain("vector")
        done += na
    step = na // 2
    while step >= 1:
        for i in range(step):
            kb.op("vector", lambda e, i=i, s2=step: e.tensor_tensor(
                accs[i][:, :], accs[i][:, :], accs[i + s2][:, :], ALU.max))
        kb.drain("vector")
        step //= 2
    vv = kb.commit("vector", "V")
    kb.wait("gpsimd", "V", vv)
    kb.op("gpsimd", lambda e: e.reduce_max(R(st, 0), accs[0][:, :], AX.C))
    gg = kb.commit("gpsimd", "G")
    kb.wait("vector", "G", gg)
    kb.op("vector", lambda e, d=dst_row: e.tensor_scalar(
        d, R(st, 0), 1e-5, None, ALU.max))
    return kb.commit("vector", "V")


def emit_ffn(st):
    kb, s, nt = st["kb"], st["s"], st["nt"]
    pb, pbh, tb, c, cs, mw = (st["pb"], st["pbh"], st["tb"], st["consts"],
                              st["cs"], st["mw"])
    arena, z1, z2, prod, fw = st["arena"], st["z1"], st["z2"], st["prod"], st["fw"]
    Pt = st["Pt"]
    sxb, tgb, tub, tdb = st["sxb"], st["tgb"], st["tub"], st["tdb"]
    tmp1, tmp2, tmpsq = st["tmp1"], st["tmp2"], st["tmpsq"]
    tmps4 = [tmp1, tmp2, tmpsq, st["tmp4"]]
    xoc = st["xoc"]
    XNT = 0
    S0 = NC * TW
    bankv = st.setdefault("ffn_bankv", {})

    def bank_wait(banks):
        for b in banks:
            v = bankv.get(b)
            if v is not None:
                kb.wait("tensor", "V", v)

    def A(base, ch):
        return arena[:, base + ch * TW: base + (ch + 1) * TW]

    STT_ = NC * TW

    def emit_q1g(t):
        t0 = t * TW
        kb.guard("gpsimd", "V")
        pairs = []
        for ch in range(NC):
            pairs.append((A(XNT, ch),
                          st["xn_scr"][ch * 128:(ch + 1) * 128, t0:t0 + TW]))
        tl = kb.dma_group("gpsimd", pairs, None)
        kb.wait_tok("vector", tl)

        av = emit_amax(st, lambda ch: A(XNT, ch), NC, R(st, 2))
        kb.op("vector", lambda e: e.tensor_scalar(
            R(st, 0), R(st, 2), mw[0:1, 0:1], 1.0 / 254.0, ALU.mult, ALU.mult))
        kb.op("vector", lambda e: e.tensor_scalar(
            R(st, 1), R(st, 2), mw[0:1, 1:2], 1.0 / 254.0, ALU.mult, ALU.mult))
        vv2 = kb.commit("vector", "V")
        kb.wait("scalar", "V", vv2)
        kb.op("scalar", lambda e: e.activation(
            st["rows"][32:33, TW:2 * TW], R(st, 2), AF.Ln))
        kb.drain("scalar")
        sxv = kb.op("scalar", lambda e: e.activation(
            R(st, 2), st["rows"][32:33, TW:2 * TW], AF.Exp,
            bias=c[0:1, 3:4], scale=-1.0), inc="SC")
        kb.guard("tensor", "V")
        kb.wait("tensor", "V", vv2)
        t1 = kb.op("tensor", lambda e: e.matmul(
            pbh[7][:, :], tb["onesq"][0:1, :], R(st, 0), start=True, stop=True))
        t2 = kb.op("tensor", lambda e: e.matmul(
            pbh[6][:, :], tb["onesq"][0:1, :], R(st, 1), start=True, stop=True),
            inc="T")
        kb.wait("vector", "T", kb.val("T"))
        kb.op("vector", lambda e: e.tensor_copy(tgb[:, :], pbh[7][:, :]))
        kb.op("vector", lambda e: e.tensor_copy(tub[:, :], pbh[6][:, :]))
        cbv = kb.commit("vector", "V")
        bankv[6] = cbv
        kb.wait("tensor", "SC", sxv)
        kb.wait("tensor", "V", cbv)
        kb.op("tensor", lambda e: e.matmul(
            pbh[7][:, :], tb["onesq"][0:1, :], R(st, 2), start=True, stop=True),
            inc="T")
        kb.wait("vector", "T", kb.val("T"))
        kb.op("vector", lambda e: e.tensor_copy(sxb[:, :], pbh[7][:, :]))
        bankv[7] = kb.commit("vector", "V")

        prevsc = st.get("tmps_sc")
        for b0 in (0, 4):
            if prevsc is not None:
                kb.wait("vector", "SC", prevsc)
            for i in range(4):
                ch = b0 + i
                kb.op("vector", lambda e, ch=ch, i=i: e.tensor_tensor(
                    tmps4[i][:, :], A(XNT, ch), sxb[:, :], ALU.mult))
            zv = kb.commit("vector", "V")
            kb.wait("scalar", "V", zv)
            if b0 == 0 and st.get("ffn_gu_last_t") is not None:
                kb.wait("scalar", "T", st["ffn_gu_last_t"])
            for i in range(4):
                ch = b0 + i
                kb.op("scalar", lambda e, ch=ch, i=i: e.activation(
                    z1[:, ch * TW:(ch + 1) * TW], tmps4[i][:, :], AF.Relu,
                    bias=c[:, 4:5], scale=1.0), inc=("SC" if i == 3 else None))
            prevsc = kb.val("SC")
            st["tmps_sc"] = prevsc
        st["ffn_z1v"] = prevsc

    def gu_load(og, slot):
        return kb.dma_group("sync" if og % 2 == 0 else "gpsimd", [
            (ap_of(fw, slot * S0, [[3 * NC * TW, 128], [256, NC], [1, 256]]),
             ap_of(st["wgq_scr"].tensor, og * 256,
                   [[INNER, 128], [128 * INNER, NC], [1, 256]])),
            (ap_of(fw, slot * S0 + NC * 256,
                   [[3 * NC * TW, 128], [256, NC], [1, 256]]),
             ap_of(st["wuq_scr"].tensor, og * 256,
                   [[INNER, 128], [128 * INNER, NC], [1, 256]])),
        ], None)

    def emit_gu(t):
        kb.wait("tensor", "SC", st["ffn_z1v"])
        gu_t_after = {}
        gu_toks = {}
        kb.guard("sync", "T")
        kb.guard("gpsimd", "T")
        gu_toks[0] = gu_load(0, 0)
        for og in range(16):
            slot = og % 2
            bs = [0, 1, 2, 3] if og % 2 == 0 else [4, 5, 6, 7]
            if og + 1 < 16:
                if og >= 1:
                    kb.wait("sync" if (og + 1) % 2 == 0 else "gpsimd",
                            "T", gu_t_after[og - 1])
                gu_toks[og + 1] = gu_load(og + 1, (og + 1) % 2)
            kb.wait_tok("tensor", gu_toks[og])
            bank_wait(bs)
            for ic in range(NC):
                for oo in range(2):
                    off = slot * S0 + ic * 256 + oo * 128
                    kb.op("tensor", lambda e, off=off, oo=oo, ic=ic, bs=bs,
                          f=(ic == 0), g=(ic == NC - 1): e.matmul(
                        pb[bs[oo]], fw[:, off:off + 128],
                        z1[:, ic * TW:(ic + 1) * TW], start=f, stop=g))
                for oo in range(2):
                    off = slot * S0 + NC * 256 + ic * 256 + oo * 128
                    kb.op("tensor", lambda e, off=off, oo=oo, ic=ic, bs=bs,
                          f=(ic == 0), g=(ic == NC - 1): e.matmul(
                        pb[bs[2 + oo]], fw[:, off:off + 128],
                        z1[:, ic * TW:(ic + 1) * TW], start=f, stop=g),
                        inc=("T" if (ic == NC - 1 and oo == 1) else None))
            gu_t_after[og] = kb.val("T")
            tt = kb.val("T")
            kb.wait("vector", "T", tt)
            if st.get("tmps_sc") is not None:
                kb.wait("vector", "SC", st["tmps_sc"])
            for oo in range(2):
                oc = og * 2 + oo
                kb.op("vector", lambda e, oo=oo, oc=oc, bs=bs:
                      e.scalar_tensor_tensor(
                          tmps4[oo][:, :], pb[bs[oo]], cs[:, oc:oc + 1],
                          tgb[:, :], ALU.subtract, ALU.mult))
            cv = kb.commit("vector", "V")
            kb.wait("scalar", "V", cv)
            for oo in range(2):
                kb.op("scalar", lambda e, oo=oo: e.activation(
                    Pt[:, oo * TW:(oo + 1) * TW], tmps4[oo][:, :], AF.Silu),
                    inc=("SC" if oo == 1 else None))
            sg = kb.val("SC")
            st["tmps_sc"] = sg
            kb.wait("vector", "SC", sg)
            for oo in range(2):
                oc = og * 2 + oo
                kb.op("vector", lambda e, oo=oo, oc=oc, bs=bs:
                      e.scalar_tensor_tensor(
                          tmps4[2 + oo][:, :], pb[bs[2 + oo]],
                          cs[:, 32 + oc:33 + oc], tub[:, :],
                          ALU.subtract, ALU.mult))
            kb.drain("vector")
            for oo in range(2):
                oc = og * 2 + oo
                kb.op("vector", lambda e, oo=oo, oc=oc: e.tensor_tensor(
                    prod[:, oc * TW:(oc + 1) * TW], Pt[:, oo * TW:(oo + 1) * TW],
                    tmps4[2 + oo][:, :], ALU.mult))
            ev = kb.commit("vector", "V")
            for b in bs:
                bankv[b] = ev
        st["ffn_gu_last_t"] = kb.val("T")

    def emit_q2a(t):
        av = emit_amax(st, lambda chh: prod[:, chh * TW:(chh + 1) * TW], 32,
                       R(st, 2))
        kb.op("vector", lambda e: e.tensor_scalar(
            R(st, 1), R(st, 2), mw[0:1, 2:3], 1.0 / 254.0, ALU.mult, ALU.mult))
        vv2 = kb.commit("vector", "V")
        kb.wait("scalar", "V", vv2)
        kb.op("scalar", lambda e: e.activation(
            st["rows"][32:33, TW:2 * TW], R(st, 2), AF.Ln))
        kb.drain("scalar")
        sxv = kb.op("scalar", lambda e: e.activation(
            R(st, 2), st["rows"][32:33, TW:2 * TW], AF.Exp,
            bias=c[0:1, 3:4], scale=-1.0), inc="SC")
        st["ffn_q2_sxv"] = sxv

    def emit_q2b(t):
        sxv = st["ffn_q2_sxv"]
        kb.guard("tensor", "V")
        kb.op("tensor", lambda e: e.matmul(
            pbh[6][:, :], tb["onesq"][0:1, :], R(st, 1), start=True, stop=True))
        kb.wait("tensor", "SC", sxv)
        kb.op("tensor", lambda e: e.matmul(
            pbh[7][:, :], tb["onesq"][0:1, :], R(st, 2), start=True, stop=True),
            inc="T")
        kb.wait("vector", "T", kb.val("T"))
        kb.op("vector", lambda e: e.tensor_copy(tdb[:, :], pbh[6][:, :]))
        kb.op("vector", lambda e: e.tensor_copy(sxb[:, :], pbh[7][:, :]))
        cbv = kb.commit("vector", "V")
        bankv[6] = cbv
        bankv[7] = cbv
        prevsc = st.get("tmps_sc")
        for b0 in range(0, 32, 4):
            if prevsc is not None:
                kb.wait("vector", "SC", prevsc)
            for i in range(4):
                chh = b0 + i
                kb.op("vector", lambda e, chh=chh, i=i: e.tensor_tensor(
                    tmps4[i][:, :], prod[:, chh * TW:(chh + 1) * TW], sxb[:, :],
                    ALU.mult))
            zv = kb.commit("vector", "V")
            kb.wait("scalar", "V", zv)
            if b0 == 0 and st.get("ffn_dn_last_t") is not None:
                kb.wait("scalar", "T", st["ffn_dn_last_t"])
            for i in range(4):
                chh = b0 + i
                kb.op("scalar", lambda e, chh=chh, i=i: e.activation(
                    z2[:, chh * TW:(chh + 1) * TW], tmps4[i][:, :], AF.Relu,
                    bias=c[:, 4:5], scale=1.0), inc=("SC" if i == 3 else None))
            prevsc = kb.val("SC")
            st["tmps_sc"] = prevsc
        st["ffn_z2v"] = prevsc

    def emit_q2(t):
        emit_q2a(t)
        emit_q2b(t)

    def dn_load(oc):
        return kb.dma("sync", ap_of(fw, 2 * S0,
                                    [[3 * NC * TW, 128], [128, 32], [1, 128]]),
                      ap_of(st["wdq_scr"].tensor, oc * 128,
                            [[D, 128], [128 * D, 32], [1, 128]]), None)

    def emit_dn(t):
        t0 = t * TW
        kb.guard("gpsimd", "V")
        stl = kb.dma_group("gpsimd", [
            (A(STT_, ch), st["s_scr"][ch * 128:(ch + 1) * 128, t0:t0 + TW])
            for ch in range(NC)], None)
        kb.wait("tensor", "SC", st["ffn_z2v"])
        dn_t_after = {}
        dn_toks = {}
        xtoks = {}
        xv_after = {}
        dn_ev = {}
        kb.guard("sync", "T")
        dn_toks[0] = dn_load(0)
        kb.guard("gpsimd", "V")
        xtoks[0] = kb.dma("gpsimd", xoc[:, 0:512],
                          st["xT"][0:128, t0:t0 + TW], None)
        for oc in range(NC):
            bank = 4 + oc % 2
            if oc + 1 < NC:
                if oc >= 1:
                    kb.wait("gpsimd", "V", xv_after[oc - 1])
                xtoks[oc + 1] = kb.dma(
                    "gpsimd",
                    xoc[:, ((oc + 1) % 2) * 512:((oc + 1) % 2) * 512 + 512],
                    st["xT"][(oc + 1) * 128:(oc + 2) * 128, t0:t0 + TW], None)
            kb.wait_tok("tensor", dn_toks[oc])
            bank_wait([bank])
            for ic in range(32):
                off = 2 * S0 + ic * 128
                kb.op("tensor", lambda e, ic=ic, off=off, bank=bank,
                      f=(ic == 0), g=(ic == 31): e.matmul(
                    pb[bank], fw[:, off:off + 128],
                    z2[:, ic * TW:(ic + 1) * TW], start=f, stop=g),
                    inc=("T" if ic == 31 else None))
            dn_t_after[oc] = kb.val("T")
            if oc + 1 < NC:
                kb.wait("sync", "T", dn_t_after[oc])
                dn_toks[oc + 1] = dn_load(oc + 1)
            tt = kb.val("T")
            kb.wait("vector", "T", tt)
            if st.get("tmps_sc") is not None:
                kb.wait("vector", "SC", st["tmps_sc"])
            kb.op("vector", lambda e, bank=bank, oc=oc: e.scalar_tensor_tensor(
                tmp1[:, :], pb[bank], cs[:, 64 + oc:65 + oc], tdb[:, :],
                ALU.subtract, ALU.mult))
            vv = kb.commit("vector", "V")
            kb.wait("scalar", "V", vv)
            sg = kb.op("scalar", lambda e: e.activation(
                tmpsq[:, :], tmp1[:, :], AF.Silu), inc="SC")
            st["tmps_sc"] = sg
            kb.wait("vector", "SC", sg)
            if oc == 0:
                kb.wait_tok("vector", stl)
            kb.op("vector", lambda e, oc=oc: e.tensor_tensor(
                tmp2[:, :], tmpsq[:, :], A(STT_, oc), ALU.mult))
            kb.drain("vector")
            kb.wait_tok("vector", xtoks[oc])
            kb.op("vector", lambda e, oc=oc: e.tensor_tensor(
                tmp2[:, :], tmp2[:, :],
                xoc[:, (oc % 2) * 512:(oc % 2) * 512 + 512], ALU.add))
            vv = kb.commit("vector", "V")
            xv_after[oc] = vv
            bankv[bank] = vv
            kb.wait("gpsimd", "V", vv)
            otok = kb.dma("gpsimd",
                          st["outT"][oc * 128:(oc + 1) * 128, t0:t0 + TW],
                          tmp2[:, :], None)
            kb.wait_tok("vector", otok)
        st["ffn_dn_last_t"] = kb.val("T")

    emit_q1g(0)
    emit_gu(0)
    if nt == 1:
        emit_q2(0)
    for t in range(1, nt):
        emit_q1g(t)
        emit_q2(t - 1)
        emit_gu(t)
        if t == nt - 1:
            emit_q2a(t)
            emit_dn(t - 1)
            emit_q2b(t)
        else:
            emit_dn(t - 1)
    emit_dn(nt - 1)



def kernel(**inputs):
    ins = {k: np.asarray(v) for k, v in inputs.items()}
    x = ins["L0_attn_hidden"]
    xTs = [np.ascontiguousarray(x[b].T) for b in range(x.shape[0])]

    qp = _q_col_perm()
    kp = _k_col_perm()
    wop = _wo_row_perm()
    wmap = {
        "g_wq": ins["L0_attn_g_wq"][:, qp],
        "g_kv_down": ins["L0_attn_g_kv_down"],
        "g_k_up": ins["L0_attn_g_k_up"][:, kp],
        "g_v_up": ins["L0_attn_g_v_up"],
        "g_wo": ins["L0_attn_g_wo"][wop, :],
        "l_wq": ins["L0_attn_l_wq"][:, qp],
        "l_wk": ins["L0_attn_l_wk"][:, kp],
        "l_wv": ins["L0_attn_l_wv"],
        "l_wo": ins["L0_attn_l_wo"][wop, :],
        "w_gate": ins["L0_ffn_w_gate"].T,
        "w_up": ins["L0_ffn_w_up"].T,
        "w_down": ins["L0_ffn_w_down"].T,
        "norm1_w": np.ascontiguousarray(ins["L0_attn_norm1_w"].reshape(NC, 128).T),
        "norm2_w": np.ascontiguousarray(ins["L0_attn_norm2_w"].reshape(NC, 128).T),
    }
    wmap = {k: np.ascontiguousarray(np.asarray(v, dtype=np.float32))
            for k, v in wmap.items()}
    tmap = host_tables(S)

    prog = build_program()
    in_maps = []
    for b in range(x.shape[0]):
        m = {"xT": xTs[b]}
        m.update(wmap)
        m.update(tmap)
        in_maps.append(m)
    res = run_bass_kernel_spmd(prog, in_maps, core_ids=list(range(len(in_maps))))
    out = np.stack([np.ascontiguousarray(r["outT"].T) for r in res.results])
    return out
